# version 02
# baseline (speedup 1.0000x reference)
import os
import numpy as np
import ml_dtypes
import concourse.bass as bass
import concourse.mybir as mybir
from concourse.bass_utils import run_bass_kernel_spmd

F32 = mybir.dt.float32
BF16 = mybir.dt.bfloat16
AF = mybir.ActivationFunctionType
ALU = mybir.AluOpType
AX = mybir.AxisListType
BF = ml_dtypes.bfloat16
f32 = np.float32

NCORES = 8
S = 2048
D = 2048
NT = 16
NH = 16
SCALE = f32(1.0 / np.sqrt(128.0))
N_SUB = int(os.environ.get("DEV_NSUB", "4"))


class Op:
    __slots__ = ("idx", "stream", "fn", "deps", "semkey", "is_dma", "milestone", "waits")

    def __init__(self, idx, stream, fn, deps, semkey, is_dma):
        self.idx = idx
        self.stream = stream
        self.fn = fn
        self.deps = deps
        self.semkey = semkey
        self.is_dma = is_dma
        self.milestone = None
        self.waits = None


class Em:
    STREAMS = ("tensor", "vector", "scalar", "pool", "sync")

    def __init__(self):
        self.ops = []
        self.last_write = {}
        self.readers = {}

    def _emit(self, stream, semkey, is_dma, fn, reads, writes):
        idx = len(self.ops)
        deps = set()
        for t in reads:
            w = self.last_write.get(t)
            if w is not None:
                deps.add(w)
        for t in writes:
            w = self.last_write.get(t)
            if w is not None:
                deps.add(w)
            for r in self.readers.get(t, ()):
                deps.add(r)
        deps.discard(idx)
        self.ops.append(Op(idx, stream, fn, deps, semkey, is_dma))
        for t in reads:
            self.readers.setdefault(t, []).append(idx)
        for t in writes:
            self.last_write[t] = idx
            self.readers[t] = []
        return idx

    def t(self, fn, reads=(), writes=()):
        return self._emit("tensor", "tensor", False, fn, reads, writes)

    def v(self, fn, reads=(), writes=()):
        return self._emit("vector", "vector", False, fn, reads, writes)

    def s(self, fn, reads=(), writes=()):
        return self._emit("scalar", "scalar", False, fn, reads, writes)

    def p(self, fn, reads=(), writes=()):
        return self._emit("pool", "pool", False, fn, reads, writes)

    def dsync(self, fn, reads=(), writes=()):
        return self._emit("sync", "q_sync", True, fn, reads, writes)

    def dscal(self, fn, reads=(), writes=()):
        return self._emit("scalar", "q_scal", True, fn, reads, writes)

    def dpool(self, fn, reads=(), writes=()):
        return self._emit("pool", "q_pool", True, fn, reads, writes)

    def finalize(self):
        needed = set()
        for o in self.ops:
            if o.is_dma:
                needed.add(o.idx)
            for d in o.deps:
                p = self.ops[d]
                if p.stream == "tensor" and o.stream == "tensor":
                    continue
                needed.add(d)
        counts = {}
        for o in self.ops:
            if o.idx in needed:
                c = counts.get(o.semkey, 0) + 1
                counts[o.semkey] = c
                o.milestone = c
        hi = {s: {} for s in self.STREAMS}
        for o in self.ops:
            waits = {}
            for d in o.deps:
                p = self.ops[d]
                if p.stream == "tensor" and o.stream == "tensor":
                    continue
                v = p.milestone * (16 if p.is_dma else 1)
                if waits.get(p.semkey, 0) < v:
                    waits[p.semkey] = v
            h = hi[o.stream]
            o.waits = []
            for k, v in waits.items():
                if h.get(k, 0) < v:
                    h[k] = v
                    o.waits.append((k, v))
        self.final_counts = {k: v * (16 if k.startswith("q_") else 1)
                             for k, v in counts.items()}

    def build(self, nc, block, sems):
        self.finalize()
        streams = {s: [o for o in self.ops if o.stream == s] for s in self.STREAMS}

        def runner(stream):
            def run(eng):
                for o in streams[stream]:
                    for k, v in o.waits:
                        eng.wait_ge(sems[k], v)
                    inst = o.fn(eng)
                    if o.milestone is not None:
                        inst.then_inc(sems[o.semkey], 16 if o.is_dma else 1)
                if stream == "sync":
                    for k, v in self.final_counts.items():
                        eng.wait_ge(sems[k], v)
            return run

        block.tensor(runner("tensor"))
        block.vector(runner("vector"))
        block.scalar(runner("scalar"))
        block.gpsimd(runner("pool"))
        block.sync(runner("sync"))


def host_consts():
    pos = np.arange(S, dtype=np.float64)
    fr_a = np.arange(0, 128, 2, dtype=np.float64) / 128.0
    inv_a = 1.0 / (1e4 ** fr_a)
    ang_a = pos[:, None] * inv_a[None, :]
    cos_a = np.cos(ang_a).astype(np.float32)
    sin_a = np.sin(ang_a).astype(np.float32)
    fr_f = np.arange(0, 64, 2, dtype=np.float64) / 64.0
    inv_f = 1.0 / (1e6 ** fr_f)
    ang_f = pos[:, None] * inv_f[None, :]
    cos_f = np.cos(ang_f).astype(np.float32)
    sin_f = np.sin(ang_f).astype(np.float32)
    ident = np.eye(128, dtype=np.float32)
    masks = np.zeros((4, 128, 512), dtype=np.float32)
    sk = np.arange(128)[:, None]
    sq = np.arange(128)[None, :]
    tri = (sk <= sq).astype(np.float32)
    for r in range(4):
        for j in range(4):
            if r < j:
                masks[r, :, j * 128:(j + 1) * 128] = 1.0
            elif r == j:
                masks[r, :, j * 128:(j + 1) * 128] = tri
    return dict(cos_a=cos_a, sin_a=sin_a, cos_f=cos_f, sin_f=sin_f,
                ident=ident, trimask=masks.astype(BF))


def build_program():
    nc = bass.Bass("TRN2", target_bir_lowering=False, debug=False)
    em = Em()

    def din(name, shape, dt=F32):
        return nc.dram_tensor(name, list(shape), dt, kind="ExternalInput").ap()

    x_in = din("x_in", [S, D])
    aq = din("aq", [D, D])
    ak = din("ak", [D, 512])
    av = din("av", [D, 512])
    ao = din("ao", [D, D])
    bwq = din("bwq", [NH, 128, D])
    bwk = din("bwk", [NH, 128, D])
    bwv = din("bwv", [NH, 128, D])
    bwo = din("bwo", [NH, D, 128])
    ln1 = din("ln1", [D])
    ln2 = din("ln2", [D])
    fwq = din("fwq", [D, D])
    fwk = din("fwk", [D, 512])
    fwv = din("fwv", [D, 512])
    fwo = din("fwo", [D, D])
    qnw = din("qnw", [D])
    knw = din("knw", [512])
    rw = din("rw", [D, 8])
    rb = din("rb", [8])
    wg = din("wg", [8, D, 1024])
    wu = din("wu", [8, D, 1024])
    wd = din("wd", [8, 1024, D])
    cos_a = din("cos_a", [S, 64])
    sin_a = din("sin_a", [S, 64])
    cos_f = din("cos_f", [S, 32])
    sin_f = din("sin_f", [S, 32])
    ident_d = din("ident", [128, 128])
    trimask_d = din("trimask", [4, 128, 512], BF16)

    out_d = nc.dram_tensor("out", [S, D], F32, kind="ExternalOutput").ap()

    spill_kind = "ExternalOutput" if "spill" in os.environ.get("DEV_TAP", "") else "Internal"
    xbuf = nc.dram_tensor("xbuf", [S, D], F32, kind="Internal").ap()
    y_dr = nc.dram_tensor("y_dr", [S, D], BF16, kind=spill_kind).ap()
    q_dr = nc.dram_tensor("q_dr", [S, D], BF16, kind=spill_kind).ap()
    k_dr = nc.dram_tensor("k_dr", [S, 512], BF16, kind=spill_kind).ap()
    t_dr = nc.dram_tensor("t_dr", [S, D], BF16, kind=spill_kind).ap()
    v_dr = nc.dram_tensor("v_dr", [S, D], BF16, kind=spill_kind).ap()
    qT_dr = nc.dram_tensor("qT_dr", [NH, 128, S], BF16, kind="Internal").ap()
    kT_dr = nc.dram_tensor("kT_dr", [NH, 128, S], BF16, kind="Internal").ap()
    wqT_dr = nc.dram_tensor("wqT_dr", [NH, 128, D], BF16, kind="Internal").ap()
    wkT_dr = nc.dram_tensor("wkT_dr", [NH, 128, D], BF16, kind="Internal").ap()
    wvT_dr = nc.dram_tensor("wvT_dr", [16, 128, D], BF16, kind="Internal").ap()
    woT_dr = nc.dram_tensor("woT_dr", [NH, 128, D], BF16, kind="Internal").ap()
    zall_dr = nc.dram_tensor("zall_dr", [4, 128, 8192], BF16, kind=spill_kind).ap()
    aob_dr = nc.dram_tensor("aob_dr", [D, D], BF16, kind="Internal").ap()
    fob_dr = nc.dram_tensor("fob_dr", [D, D], BF16, kind="Internal").ap()

    from contextlib import ExitStack
    st_ctx = ExitStack()

    def sb(name, shape, dt, **kw):
        return st_ctx.enter_context(nc.sbuf_tensor(name, shape, dt, **kw))

    BIG1 = sb("BIG1", [128, 32768], BF16, is_transpose_dest=True)
    BIG2 = sb("BIG2", [128, 32768], BF16, is_transpose_dest=True)
    STF = sb("STF", [128, 8192], F32)
    STB = sb("STB", [128, 6144], BF16)
    CON = sb("CON", [128, 5632], F32)
    MSK = sb("MSK", [128, 2048], BF16)
    IDE = sb("IDE", [128, 128], F32)
    ONEC = sb("ONEC", [128, 1], BF16)
    ONERB = sb("ONERB", [1, 128], BF16)
    ONERF = sb("ONERF", [1, 128], F32)
    SMF = sb("SMF", [128, 32], F32)
    STASH = sb("STASH", [128, 64], F32)
    MCOL = sb("MCOL", [128, 4], F32)
    MROW = sb("MROW", [1, 512], BF16)
    RCROW = sb("RCROW", [1, 512], F32)
    BCS = sb("BCS", [128, 512], F32)
    CMB = sb("CMB", [128, 128], F32)
    T8 = sb("T8", [128, 64], F32)
    WR8 = sb("WR8", [128, 128], F32)
    LNC = sb("LNC", [128, 32], F32)
    PS = st_ctx.enter_context(nc.psum_tensor("PS", [128, 4096], F32))

    sems = {}
    for k in ("tensor", "vector", "scalar", "pool", "q_sync", "q_scal", "q_pool"):
        sems[k] = st_ctx.enter_context(nc.semaphore("sem_" + k))
    block = st_ctx.enter_context(nc.Block())

    def b1(c0, c1=None):
        c1 = c0 + 512 if c1 is None else c1
        return [("B1", i) for i in range(c0 // 512, (c1 + 511) // 512)]

    def b2(c0, c1=None):
        c1 = c0 + 512 if c1 is None else c1
        return [("B2", i) for i in range(c0 // 512, (c1 + 511) // 512)]

    def stf(c0, c1=None):
        c1 = c0 + 2048 if c1 is None else c1
        return [("STF", i) for i in range(c0 // 2048, (c1 + 2047) // 2048)]

    def stb(c0, c1=None):
        c1 = c0 + 2048 if c1 is None else c1
        return [("STB", i) for i in range(c0 // 2048, (c1 + 2047) // 2048)]

    def pst(c0, c1=None):
        c1 = c0 + 512 if c1 is None else c1
        return [("PS", i) for i in range(c0 // 512, (c1 + 511) // 512)]

    em.dsync(lambda e: e.dma_start(out=IDE[:, :], in_=ident_d[:, :]), writes=[("IDE",)])
    em.v(lambda e: e.memset(ONEC[:, :], 1.0), writes=[("ONEC",)])
    em.v(lambda e: e.memset(ONERB[:, :], 1.0), writes=[("ONERB",)])
    em.v(lambda e: e.memset(ONERF[:, :], 1.0), writes=[("ONERF",)])
    em.dsync(lambda e: e.dma_start(out=MSK[:, :].rearrange("p (r c) -> p r c", r=4),
                                   in_=trimask_d[:, :, :].rearrange("r p c -> p r c")),
             writes=[("MSK",)])
    for dram, w, off in ((cos_a, 64, 0), (sin_a, 64, 1024),
                         (cos_f, 32, 2048), (sin_f, 32, 2560)):
        em.dsync(lambda e, dram=dram, w=w, off=off: e.dma_start(
            out=CON[:, off:off + NT * w].rearrange("p (t w) -> p t w", t=NT),
            in_=dram.rearrange("(t p) w -> p t w", p=128)),
            writes=[("CON", "tab")])

    def bcast_gain(dram_vec, width, dst_off, scale, tagname):
        em.dsync(lambda e: e.dma_start(out=STF[0:1, 2048:2048 + width], in_=dram_vec[None, :]),
                 writes=stf(2048))
        for i in range((width + 511) // 512):
            w0 = i * 512
            w1 = min(width, w0 + 512)
            em.t(lambda e, w0=w0, w1=w1: e.matmul(
                PS[:, 0:w1 - w0], ONERF[0:1, :], STF[0:1, 2048 + w0:2048 + w1],
                start=True, stop=True),
                reads=[("ONERF",)] + stf(2048), writes=pst(0))
            em.s(lambda e, w0=w0, w1=w1: e.activation(
                CON[:, dst_off + w0:dst_off + w1], PS[:, 0:w1 - w0], AF.Copy, scale=scale),
                reads=pst(0), writes=[("CON", tagname)])

    bcast_gain(qnw, D, 3072, SCALE, "gq")
    bcast_gain(knw, 512, 5120, f32(1.0), "gk")
    for vec, cbase, tag in ((ln1, 0, ("LNC", 1)), (ln2, 16, ("LNC", 2))):
        em.dsync(lambda e, vec=vec: e.dma_start(out=STF[0:1, 2048:4096], in_=vec[None, :]),
                 writes=stf(2048))
        for c in range(16):
            em.t(lambda e, c=c: e.transpose(
                PS[:, 1024 + (c % 2) * 512:1024 + (c % 2) * 512 + 1],
                STF[0:1, 2048 + c * 128:2048 + c * 128 + 128], IDE[0:1, 0:1]),
                reads=stf(2048) + [("IDE",)], writes=pst(1024 + (c % 2) * 512))
            em.s(lambda e, c=c, cbase=cbase: e.activation(
                LNC[:, cbase + c:cbase + c + 1],
                PS[:, 1024 + (c % 2) * 512:1024 + (c % 2) * 512 + 1],
                AF.Copy, scale=f32(1.0)),
                reads=pst(1024 + (c % 2) * 512), writes=[tag])
    em.dsync(lambda e: e.dma_start(out=WR8[:, :].rearrange("p (c w) -> p c w", c=16),
                                   in_=rw.rearrange("(c p) w -> p c w", p=128)),
             writes=[("WR8",)])
    em.dsync(lambda e: e.dma_start(out=STF[0:1, 2048:2056], in_=rb[None, :]), writes=stf(2048))
    em.t(lambda e: e.matmul(PS[:, 0:8], ONERF[0:1, :], STF[0:1, 2048:2056], start=True, stop=True),
         reads=[("ONERF",)] + stf(2048), writes=pst(0))
    em.s(lambda e: e.activation(T8[:, 56:64], PS[:, 0:8], AF.Copy, scale=f32(1.0)),
         reads=pst(0), writes=[("T8", "bias")])

    def prep_wo_bf(src, dst, tag):
        for dc in range(16):
            em.dsync(lambda e, dc=dc: e.dma_start(
                out=STF[:, 2048:4096], in_=src[dc * 128:dc * 128 + 128, :]),
                writes=stf(2048))
            em.v(lambda e, dc=dc: e.tensor_copy(STB[:, 0:2048], STF[:, 2048:4096]),
                 reads=stf(2048), writes=stb(0))
            em.dscal(lambda e, dc=dc: e.dma_start(
                out=dst[dc * 128:dc * 128 + 128, :], in_=STB[:, 0:2048]),
                reads=stb(0), writes=[(tag, dc)])

    prep_wo_bf(ao, aob_dr, "aob")
    if N_SUB >= 2:
        prep_wo_bf(fwo, fob_dr, "fob")

    def prep_bw(src3, dst, scale, vmode=False, tagbase="w"):
        for h in range(NH):
            ws = 2048 + (h % 2) * 2048
            em.dsync(lambda e, h=h, ws=ws: e.dma_start(
                out=STF[:, ws:ws + 2048], in_=src3[h]),
                writes=stf(ws))
            for dc in range(16):
                pb = 2048 + (dc % 4) * 512
                em.t(lambda e, h=h, dc=dc, pb=pb, ws=ws: e.transpose(
                    PS[:, pb:pb + 128], STF[:, ws + dc * 128:ws + dc * 128 + 128],
                    IDE[:, :]),
                    reads=stf(ws) + [("IDE",)], writes=pst(pb))
                if scale != 1.0:
                    em.v(lambda e, dc=dc, pb=pb: e.tensor_scalar(
                        STB[:, dc * 128:dc * 128 + 128], PS[:, pb:pb + 128],
                        float(scale), None, ALU.mult),
                        reads=pst(pb), writes=stb(0))
                else:
                    em.v(lambda e, dc=dc, pb=pb: e.tensor_copy(
                        STB[:, dc * 128:dc * 128 + 128], PS[:, pb:pb + 128]),
                        reads=pst(pb), writes=stb(0))
            if vmode:
                em.dscal(lambda e, h=h: e.dma_start(
                    out=dst[:, :, h * 128:(h + 1) * 128].rearrange("c p e -> p c e"),
                    in_=STB[:, 0:2048].rearrange("p (c e) -> p c e", c=16)),
                    reads=stb(0), writes=[("wv_scr",)])
            else:
                em.dscal(lambda e, h=h: e.dma_start(out=dst[h], in_=STB[:, 0:2048]),
                         reads=stb(0), writes=[(tagbase, h)])

    if N_SUB >= 3:
        prep_bw(bwq, wqT_dr, SCALE, tagbase="wqs")
        prep_bw(bwk, wkT_dr, 1.0, tagbase="wks")
        prep_bw(bwv, wvT_dr, 1.0, vmode=True)
        for h in range(NH):
            em.dsync(lambda e, h=h: e.dma_start(
                out=STF[:, 2048:4096].rearrange("p (c e) -> p c e", c=16),
                in_=bwo[h].rearrange("(c p) e -> p c e", p=128)),
                writes=stf(2048))
            for dc in range(16):
                pb = 2048 + (dc % 4) * 512
                em.t(lambda e, dc=dc, pb=pb: e.transpose(
                    PS[:, pb:pb + 128], STF[:, 2048 + dc * 128:2048 + dc * 128 + 128],
                    IDE[:, :]),
                    reads=stf(2048) + [("IDE",)], writes=pst(pb))
                em.v(lambda e, dc=dc, pb=pb: e.tensor_copy(
                    STB[:, dc * 128:dc * 128 + 128], PS[:, pb:pb + 128]),
                    reads=pst(pb), writes=stb(0))
            em.dscal(lambda e, h=h: e.dma_start(out=woT_dr[h], in_=STB[:, 0:2048]),
                     reads=stb(0), writes=[("wos", h)])

    def phase0(sub, kind):
        xsrc = x_in if sub == 0 else xbuf
        for st in range(NT):
            xs = (st % 2) * 2048
            r0 = st * 128
            em.dsync(lambda e, xs=xs, r0=r0: e.dma_start(
                out=STF[:, xs:xs + 2048], in_=xsrc[r0:r0 + 128, :]),
                reads=[("xb", st)] if sub > 0 else [], writes=stf(xs))
            if sub == 0:
                em.dscal(lambda e, xs=xs, r0=r0: e.dma_start(
                    out=xbuf[r0:r0 + 128, :], in_=STF[:, xs:xs + 2048]),
                    reads=stf(xs), writes=[("xb", st)])
            if sub == N_SUB - 1:
                em.dscal(lambda e, xs=xs, r0=r0: e.dma_start(
                    out=out_d[r0:r0 + 128, :], in_=STF[:, xs:xs + 2048]),
                    reads=stf(xs), writes=[("ob", st)])
            em.v(lambda e, xs=xs: e.reduce_sum(SMF[:, 0:1], STF[:, xs:xs + 2048], axis=AX.X),
                 reads=stf(xs), writes=[("SMF", 0)])
            em.s(lambda e, xs=xs: e.activation(
                STF[:, 4096:6144], STF[:, xs:xs + 2048], AF.Square, accum_out=SMF[:, 1:2]),
                reads=stf(xs), writes=stf(4096) + [("SMF", 1)])
            em.v(lambda e: e.tensor_scalar_mul(SMF[:, 2:3], SMF[:, 0:1], 1.0 / D),
                 reads=[("SMF", 0)], writes=[("SMF", 2)])
            em.v(lambda e, st=st: e.tensor_copy(STASH[:, st:st + 1], SMF[:, 2:3]),
                 reads=[("SMF", 2)], writes=[("STH", "m", st)])
            em.v(lambda e: e.tensor_mul(SMF[:, 3:4], SMF[:, 2:3], SMF[:, 2:3]),
                 reads=[("SMF", 2)], writes=[("SMF", 3)])
            em.v(lambda e: e.scalar_tensor_tensor(
                SMF[:, 4:5], SMF[:, 1:2], 1.0 / D, SMF[:, 3:4],
                ALU.mult, ALU.subtract),
                reads=[("SMF", 1), ("SMF", 3)], writes=[("SMF", 4)])
            em.s(lambda e: e.activation(SMF[:, 5:6], SMF[:, 4:5], AF.Ln, bias=f32(1e-5)),
                 reads=[("SMF", 4)], writes=[("SMF", 5)])
            em.s(lambda e, st=st: e.activation(
                STASH[:, 32 + st:32 + st + 1], SMF[:, 5:6], AF.Exp, scale=f32(-0.5)),
                reads=[("SMF", 5)], writes=[("STH", "r", st)])
            if kind == "ffn":
                em.v(lambda e, st=st: e.tensor_mul(
                    SMF[:, 8:9], STASH[:, 32 + st:32 + st + 1], STASH[:, 32 + st:32 + st + 1]),
                    reads=[("STH", "r", st)], writes=[("SMF", 8)])
                em.v(lambda e: e.tensor_mul(SMF[:, 9:10], SMF[:, 4:5], SMF[:, 8:9]),
                     reads=[("SMF", 4), ("SMF", 8)], writes=[("SMF", 9)])
                em.s(lambda e: e.activation(SMF[:, 10:11], SMF[:, 9:10], AF.Ln,
                                            bias=f32(1e-6)),
                     reads=[("SMF", 9)], writes=[("SMF", 10)])
                em.s(lambda e: e.activation(SMF[:, 11:12], SMF[:, 10:11], AF.Exp,
                                            scale=f32(-0.5)),
                     reads=[("SMF", 10)], writes=[("SMF", 11)])
                em.v(lambda e, st=st: e.tensor_mul(
                    SMF[:, 6:7], STASH[:, 32 + st:32 + st + 1], SMF[:, 11:12]),
                    reads=[("STH", "r", st), ("SMF", 11)], writes=[("SMF", 6)])
                em.v(lambda e, st=st: e.tensor_scalar(
                    STASH[:, 48 + st:48 + st + 1], SMF[:, 2:3],
                    STASH[:, 32 + st:32 + st + 1], -1.0, ALU.mult, ALU.mult),
                    reads=[("SMF", 2), ("STH", "r", st)], writes=[("STH", "bln", st)])
            else:
                em.v(lambda e, st=st: e.tensor_copy(SMF[:, 6:7], STASH[:, 32 + st:32 + st + 1]),
                     reads=[("STH", "r", st)], writes=[("SMF", 6)])
            em.v(lambda e, st=st: e.tensor_scalar(
                STASH[:, 16 + st:16 + st + 1], SMF[:, 2:3], SMF[:, 6:7], -1.0,
                ALU.mult, ALU.mult),
                reads=[("SMF", 2), ("SMF", 6)], writes=[("STH", "b", st)])
            ys = (st % 2) * 2048
            em.v(lambda e, xs=xs, ys=ys, st=st: e.tensor_scalar(
                STB[:, ys:ys + 2048], STF[:, xs:xs + 2048], SMF[:, 6:7],
                STASH[:, 16 + st:16 + st + 1], ALU.mult, ALU.add),
                reads=stf(xs) + [("SMF", 6), ("STH", "b", st)], writes=stb(ys))
            em.dsync(lambda e, ys=ys, r0=r0: e.dma_start(
                out=y_dr[r0:r0 + 128, :], in_=STB[:, ys:ys + 2048]),
                reads=stb(ys), writes=[("ydr", r0 // 128)])

    def load_yT(kind):
        ally = [("ydr", t) for t in range(NT)]
        for dc in range(16):
            em.dscal(lambda e, dc=dc: e.dma_start(
                out=BIG1[:, dc * 2048:(dc + 1) * 2048],
                in_=y_dr[:, dc * 128:(dc + 1) * 128], transpose=True),
                reads=ally, writes=b1(dc * 2048, (dc + 1) * 2048))
            if kind == "ffn":
                em.v(lambda e, dc=dc: e.tensor_scalar(
                    BIG1[:, dc * 2048:(dc + 1) * 2048],
                    BIG1[:, dc * 2048:(dc + 1) * 2048],
                    LNC[:, dc:dc + 1], None, ALU.mult),
                    reads=b1(dc * 2048, (dc + 1) * 2048) + [("LNC", 1)],
                    writes=b1(dc * 2048, (dc + 1) * 2048))

    def rope_natural(qf_off, out_off, st, nheads, mode, reads, writes):
        if mode == "full":
            half, coff, soff, tw, rot = 64, 0, 1024, 64, 128
        else:
            half, coff, soff, tw, rot = 32, 2048, 2560, 32, 64
        qv = STF[:, qf_off:qf_off + nheads * 128].rearrange("p (h e) -> p h e", h=nheads)
        ov = STB[:, out_off:out_off + nheads * 128].rearrange("p (h e) -> p h e", h=nheads)
        cosb = CON[:, coff + st * tw:coff + st * tw + half]
        sinb = CON[:, soff + st * tw:soff + st * tw + half]
        cosB = bass.AP(cosb.tensor, cosb.offset, [cosb.ap[0], [0, nheads], cosb.ap[1]])
        sinB = bass.AP(sinb.tensor, sinb.offset, [sinb.ap[0], [0, nheads], sinb.ap[1]])
        t1 = STF[:, 6144:6144 + nheads * half].rearrange("p (h e) -> p h e", h=nheads)
        t2 = STF[:, 7168:7168 + nheads * half].rearrange("p (h e) -> p h e", h=nheads)
        lo = (slice(None), slice(None), slice(0, half))
        hi = (slice(None), slice(None), slice(half, 2 * half))
        rd = list(reads) + [("CON", "tab")]
        wrt = list(writes)
        t3 = stf(6144, 8192)
        em.v(lambda e: e.tensor_mul(t1, qv[lo], cosB), reads=rd, writes=t3)
        em.v(lambda e: e.tensor_mul(t2, qv[hi], sinB), reads=rd, writes=t3)
        em.v(lambda e: e.tensor_sub(ov[lo], t1, t2), reads=t3, writes=wrt)
        em.v(lambda e: e.tensor_mul(t1, qv[hi], cosB), reads=rd, writes=t3)
        em.v(lambda e: e.tensor_mul(t2, qv[lo], sinB), reads=rd, writes=t3)
        em.v(lambda e: e.tensor_add(ov[hi], t1, t2), reads=t3, writes=wrt)
        if mode == "part":
            pas = (slice(None), slice(None), slice(rot, 128))
            em.v(lambda e: e.tensor_copy(ov[pas], qv[pas]), reads=rd, writes=wrt)

    def phase1_qkv(kind):
        wq_d, wk_d, wv_d = (aq, ak, av) if kind == "attn_a" else (fwq, fwk, fwv)
        gq_rd = [("CON", "gq")] if kind == "ffn" else []
        gk_rd = [("CON", "gk")] if kind == "ffn" else []
        mode = "full" if kind == "attn_a" else "part"
        em.dpool(lambda e: e.dma_start(
            out=BIG2[:, 0:32768].rearrange("p (c e) -> p c e", c=16),
            in_=wq_d.rearrange("(c p) e -> p c e", p=128)),
            reads=[], writes=b2(0, 32768))
        for st in range(NT):
            pa = (st % 2) * 2048
            for dc in range(16):
                for n in range(4):
                    em.t(lambda e, st=st, dc=dc, n=n, pa=pa: e.matmul(
                        PS[:, pa + n * 512:pa + n * 512 + 512],
                        BIG1[:, dc * 2048 + st * 128:dc * 2048 + st * 128 + 128],
                        BIG2[:, dc * 2048 + n * 512:dc * 2048 + n * 512 + 512],
                        start=(dc == 0), stop=(dc == 15)),
                        reads=b1(dc * 2048 + st * 128) + b2(dc * 2048, (dc + 1) * 2048),
                        writes=pst(pa + n * 512))
            pb_rd = pst(pa, pa + 2048)
            em.s(lambda e, pa=pa: e.activation(
                STF[:, 4096:6144], PS[:, pa:pa + 2048], AF.Square, accum_out=SMF[:, 12:13]),
                reads=pb_rd, writes=stf(4096) + [("SMF", 12)])
            em.v(lambda e: e.tensor_scalar_mul(SMF[:, 13:14], SMF[:, 12:13], 1.0 / D),
                 reads=[("SMF", 12)], writes=[("SMF", 13)])
            em.s(lambda e: e.activation(SMF[:, 14:15], SMF[:, 13:14], AF.Ln, bias=f32(1e-6)),
                 reads=[("SMF", 13)], writes=[("SMF", 14)])
            em.s(lambda e: e.activation(SMF[:, 15:16], SMF[:, 14:15], AF.Exp, scale=f32(-0.5)),
                 reads=[("SMF", 14)], writes=[("SMF", 15)])
            if kind == "attn_a":
                em.v(lambda e, pa=pa: e.tensor_scalar(
                    STF[:, 0:2048], PS[:, pa:pa + 2048], SMF[:, 15:16], float(SCALE),
                    ALU.mult, ALU.mult),
                    reads=pb_rd + [("SMF", 15)], writes=stf(0))
            else:
                em.v(lambda e, pa=pa: e.scalar_tensor_tensor(
                    STF[:, 0:2048], PS[:, pa:pa + 2048], SMF[:, 15:16],
                    CON[:, 3072:5120], ALU.mult, ALU.mult),
                    reads=pb_rd + [("SMF", 15)] + gq_rd, writes=stf(0))
            rope_natural(0, 0, st, 16, mode, reads=stf(0), writes=stb(0))
            em.dsync(lambda e, st=st: e.dma_start(
                out=q_dr[st * 128:st * 128 + 128, :], in_=STB[:, 0:2048]),
                reads=stb(0), writes=[("qdr", st)])
        em.dpool(lambda e: e.dma_start(
            out=BIG2[:, 0:8192].rearrange("p (c e) -> p c e", c=16),
            in_=wk_d.rearrange("(c p) e -> p c e", p=128)),
            reads=[], writes=b2(0, 8192))
        for st in range(NT):
            pa = (st % 2) * 2048
            for dc in range(16):
                em.t(lambda e, st=st, dc=dc, pa=pa: e.matmul(
                    PS[:, pa:pa + 512],
                    BIG1[:, dc * 2048 + st * 128:dc * 2048 + st * 128 + 128],
                    BIG2[:, dc * 512:dc * 512 + 512],
                    start=(dc == 0), stop=(dc == 15)),
                    reads=b1(dc * 2048 + st * 128) + b2(dc * 512),
                    writes=pst(pa))
            em.s(lambda e, pa=pa: e.activation(
                STF[:, 4096:4608], PS[:, pa:pa + 512], AF.Square, accum_out=SMF[:, 12:13]),
                reads=pst(pa), writes=stf(4096) + [("SMF", 12)])
            em.v(lambda e: e.tensor_scalar_mul(SMF[:, 13:14], SMF[:, 12:13], 1.0 / 512),
                 reads=[("SMF", 12)], writes=[("SMF", 13)])
            em.s(lambda e: e.activation(SMF[:, 14:15], SMF[:, 13:14], AF.Ln, bias=f32(1e-6)),
                 reads=[("SMF", 13)], writes=[("SMF", 14)])
            em.s(lambda e: e.activation(SMF[:, 15:16], SMF[:, 14:15], AF.Exp, scale=f32(-0.5)),
                 reads=[("SMF", 14)], writes=[("SMF", 15)])
            if kind == "attn_a":
                em.v(lambda e, pa=pa: e.tensor_scalar(
                    STF[:, 0:512], PS[:, pa:pa + 512], SMF[:, 15:16], None, ALU.mult),
                    reads=pst(pa) + [("SMF", 15)], writes=stf(0))
            else:
                em.v(lambda e, pa=pa: e.scalar_tensor_tensor(
                    STF[:, 0:512], PS[:, pa:pa + 512], SMF[:, 15:16],
                    CON[:, 5120:5632], ALU.mult, ALU.mult),
                    reads=pst(pa) + [("SMF", 15)] + gk_rd, writes=stf(0))
            rope_natural(0, 0, st, 4, mode, reads=stf(0), writes=stb(0))
            em.dsync(lambda e, st=st: e.dma_start(
                out=k_dr[st * 128:st * 128 + 128, :], in_=STB[:, 0:512]),
                reads=stb(0), writes=[("kdr", st)])
        em.dpool(lambda e: e.dma_start(
            out=BIG2[:, 0:8192].rearrange("p (c e) -> p c e", c=16),
            in_=wv_d.rearrange("(c p) e -> p c e", p=128)),
            reads=[], writes=b2(0, 8192))
        for st in range(NT):
            pa = (st % 2) * 2048
            for dc in range(16):
                em.t(lambda e, st=st, dc=dc, pa=pa: e.matmul(
                    PS[:, pa:pa + 512],
                    BIG1[:, dc * 2048 + st * 128:dc * 2048 + st * 128 + 128],
                    BIG2[:, dc * 512:dc * 512 + 512],
                    start=(dc == 0), stop=(dc == 15)),
                    reads=b1(dc * 2048 + st * 128) + b2(dc * 512),
                    writes=pst(pa))
            em.v(lambda e, st=st, pa=pa: e.tensor_copy(
                BIG2[:, 8192 + st * 512:8192 + st * 512 + 512], PS[:, pa:pa + 512]),
                reads=pst(pa), writes=b2(8192 + st * 512))
            if "spill" in os.environ.get("DEV_TAP", ""):
                em.dsync(lambda e, st=st: e.dma_start(
                    out=v_dr[st * 128:st * 128 + 128, 0:512],
                    in_=BIG2[:, 8192 + st * 512:8192 + st * 512 + 512]),
                    reads=b2(8192 + st * 512), writes=[("vdr", st, 0)])

    def phase1_b():
        for which, wsrc, dst, dtag in (("wqs", wqT_dr, qT_dr, "qTd"),
                                       ("wks", wkT_dr, kT_dr, "kTd")):
            for h in range(NH):
                ws = (h % 2) * 2048
                em.dpool(lambda e, h=h, ws=ws, wsrc=wsrc: e.dma_start(
                    out=STB[:, ws:ws + 2048], in_=wsrc[h]),
                    reads=[(which, h)], writes=stb(ws))
                pa = (h % 2) * 2048
                for dc in range(16):
                    for n in range(4):
                        em.t(lambda e, dc=dc, n=n, pa=pa, ws=ws: e.matmul(
                            PS[:, pa + n * 512:pa + n * 512 + 512],
                            STB[:, ws + dc * 128:ws + dc * 128 + 128],
                            BIG1[:, dc * 2048 + n * 512:dc * 2048 + n * 512 + 512],
                            start=(dc == 0), stop=(dc == 15)),
                            reads=stb(ws) + b1(dc * 2048, (dc + 1) * 2048),
                            writes=pst(pa + n * 512))
                em.v(lambda e, pa=pa: e.tensor_copy(STB[:, 4096:6144], PS[:, pa:pa + 2048]),
                     reads=pst(pa, pa + 2048), writes=stb(4096))
                em.dsync(lambda e, h=h, dst=dst: e.dma_start(out=dst[h], in_=STB[:, 4096:6144]),
                         reads=stb(4096), writes=[(dtag, h)])
        for eh in range(2):
            em.dpool(lambda e, eh=eh: e.dma_start(
                out=BIG2[:, 16384:32768].rearrange("p (c e) -> p c e", c=16),
                in_=wvT_dr[:, :, eh * 1024:eh * 1024 + 1024].rearrange("c p e -> p c e")),
                reads=[("wv_scr",)], writes=b2(16384, 32768))
            for st in range(NT):
                pa = (st % 2) * 2048
                for dc in range(16):
                    for n in range(2):
                        em.t(lambda e, st=st, dc=dc, n=n, pa=pa: e.matmul(
                            PS[:, pa + n * 512:pa + n * 512 + 512],
                            BIG1[:, dc * 2048 + st * 128:dc * 2048 + st * 128 + 128],
                            BIG2[:, 16384 + dc * 1024 + n * 512:16384 + dc * 1024 + n * 512 + 512],
                            start=(dc == 0), stop=(dc == 15)),
                            reads=b1(dc * 2048 + st * 128) + b2(16384 + dc * 1024, 16384 + (dc + 1) * 1024),
                            writes=pst(pa + n * 512))
                em.v(lambda e, pa=pa, eh=eh: e.tensor_copy(
                    STB[:, 4096 + eh * 1024:4096 + eh * 1024 + 1024], PS[:, pa:pa + 1024]),
                    reads=pst(pa, pa + 1024), writes=stb(4096))
                em.dsync(lambda e, st=st, eh=eh: e.dma_start(
                    out=v_dr[st * 128:st * 128 + 128, eh * 1024:eh * 1024 + 1024],
                    in_=STB[:, 4096 + eh * 1024:4096 + eh * 1024 + 1024]),
                    reads=stb(4096), writes=[("vdr", st, eh)])

    def phase2(sub, kind):
        causal = kind != "attn_a" and "nomask" not in os.environ.get("DEV_TAP", "")
        is_b = kind == "attn_b"
        xdst = out_d if sub == N_SUB - 1 else xbuf
        xtag = "ob" if sub == N_SUB - 1 else "xb"
        wo_bf = aob_dr if kind == "attn_a" else (fob_dr if kind == "ffn" else None)
        if not is_b:
            for kv in range(4):
                em.dscal(lambda e, kv=kv: e.dma_start(
                    out=BIG2[:, kv * 2048:kv * 2048 + 2048],
                    in_=k_dr[:, kv * 128:kv * 128 + 128], transpose=True),
                    reads=[("kdr", t) for t in range(NT)],
                    writes=b2(kv * 2048, (kv + 1) * 2048))
        QG0, PB0, WO0, ZA0, KT0, VH0 = 0, 1536, 4096, 8192, 16384, 20480
        for g in range(4):
            qt0 = g * 4
            nch = (qt0 + 4) if causal else NT
            for h in range(NH):
                kvh = h // 4
                ring = h % 3
                qgs = QG0 + ring * 512
                if is_b:
                    em.dscal(lambda e, h=h, g=g, qgs=qgs: e.dma_start(
                        out=BIG1[:, qgs:qgs + 512], in_=qT_dr[h][:, g * 512:g * 512 + 512]),
                        reads=[("qTd", h)], writes=b1(qgs))
                    kts = KT0 + (h % 2) * 2048
                    em.dscal(lambda e, h=h, kts=kts, nch=nch: e.dma_start(
                        out=BIG1[:, kts:kts + nch * 128], in_=kT_dr[h][:, 0:nch * 128]),
                        reads=[("kTd", h)], writes=b1(kts, kts + 2048))
                    vhs = VH0 + (h % 2) * 2048
                    em.dscal(lambda e, h=h, vhs=vhs, nch=nch: e.dma_start(
                        out=BIG1[:, vhs:vhs + nch * 128].rearrange("p (c e) -> p c e", c=nch),
                        in_=v_dr[0:nch * 128, h * 128:h * 128 + 128].rearrange(
                            "(c p) e -> p c e", p=128)),
                        reads=[("vdr", t, eh) for t in range(nch) for eh in range(2)],
                        writes=b1(vhs, vhs + 2048))
                    kt_ap = (lambda c, kts=kts: BIG1[:, kts + c * 128:kts + c * 128 + 128])
                    kt_rd = (lambda c, kts=kts: b1(kts + c * 128))
                    v_ap = (lambda c, vhs=vhs: BIG1[:, vhs + c * 128:vhs + c * 128 + 128])
                    v_rd = (lambda c, vhs=vhs: b1(vhs + c * 128))
                else:
                    em.dscal(lambda e, h=h, g=g, qgs=qgs: e.dma_start(
                        out=BIG1[:, qgs:qgs + 512],
                        in_=q_dr[g * 512:g * 512 + 512, h * 128:h * 128 + 128],
                        transpose=True),
                        reads=[("qdr", t) for t in range(qt0, qt0 + 4)], writes=b1(qgs))
                    kt_ap = (lambda c, kvh=kvh: BIG2[:, kvh * 2048 + c * 128:kvh * 2048 + c * 128 + 128])
                    kt_rd = (lambda c, kvh=kvh: b2(kvh * 2048 + c * 128))
                    v_ap = (lambda c, kvh=kvh: BIG2[:, 8192 + c * 512 + kvh * 128:8192 + c * 512 + kvh * 128 + 128])
                    v_rd = (lambda c: b2(8192 + c * 512))
                for j in range(4):
                    qt = qt0 + j
                    em.t(lambda e, j=j, qt=qt, qgs=qgs, kt_ap=kt_ap: e.matmul(
                        PS[:, 2048 + j * 128:2048 + j * 128 + 128],
                        BIG1[:, qgs + j * 128:qgs + j * 128 + 128],
                        kt_ap(qt), start=True, stop=True),
                        reads=b1(qgs) + kt_rd(qt), writes=pst(2048, 2560))
                    em.v(lambda e, j=j: e.reduce_max(
                        MCOL[:, j:j + 1], PS[:, 2048 + j * 128:2048 + j * 128 + 128],
                        axis=AX.X),
                        reads=pst(2048, 2560), writes=[("MCOL", j)])
                    em.t(lambda e, j=j: e.transpose(
                        PS[0:1, 2560 + j * 128:2560 + j * 128 + 128],
                        MCOL[:, j:j + 1], IDE[:, :]),
                        reads=[("MCOL", j), ("IDE",)], writes=pst(2560, 3072))
                em.s(lambda e: e.activation(MROW[0:1, :], PS[0:1, 2560:3072], AF.Copy,
                                            scale=f32(-1.0)),
                     reads=pst(2560, 3072), writes=[("MROW",)])
                for c in range(nch):
                    pb = (c % 2) * 512
                    pbs = PB0 + (c % 3) * 512
                    em.t(lambda e, c=c, pb=pb, qgs=qgs, kt_ap=kt_ap: e.matmul(
                        PS[:, pb:pb + 512], kt_ap(c), BIG1[:, qgs:qgs + 512],
                        start=True, stop=False),
                        reads=kt_rd(c) + b1(qgs), writes=pst(pb))
                    em.t(lambda e, pb=pb: e.matmul(
                        PS[:, pb:pb + 512], ONERB[0:1, :], MROW[0:1, :],
                        start=False, stop=True),
                        reads=[("ONERB",), ("MROW",)], writes=pst(pb))
                    em.s(lambda e, pb=pb, pbs=pbs: e.activation(
                        BIG1[:, pbs:pbs + 512], PS[:, pb:pb + 512], AF.Exp),
                        reads=pst(pb), writes=b1(pbs))
                    if causal and c >= qt0:
                        r = c - qt0
                        em.v(lambda e, r=r, pbs=pbs: e.tensor_mul(
                            BIG1[:, pbs:pbs + 512], BIG1[:, pbs:pbs + 512],
                            MSK[:, r * 512:r * 512 + 512]),
                            reads=b1(pbs) + [("MSK",)], writes=b1(pbs))
                    em.t(lambda e, c=c, pbs=pbs: e.matmul(
                        PS[0:1, 1536:2048], ONEC[:, :], BIG1[:, pbs:pbs + 512],
                        start=(c == 0), stop=(c == nch - 1)),
                        reads=[("ONEC",)] + b1(pbs), writes=pst(1536))
                    em.t(lambda e, c=c, pbs=pbs, v_ap=v_ap: e.matmul(
                        PS[:, 1024:1536], v_ap(c), BIG1[:, pbs:pbs + 512],
                        start=(c == 0), stop=(c == nch - 1)),
                        reads=v_rd(c) + b1(pbs), writes=pst(1024))
                em.v(lambda e: e.reciprocal(RCROW[0:1, :], PS[0:1, 1536:2048]),
                     reads=pst(1536), writes=[("RCROW",)])
                em.t(lambda e: e.matmul(PS[:, 3072:3584], ONERF[0:1, :], RCROW[0:1, :],
                                        start=True, stop=True),
                     reads=[("ONERF",), ("RCROW",)], writes=pst(3072))
                em.s(lambda e: e.activation(BCS[:, :], PS[:, 3072:3584], AF.Copy,
                                            scale=f32(1.0)),
                     reads=pst(3072), writes=[("BCS",)])
                em.v(lambda e, h=h: e.tensor_mul(
                    BIG1[:, ZA0 + h * 512:ZA0 + h * 512 + 512], PS[:, 1024:1536],
                    BCS[:, :]),
                    reads=pst(1024) + [("BCS",)], writes=b1(ZA0 + h * 512))
            if "spill" in os.environ.get("DEV_TAP", "") and kind == "ffn":
                em.dsync(lambda e, g=g: e.dma_start(
                    out=zall_dr[g], in_=BIG1[:, ZA0:ZA0 + 8192]),
                    reads=b1(ZA0, ZA0 + 8192), writes=[("zspill", g)])
            for jp in range(2):
                for h in range(NH):
                    ws = WO0 + (h % 2) * 2048
                    if is_b:
                        em.dpool(lambda e, h=h, ws=ws: e.dma_start(
                            out=BIG1[:, ws:ws + 2048], in_=woT_dr[h]),
                            reads=[("wos", h)], writes=b1(ws, ws + 2048))
                    else:
                        em.dpool(lambda e, h=h, ws=ws: e.dma_start(
                            out=BIG1[:, ws:ws + 2048],
                            in_=wo_bf[h * 128:h * 128 + 128, :]),
                            reads=[("aob" if kind == "attn_a" else "fob", h)],
                            writes=b1(ws, ws + 2048))
                    for jj in range(2):
                        j = jp * 2 + jj
                        for n in range(4):
                            em.t(lambda e, h=h, j=j, jj=jj, n=n, ws=ws: e.matmul(
                                PS[:, jj * 2048 + n * 512:jj * 2048 + n * 512 + 512],
                                BIG1[:, ZA0 + h * 512 + j * 128:ZA0 + h * 512 + j * 128 + 128],
                                BIG1[:, ws + n * 512:ws + n * 512 + 512],
                                start=(h == 0), stop=(h == NH - 1)),
                                reads=b1(ZA0 + h * 512) + b1(ws + n * 512),
                                writes=pst(jj * 2048 + n * 512))
                for jj in range(2):
                    j = jp * 2 + jj
                    st = qt0 + j
                    obanks = pst(jj * 2048, jj * 2048 + 2048)
                    if kind != "ffn" or "plainffn" in os.environ.get("DEV_TAP", ""):
                        osl = 4096 + jj * 2048
                        em.v(lambda e, jj=jj, osl=osl: e.tensor_copy(
                            STF[:, osl:osl + 2048], PS[:, jj * 2048:jj * 2048 + 2048]),
                            reads=obanks, writes=stf(osl))
                        em.dpool(lambda e, st=st, osl=osl: e.dma_start(
                            out=xdst[st * 128:st * 128 + 128, :],
                            in_=STF[:, osl:osl + 2048], accum_op=ALU.add),
                            reads=stf(osl), writes=[(xtag, st)])
                    else:
                        ffn_epilogue(sub, st, jj, xdst, xtag)

    def ffn_epilogue(sub, st, jj, xdst, xtag):
        xs = (st % 2) * 2048
        obanks = pst(jj * 2048, jj * 2048 + 2048)
        em.dsync(lambda e, xs=xs, st=st: e.dma_start(
            out=STF[:, xs:xs + 2048], in_=xbuf[st * 128:st * 128 + 128, :]),
            reads=[("xb", st)], writes=stf(xs))
        em.s(lambda e, xs=xs, st=st: e.activation(
            STF[:, 4096:6144], STF[:, xs:xs + 2048], AF.Identity,
            bias=STASH[:, 48 + st:48 + st + 1], scale=STASH[:, 32 + st:32 + st + 1]),
            reads=stf(xs) + [("STH", "bln", st), ("STH", "r", st)],
            writes=stf(4096))
        em.v(lambda e, jj=jj: e.tensor_add(
            STF[:, 6144:8192], STF[:, 4096:6144], PS[:, jj * 2048:jj * 2048 + 2048]),
            reads=stf(4096) + obanks, writes=stf(6144))
        em.dpool(lambda e, st=st: e.dma_start(
            out=xdst[st * 128:st * 128 + 128, :], in_=STF[:, 6144:8192],
            accum_op=ALU.add),
            reads=stf(6144), writes=[(xtag, st)])
        em.s(lambda e, xs=xs: e.activation(
            STF[:, xs:xs + 2048], STF[:, 6144:8192], AF.Square, accum_out=SMF[:, 20:21]),
            reads=stf(6144), writes=stf(xs) + [("SMF", 20)])
        em.v(lambda e: e.tensor_scalar_mul(SMF[:, 21:22], SMF[:, 20:21], 1.0 / D),
             reads=[("SMF", 20)], writes=[("SMF", 21)])
        em.s(lambda e: e.activation(SMF[:, 22:23], SMF[:, 21:22], AF.Ln, bias=f32(1e-6)),
             reads=[("SMF", 21)], writes=[("SMF", 22)])
        em.s(lambda e: e.activation(SMF[:, 23:24], SMF[:, 22:23], AF.Exp, scale=f32(-0.5)),
             reads=[("SMF", 22)], writes=[("SMF", 23)])
        em.v(lambda e: e.tensor_scalar(
            STF[:, 4096:6144], STF[:, 6144:8192], SMF[:, 23:24], None, ALU.mult),
            reads=stf(6144) + [("SMF", 23)], writes=stf(4096))
        em.v(lambda e: e.tensor_copy(STB[:, 0:2048], STF[:, 4096:6144]),
             reads=stf(4096), writes=stb(0))
        em.dsync(lambda e, st=st: e.dma_start(
            out=t_dr[st * 128:st * 128 + 128, :], in_=STB[:, 0:2048]),
            reads=stb(0), writes=[("tdr", st)])
        rb0 = jj * 2048
        for dc in range(16):
            pb = rb0 + (dc % 2) * 512
            em.t(lambda e, dc=dc, pb=pb: e.transpose(
                PS[:, pb:pb + 128], STF[:, 4096 + dc * 128:4096 + dc * 128 + 128],
                IDE[:, :]),
                reads=stf(4096) + [("IDE",)], writes=pst(pb))
            em.v(lambda e, dc=dc, pb=pb: e.tensor_scalar(
                BCS[:, 0:128], PS[:, pb:pb + 128], LNC[:, 16 + dc:16 + dc + 1],
                None, ALU.mult),
                reads=pst(pb) + [("LNC", 2)], writes=[("BCS",)])
            em.t(lambda e, dc=dc, rb0=rb0: e.matmul(
                PS[:, rb0 + 1024:rb0 + 1032], BCS[:, 0:128], WR8[:, dc * 8:dc * 8 + 8],
                start=(dc == 0), stop=(dc == 15)),
                reads=[("BCS",), ("WR8",)], writes=pst(rb0 + 1024))
        em.s(lambda e, rb0=rb0: e.activation(T8[:, 0:8], PS[:, rb0 + 1024:rb0 + 1032], AF.Sigmoid),
             reads=pst(rb0 + 1024), writes=[("T8", 0)])
        em.v(lambda e: e.tensor_add(T8[:, 8:16], T8[:, 0:8], T8[:, 56:64]),
             reads=[("T8", 0), ("T8", "bias")], writes=[("T8", 1)])
        em.v(lambda e: e.reduce_max(SMF[:, 24:25], T8[:, 8:16], axis=AX.X),
             reads=[("T8", 1)], writes=[("SMF", 24)])
        em.v(lambda e: e.tensor_scalar(T8[:, 16:24], T8[:, 8:16], SMF[:, 24:25],
                                       None, ALU.is_ge),
             reads=[("T8", 1), ("SMF", 24)], writes=[("T8", 2)])
        em.v(lambda e: e.scalar_tensor_tensor(T8[:, 24:32], T8[:, 16:24], -1e30,
                                              T8[:, 8:16], ALU.mult, ALU.add),
             reads=[("T8", 2), ("T8", 1)], writes=[("T8", 3)])
        em.v(lambda e: e.reduce_max(SMF[:, 25:26], T8[:, 24:32], axis=AX.X),
             reads=[("T8", 3)], writes=[("SMF", 25)])
        em.v(lambda e: e.tensor_scalar(T8[:, 32:40], T8[:, 24:32], SMF[:, 25:26],
                                       None, ALU.is_ge),
             reads=[("T8", 3), ("SMF", 25)], writes=[("T8", 4)])
        em.v(lambda e: e.tensor_add(T8[:, 40:48], T8[:, 16:24], T8[:, 32:40]),
             reads=[("T8", 2), ("T8", 4)], writes=[("T8", 5)])
        em.v(lambda e: e.tensor_mul(T8[:, 48:56], T8[:, 0:8], T8[:, 40:48]),
             reads=[("T8", 0), ("T8", 5)], writes=[("T8", 6)])
        em.v(lambda e: e.reduce_sum(SMF[:, 26:27], T8[:, 48:56], axis=AX.X),
             reads=[("T8", 6)], writes=[("SMF", 26)])
        em.v(lambda e: e.reciprocal(SMF[:, 27:28], SMF[:, 26:27]),
             reads=[("SMF", 26)], writes=[("SMF", 27)])
        em.v(lambda e, st=st: e.tensor_scalar(
            CMB[:, st * 8:st * 8 + 8], T8[:, 48:56], SMF[:, 27:28], None, ALU.mult),
            reads=[("T8", 6), ("SMF", 27)], writes=[("CMB", st)])

    def moe(sub):
        if "nomoe" in os.environ.get("DEV_TAP", ""):
            return
        xdst = out_d if sub == N_SUB - 1 else xbuf
        xtag = "ob" if sub == N_SUB - 1 else "xb"
        WA, WB, TT0, WD0 = 0, 16384, 16384, 8192
        for ex in range(8):
            em.dpool(lambda e, ex=ex: e.dma_start(
                out=BIG1[:, WA:WA + 16384].rearrange("p (c i) -> p c i", c=16),
                in_=wg[ex].rearrange("(c p) i -> p c i", p=128)),
                reads=[], writes=b1(WA, WA + 16384))
            em.dpool(lambda e, ex=ex: e.dma_start(
                out=BIG1[:, WB:WB + 16384].rearrange("p (c i) -> p c i", c=16),
                in_=wu[ex].rearrange("(c p) i -> p c i", p=128)),
                reads=[], writes=b1(WB, WB + 16384))
            for sh in range(2):
                for dc in range(16):
                    em.dscal(lambda e, dc=dc, sh=sh: e.dma_start(
                        out=BIG2[:, TT0 + dc * 1024:TT0 + dc * 1024 + 1024],
                        in_=t_dr[sh * 1024:sh * 1024 + 1024, dc * 128:dc * 128 + 128],
                        transpose=True),
                        reads=[("tdr", t) for t in range(sh * 8, sh * 8 + 8)],
                        writes=b2(TT0 + dc * 1024, TT0 + (dc + 1) * 1024))
                    em.v(lambda e, dc=dc: e.tensor_scalar(
                        BIG2[:, TT0 + dc * 1024:TT0 + dc * 1024 + 1024],
                        BIG2[:, TT0 + dc * 1024:TT0 + dc * 1024 + 1024],
                        LNC[:, 16 + dc:16 + dc + 1], None, ALU.mult),
                        reads=b2(TT0 + dc * 1024, TT0 + (dc + 1) * 1024) + [("LNC", 2)],
                        writes=b2(TT0 + dc * 1024, TT0 + (dc + 1) * 1024))
                for ic in range(8):
                    for dc in range(16):
                        for n in range(2):
                            em.t(lambda e, ic=ic, dc=dc, n=n: e.matmul(
                                PS[:, n * 512:n * 512 + 512],
                                BIG1[:, WA + dc * 1024 + ic * 128:WA + dc * 1024 + ic * 128 + 128],
                                BIG2[:, TT0 + dc * 1024 + n * 512:TT0 + dc * 1024 + n * 512 + 512],
                                start=(dc == 0), stop=(dc == 15)),
                                reads=b1(WA + dc * 1024 + ic * 128) + b2(TT0 + dc * 1024, TT0 + (dc + 1) * 1024),
                                writes=pst(n * 512))
                            em.t(lambda e, ic=ic, dc=dc, n=n: e.matmul(
                                PS[:, 2048 + n * 512:2048 + n * 512 + 512],
                                BIG1[:, WB + dc * 1024 + ic * 128:WB + dc * 1024 + ic * 128 + 128],
                                BIG2[:, TT0 + dc * 1024 + n * 512:TT0 + dc * 1024 + n * 512 + 512],
                                start=(dc == 0), stop=(dc == 15)),
                                reads=b1(WB + dc * 1024 + ic * 128) + b2(TT0 + dc * 1024, TT0 + (dc + 1) * 1024),
                                writes=pst(2048 + n * 512))
                    em.s(lambda e: e.activation(STF[:, 4096:5120], PS[:, 0:1024], AF.Silu),
                         reads=pst(0, 1024), writes=stf(4096))
                    em.v(lambda e, ic=ic: e.tensor_mul(
                        BIG2[:, ic * 1024:ic * 1024 + 1024], STF[:, 4096:5120],
                        PS[:, 2048:3072]),
                        reads=stf(4096) + pst(2048, 3072), writes=b2(ic * 1024, (ic + 1) * 1024))
                for dh in range(2):
                    em.dpool(lambda e, ex=ex, dh=dh: e.dma_start(
                        out=BIG2[:, WD0:WD0 + 8192].rearrange("p (c d) -> p c d", c=8),
                        in_=wd[ex, :, dh * 1024:dh * 1024 + 1024].rearrange(
                            "(c p) d -> p c d", p=128)),
                        reads=[], writes=b2(WD0, WD0 + 8192))
                    for st_l in range(8):
                        st = sh * 8 + st_l
                        pa = 2048 * (st_l % 2)
                        for ic in range(8):
                            for n in range(2):
                                em.t(lambda e, ic=ic, st_l=st_l, n=n, pa=pa: e.matmul(
                                    PS[:, pa + n * 512:pa + n * 512 + 512],
                                    BIG2[:, ic * 1024 + st_l * 128:ic * 1024 + st_l * 128 + 128],
                                    BIG2[:, WD0 + ic * 1024 + n * 512:WD0 + ic * 1024 + n * 512 + 512],
                                    start=(ic == 0), stop=(ic == 7)),
                                    reads=b2(ic * 1024 + st_l * 128) + b2(WD0 + ic * 1024, WD0 + (ic + 1) * 1024),
                                    writes=pst(pa + n * 512))
                        osl = 4096 + (st_l % 2) * 2048
                        em.v(lambda e, st=st, pa=pa, osl=osl, ex=ex: e.tensor_scalar(
                            STF[:, osl:osl + 1024], PS[:, pa:pa + 1024],
                            CMB[:, st * 8 + ex:st * 8 + ex + 1], None, ALU.mult),
                            reads=pst(pa, pa + 1024) + [("CMB", st)], writes=stf(osl))
                        em.dpool(lambda e, st=st, osl=osl, dh=dh: e.dma_start(
                            out=xdst[st * 128:st * 128 + 128, dh * 1024:dh * 1024 + 1024],
                            in_=STF[:, osl:osl + 1024], accum_op=ALU.add),
                            reads=stf(osl), writes=[(xtag, st)])

    kinds = ["attn_a", "ffn", "attn_b", "ffn"]
    for sub in range(N_SUB):
        kind = kinds[sub]
        phase0(sub, kind)
        load_yT(kind)
        if kind == "attn_b":
            phase1_b()
        else:
            phase1_qkv(kind)
        phase2(sub, kind)
        if kind == "ffn":
            moe(sub)

    em.build(nc, block, sems)
    st_ctx.close()
    return nc


def kernel(**inputs):
    consts = host_consts()
    nc = build_program()
    x_full = np.ascontiguousarray(np.asarray(inputs["L0_attn_a_x"]))
    base = dict(
        aq=inputs["L0_attn_a_q_kernel"], ak=inputs["L0_attn_a_k_kernel"],
        av=inputs["L0_attn_a_v_kernel"], ao=inputs["L0_attn_a_out_kernel"],
        bwq=inputs["L0_attn_b_W_Q"], bwk=inputs["L0_attn_b_W_K"],
        bwv=inputs["L0_attn_b_W_V"], bwo=inputs["L0_attn_b_W_O"],
        ln1=inputs["L0_ffn_ln1_w"], ln2=inputs["L0_ffn_ln2_w"],
        fwq=inputs["L0_ffn_wq"], fwk=inputs["L0_ffn_wk"],
        fwv=inputs["L0_ffn_wv"], fwo=inputs["L0_ffn_wo"],
        qnw=inputs["L0_ffn_qnorm_w"], knw=inputs["L0_ffn_knorm_w"],
        rw=inputs["L0_ffn_router_w"], rb=inputs["L0_ffn_router_bias"],
        wg=inputs["L0_ffn_wg"], wu=inputs["L0_ffn_wu"], wd=inputs["L0_ffn_wd"],
        **consts,
    )
    base = {k: np.ascontiguousarray(np.asarray(v)) for k, v in base.items()}
    in_maps = []
    for c in range(NCORES):
        m = dict(base)
        m["x_in"] = np.ascontiguousarray(x_full[c])
        in_maps.append(m)
    res = run_bass_kernel_spmd(nc, in_maps, core_ids=list(range(NCORES)))
    return np.stack([r["out"] for r in res.results], axis=0)


# version 03
# speedup vs baseline: 1.0077x; 1.0077x over previous
import os
import numpy as np
import ml_dtypes
import concourse.bass as bass
import concourse.mybir as mybir
from concourse.bass_utils import run_bass_kernel_spmd

F32 = mybir.dt.float32
BF16 = mybir.dt.bfloat16
AF = mybir.ActivationFunctionType
ALU = mybir.AluOpType
AX = mybir.AxisListType
BF = ml_dtypes.bfloat16
f32 = np.float32

NCORES = 8
S = 2048
D = 2048
NT = 16
NH = 16
SCALE = f32(1.0 / np.sqrt(128.0))
N_SUB = int(os.environ.get("DEV_NSUB", "4"))


class Op:
    __slots__ = ("idx", "stream", "fn", "deps", "semkey", "is_dma", "milestone", "waits")

    def __init__(self, idx, stream, fn, deps, semkey, is_dma):
        self.idx = idx
        self.stream = stream
        self.fn = fn
        self.deps = deps
        self.semkey = semkey
        self.is_dma = is_dma
        self.milestone = None
        self.waits = None


class Em:
    STREAMS = ("tensor", "vector", "scalar", "pool", "sync")

    def __init__(self):
        self.ops = []
        self.last_write = {}
        self.readers = {}
        self.dsem_map = {}
        self.last_dma_sem = {}

    NDSEM = 64

    def _emit(self, stream, semkey, is_dma, fn, reads, writes):
        idx = len(self.ops)
        deps = set()
        for t in reads:
            w = self.last_write.get(t)
            if w is not None:
                deps.add(w)
        for t in writes:
            w = self.last_write.get(t)
            if w is not None:
                deps.add(w)
            for r in self.readers.get(t, ()):
                deps.add(r)
        deps.discard(idx)
        if is_dma:
            tag = writes[0] if writes else ("q", semkey)
            slot = self.dsem_map.setdefault(tag, len(self.dsem_map) % self.NDSEM)
            semkey = ("d", slot)
            prev = self.last_dma_sem.get(slot)
            if prev is not None:
                deps.add(prev)
            self.last_dma_sem[slot] = idx
        self.ops.append(Op(idx, stream, fn, deps, semkey, is_dma))
        for t in reads:
            self.readers.setdefault(t, []).append(idx)
        for t in writes:
            self.last_write[t] = idx
            self.readers[t] = []
        return idx

    def t(self, fn, reads=(), writes=()):
        return self._emit("tensor", "tensor", False, fn, reads, writes)

    def v(self, fn, reads=(), writes=()):
        return self._emit("vector", "vector", False, fn, reads, writes)

    def s(self, fn, reads=(), writes=()):
        return self._emit("scalar", "scalar", False, fn, reads, writes)

    def p(self, fn, reads=(), writes=()):
        return self._emit("pool", "pool", False, fn, reads, writes)

    def dsync(self, fn, reads=(), writes=()):
        return self._emit("sync", "q_sync", True, fn, reads, writes)

    def dscal(self, fn, reads=(), writes=()):
        return self._emit("scalar", "q_scal", True, fn, reads, writes)

    def dpool(self, fn, reads=(), writes=()):
        return self._emit("pool", "q_pool", True, fn, reads, writes)

    def finalize(self):
        needed = set()
        for o in self.ops:
            if o.is_dma:
                needed.add(o.idx)
            for d in o.deps:
                p = self.ops[d]
                if p.stream == "tensor" and o.stream == "tensor":
                    continue
                needed.add(d)
        counts = {}
        for o in self.ops:
            if o.idx in needed:
                c = counts.get(o.semkey, 0) + 1
                counts[o.semkey] = c
                o.milestone = c
        hi = {s: {} for s in self.STREAMS}
        for o in self.ops:
            waits = {}
            for d in o.deps:
                p = self.ops[d]
                if p.stream == "tensor" and o.stream == "tensor":
                    continue
                v = p.milestone * (16 if p.is_dma else 1)
                if waits.get(p.semkey, 0) < v:
                    waits[p.semkey] = v
            h = hi[o.stream]
            o.waits = []
            for k, v in waits.items():
                if h.get(k, 0) < v:
                    h[k] = v
                    o.waits.append((k, v))
        self.final_counts = {k: v * (16 if (isinstance(k, tuple) and k[0] == "d")
                                      else 1)
                             for k, v in counts.items()}

    def build(self, nc, block, sems):
        self.finalize()
        streams = {s: [o for o in self.ops if o.stream == s] for s in self.STREAMS}

        def runner(stream):
            def run(eng):
                for o in streams[stream]:
                    for k, v in o.waits:
                        eng.wait_ge(sems[k], v)
                    inst = o.fn(eng)
                    if o.milestone is not None:
                        inst.then_inc(sems[o.semkey], 16 if o.is_dma else 1)
                if stream == "sync":
                    for k, v in self.final_counts.items():
                        eng.wait_ge(sems[k], v)
            return run

        block.tensor(runner("tensor"))
        block.vector(runner("vector"))
        block.scalar(runner("scalar"))
        block.gpsimd(runner("pool"))
        block.sync(runner("sync"))


def host_consts():
    pos = np.arange(S, dtype=np.float64)
    fr_a = np.arange(0, 128, 2, dtype=np.float64) / 128.0
    inv_a = 1.0 / (1e4 ** fr_a)
    ang_a = pos[:, None] * inv_a[None, :]
    cos_a = np.cos(ang_a).astype(np.float32)
    sin_a = np.sin(ang_a).astype(np.float32)
    fr_f = np.arange(0, 64, 2, dtype=np.float64) / 64.0
    inv_f = 1.0 / (1e6 ** fr_f)
    ang_f = pos[:, None] * inv_f[None, :]
    cos_f = np.cos(ang_f).astype(np.float32)
    sin_f = np.sin(ang_f).astype(np.float32)
    ident = np.eye(128, dtype=np.float32)
    masks = np.zeros((4, 128, 512), dtype=np.float32)
    sk = np.arange(128)[:, None]
    sq = np.arange(128)[None, :]
    tri = (sk <= sq).astype(np.float32)
    for r in range(4):
        for j in range(4):
            if r < j:
                masks[r, :, j * 128:(j + 1) * 128] = 1.0
            elif r == j:
                masks[r, :, j * 128:(j + 1) * 128] = tri
    return dict(cos_a=cos_a, sin_a=sin_a, cos_f=cos_f, sin_f=sin_f,
                ident=ident, trimask=masks.astype(BF))


def build_program():
    nc = bass.Bass("TRN2", target_bir_lowering=False, debug=False)
    em = Em()

    def din(name, shape, dt=F32):
        return nc.dram_tensor(name, list(shape), dt, kind="ExternalInput").ap()

    x_in = din("x_in", [S, D])
    aq = din("aq", [D, D])
    ak = din("ak", [D, 512])
    av = din("av", [D, 512])
    ao = din("ao", [D, D])
    bwq = din("bwq", [NH, 128, D])
    bwk = din("bwk", [NH, 128, D])
    bwv = din("bwv", [NH, 128, D])
    bwo = din("bwo", [NH, D, 128])
    ln1 = din("ln1", [D])
    ln2 = din("ln2", [D])
    fwq = din("fwq", [D, D])
    fwk = din("fwk", [D, 512])
    fwv = din("fwv", [D, 512])
    fwo = din("fwo", [D, D])
    qnw = din("qnw", [D])
    knw = din("knw", [512])
    rw = din("rw", [D, 8])
    rb = din("rb", [8])
    wg = din("wg", [8, D, 1024])
    wu = din("wu", [8, D, 1024])
    wd = din("wd", [8, 1024, D])
    cos_a = din("cos_a", [S, 64])
    sin_a = din("sin_a", [S, 64])
    cos_f = din("cos_f", [S, 32])
    sin_f = din("sin_f", [S, 32])
    ident_d = din("ident", [128, 128])
    trimask_d = din("trimask", [4, 128, 512], BF16)

    out_d = nc.dram_tensor("out", [S, D], F32, kind="ExternalOutput").ap()

    spill_kind = "ExternalOutput" if "spill" in os.environ.get("DEV_TAP", "") else "Internal"
    xbuf = nc.dram_tensor("xbuf", [S, D], F32, kind="Internal").ap()
    y_dr = nc.dram_tensor("y_dr", [S, D], BF16, kind=spill_kind).ap()
    q_dr = nc.dram_tensor("q_dr", [S, D], BF16, kind=spill_kind).ap()
    k_dr = nc.dram_tensor("k_dr", [S, 512], BF16, kind=spill_kind).ap()
    t_dr = nc.dram_tensor("t_dr", [S, D], BF16, kind=spill_kind).ap()
    v_dr = nc.dram_tensor("v_dr", [S, D], BF16, kind=spill_kind).ap()
    qT_dr = nc.dram_tensor("qT_dr", [NH, 128, S], BF16, kind="Internal").ap()
    kT_dr = nc.dram_tensor("kT_dr", [NH, 128, S], BF16, kind="Internal").ap()
    wqT_dr = nc.dram_tensor("wqT_dr", [NH, 128, D], BF16, kind="Internal").ap()
    wkT_dr = nc.dram_tensor("wkT_dr", [NH, 128, D], BF16, kind="Internal").ap()
    wvT_dr = nc.dram_tensor("wvT_dr", [16, 128, D], BF16, kind="Internal").ap()
    woT_dr = nc.dram_tensor("woT_dr", [NH, 128, D], BF16, kind="Internal").ap()
    zall_dr = nc.dram_tensor("zall_dr", [4, 128, 8192], BF16, kind=spill_kind).ap()
    aob_dr = nc.dram_tensor("aob_dr", [D, D], BF16, kind="Internal").ap()
    fob_dr = nc.dram_tensor("fob_dr", [D, D], BF16, kind="Internal").ap()

    from contextlib import ExitStack
    st_ctx = ExitStack()

    def sb(name, shape, dt, **kw):
        return st_ctx.enter_context(nc.sbuf_tensor(name, shape, dt, **kw))

    BIG1 = sb("BIG1", [128, 32768], BF16, is_transpose_dest=True)
    BIG2 = sb("BIG2", [128, 32768], BF16, is_transpose_dest=True)
    STF = sb("STF", [128, 8192], F32)
    STB = sb("STB", [128, 6144], BF16)
    CON = sb("CON", [128, 3072], F32)
    CONB = sb("CONB", [128, 2560], BF16)
    MSK = sb("MSK", [128, 2048], BF16)
    IDE = sb("IDE", [128, 128], F32)
    ONEC = sb("ONEC", [128, 1], BF16)
    ONERB = sb("ONERB", [1, 128], BF16)
    ONERF = sb("ONERF", [1, 128], F32)
    SMF = sb("SMF", [128, 224], F32)
    STASH = sb("STASH", [128, 64], F32)
    MCOL = sb("MCOL", [128, 8], F32)
    MROW = sb("MROW", [1, 1024], BF16)
    RCROW = sb("RCROW", [1, 512], F32)
    BCS = sb("BCS", [128, 1024], F32)
    CMB = sb("CMB", [128, 128], F32)
    T8 = sb("T8", [128, 128], F32)
    WR8 = sb("WR8", [128, 128], F32)
    LNC = sb("LNC", [128, 32], F32)
    PS = st_ctx.enter_context(nc.psum_tensor("PS", [128, 4096], F32))

    sems = {}
    for k in ("tensor", "vector", "scalar", "pool"):
        sems[k] = st_ctx.enter_context(nc.semaphore("sem_" + k))
    for i in range(Em.NDSEM):
        sems[("d", i)] = st_ctx.enter_context(nc.semaphore("dsem_%d" % i))
    block = st_ctx.enter_context(nc.Block())

    def b1(c0, c1=None):
        c1 = c0 + 512 if c1 is None else c1
        return [("B1", i) for i in range(c0 // 512, (c1 + 511) // 512)]

    def b2(c0, c1=None):
        c1 = c0 + 512 if c1 is None else c1
        return [("B2", i) for i in range(c0 // 512, (c1 + 511) // 512)]

    def stf(c0, c1=None):
        c1 = c0 + 2048 if c1 is None else c1
        return [("STF", i) for i in range(c0 // 2048, (c1 + 2047) // 2048)]

    def stb(c0, c1=None):
        c1 = c0 + 2048 if c1 is None else c1
        return [("STB", i) for i in range(c0 // 2048, (c1 + 2047) // 2048)]

    def pst(c0, c1=None):
        c1 = c0 + 512 if c1 is None else c1
        return [("PS", i) for i in range(c0 // 512, (c1 + 511) // 512)]

    em.dsync(lambda e: e.dma_start(out=IDE[:, :], in_=ident_d[:, :]), writes=[("IDE",)])
    em.v(lambda e: e.memset(ONEC[:, :], 1.0), writes=[("ONEC",)])
    em.v(lambda e: e.memset(ONERB[:, :], 1.0), writes=[("ONERB",)])
    em.v(lambda e: e.memset(ONERF[:, :], 1.0), writes=[("ONERF",)])
    em.dsync(lambda e: e.dma_start(out=MSK[:, :].rearrange("p (r c) -> p r c", r=4),
                                   in_=trimask_d[:, :, :].rearrange("r p c -> p r c")),
             writes=[("MSK",)])
    for dram, w, off in ((cos_a, 64, 0), (sin_a, 64, 1024),
                         (cos_f, 32, 2048), (sin_f, 32, 2560)):
        em.dsync(lambda e, dram=dram, w=w, off=off: e.dma_start(
            out=CON[:, off:off + NT * w].rearrange("p (t w) -> p t w", t=NT),
            in_=dram.rearrange("(t p) w -> p t w", p=128)),
            writes=[("CON", "tab")])

    def bcast_gain(dram_vec, width, dst_off, scale, tagname):
        em.dsync(lambda e: e.dma_start(out=STF[0:1, 2048:2048 + width], in_=dram_vec[None, :]),
                 writes=stf(2048))
        for i in range((width + 511) // 512):
            w0 = i * 512
            w1 = min(width, w0 + 512)
            em.t(lambda e, w0=w0, w1=w1: e.matmul(
                PS[:, 0:w1 - w0], ONERF[0:1, :], STF[0:1, 2048 + w0:2048 + w1],
                start=True, stop=True),
                reads=[("ONERF",)] + stf(2048), writes=pst(0))
            em.s(lambda e, w0=w0, w1=w1: e.activation(
                CONB[:, dst_off + w0:dst_off + w1], PS[:, 0:w1 - w0], AF.Copy, scale=scale),
                reads=pst(0), writes=[("CON", tagname)])

    bcast_gain(qnw, D, 0, SCALE, "gq")
    bcast_gain(knw, 512, 2048, f32(1.0), "gk")
    for vec, cbase, tag in ((ln1, 0, ("LNC", 1)), (ln2, 16, ("LNC", 2))):
        em.dsync(lambda e, vec=vec: e.dma_start(out=STF[0:1, 2048:4096], in_=vec[None, :]),
                 writes=stf(2048))
        for c in range(16):
            em.t(lambda e, c=c: e.transpose(
                PS[:, 1024 + (c % 2) * 512:1024 + (c % 2) * 512 + 1],
                STF[0:1, 2048 + c * 128:2048 + c * 128 + 128], IDE[0:1, 0:1]),
                reads=stf(2048) + [("IDE",)], writes=pst(1024 + (c % 2) * 512))
            em.s(lambda e, c=c, cbase=cbase: e.activation(
                LNC[:, cbase + c:cbase + c + 1],
                PS[:, 1024 + (c % 2) * 512:1024 + (c % 2) * 512 + 1],
                AF.Copy, scale=f32(1.0)),
                reads=pst(1024 + (c % 2) * 512), writes=[tag])
    em.dsync(lambda e: e.dma_start(out=WR8[:, :].rearrange("p (c w) -> p c w", c=16),
                                   in_=rw.rearrange("(c p) w -> p c w", p=128)),
             writes=[("WR8",)])
    em.dsync(lambda e: e.dma_start(out=STF[0:1, 2048:2056], in_=rb[None, :]), writes=stf(2048))
    em.t(lambda e: e.matmul(PS[:, 0:8], ONERF[0:1, :], STF[0:1, 2048:2056], start=True, stop=True),
         reads=[("ONERF",)] + stf(2048), writes=pst(0))
    em.s(lambda e: e.activation(T8[:, 112:120], PS[:, 0:8], AF.Copy, scale=f32(1.0)),
         reads=pst(0), writes=[("T8", "bias")])

    def prep_wo_bf(src, dst, tag):
        for dc in range(16):
            em.dsync(lambda e, dc=dc: e.dma_start(
                out=STF[:, 2048:4096], in_=src[dc * 128:dc * 128 + 128, :]),
                writes=stf(2048))
            em.v(lambda e, dc=dc: e.tensor_copy(STB[:, 0:2048], STF[:, 2048:4096]),
                 reads=stf(2048), writes=stb(0))
            em.dscal(lambda e, dc=dc: e.dma_start(
                out=dst[dc * 128:dc * 128 + 128, :], in_=STB[:, 0:2048]),
                reads=stb(0), writes=[(tag, dc)])

    prep_wo_bf(ao, aob_dr, "aob")
    if N_SUB >= 2:
        prep_wo_bf(fwo, fob_dr, "fob")

    def prep_bw(src3, dst, scale, vmode=False, tagbase="w"):
        for h in range(NH):
            ws = 2048 + (h % 2) * 2048
            em.dsync(lambda e, h=h, ws=ws: e.dma_start(
                out=STF[:, ws:ws + 2048], in_=src3[h]),
                writes=stf(ws))
            for dc in range(16):
                pb = 2048 + (dc % 4) * 512
                em.t(lambda e, h=h, dc=dc, pb=pb, ws=ws: e.transpose(
                    PS[:, pb:pb + 128], STF[:, ws + dc * 128:ws + dc * 128 + 128],
                    IDE[:, :]),
                    reads=stf(ws) + [("IDE",)], writes=pst(pb))
                if scale != 1.0:
                    em.v(lambda e, dc=dc, pb=pb: e.tensor_scalar(
                        STB[:, dc * 128:dc * 128 + 128], PS[:, pb:pb + 128],
                        float(scale), None, ALU.mult),
                        reads=pst(pb), writes=stb(0))
                else:
                    em.v(lambda e, dc=dc, pb=pb: e.tensor_copy(
                        STB[:, dc * 128:dc * 128 + 128], PS[:, pb:pb + 128]),
                        reads=pst(pb), writes=stb(0))
            if vmode:
                em.dscal(lambda e, h=h: e.dma_start(
                    out=dst[:, :, h * 128:(h + 1) * 128].rearrange("c p e -> p c e"),
                    in_=STB[:, 0:2048].rearrange("p (c e) -> p c e", c=16)),
                    reads=stb(0), writes=[("wv_scr",)])
            else:
                em.dscal(lambda e, h=h: e.dma_start(out=dst[h], in_=STB[:, 0:2048]),
                         reads=stb(0), writes=[(tagbase, h)])

    if N_SUB >= 3:
        prep_bw(bwq, wqT_dr, SCALE, tagbase="wqs")
        prep_bw(bwk, wkT_dr, 1.0, tagbase="wks")
        prep_bw(bwv, wvT_dr, 1.0, vmode=True)
        for h in range(NH):
            em.dsync(lambda e, h=h: e.dma_start(
                out=STF[:, 2048:4096].rearrange("p (c e) -> p c e", c=16),
                in_=bwo[h].rearrange("(c p) e -> p c e", p=128)),
                writes=stf(2048))
            for dc in range(16):
                pb = 2048 + (dc % 4) * 512
                em.t(lambda e, dc=dc, pb=pb: e.transpose(
                    PS[:, pb:pb + 128], STF[:, 2048 + dc * 128:2048 + dc * 128 + 128],
                    IDE[:, :]),
                    reads=stf(2048) + [("IDE",)], writes=pst(pb))
                em.v(lambda e, dc=dc, pb=pb: e.tensor_copy(
                    STB[:, dc * 128:dc * 128 + 128], PS[:, pb:pb + 128]),
                    reads=pst(pb), writes=stb(0))
            em.dscal(lambda e, h=h: e.dma_start(out=woT_dr[h], in_=STB[:, 0:2048]),
                     reads=stb(0), writes=[("wos", h)])

    def phase0(sub, kind):
        xsrc = x_in if sub == 0 else xbuf
        for st in range(NT):
            B = st * 12
            xs = (st % 2) * 2048
            r0 = st * 128
            em.dsync(lambda e, xs=xs, r0=r0: e.dma_start(
                out=STF[:, xs:xs + 2048], in_=xsrc[r0:r0 + 128, :]),
                reads=[("xb", st)] if sub > 0 else [], writes=stf(xs))
            if sub == 0:
                em.dscal(lambda e, xs=xs, r0=r0: e.dma_start(
                    out=xbuf[r0:r0 + 128, :], in_=STF[:, xs:xs + 2048]),
                    reads=stf(xs), writes=[("xb", st)])
            if sub == N_SUB - 1:
                em.dscal(lambda e, xs=xs, r0=r0: e.dma_start(
                    out=out_d[r0:r0 + 128, :], in_=STF[:, xs:xs + 2048]),
                    reads=stf(xs), writes=[("ob", st)])
            em.v(lambda e, xs=xs, B=B: e.reduce_sum(SMF[:, B:B + 1], STF[:, xs:xs + 2048], axis=AX.X),
                 reads=stf(xs), writes=[("SMF", B)])
            ys = (st % 2) * 2048
            em.s(lambda e, xs=xs, B=B, ys=ys: e.activation(
                STB[:, ys:ys + 2048], STF[:, xs:xs + 2048], AF.Square,
                accum_out=SMF[:, B + 1:B + 2]),
                reads=stf(xs), writes=stb(ys) + [("SMF", B + 1)])
            em.v(lambda e, B=B: e.tensor_scalar_mul(SMF[:, B + 2:B + 3], SMF[:, B:B + 1], 1.0 / D),
                 reads=[("SMF", B)], writes=[("SMF", B + 2)])
            em.v(lambda e, st=st, B=B: e.tensor_copy(STASH[:, st:st + 1], SMF[:, B + 2:B + 3]),
                 reads=[("SMF", B + 2)], writes=[("STH", "m", st)])
            em.v(lambda e, B=B: e.tensor_mul(SMF[:, B + 3:B + 4], SMF[:, B + 2:B + 3], SMF[:, B + 2:B + 3]),
                 reads=[("SMF", B + 2)], writes=[("SMF", B + 3)])
            em.v(lambda e, B=B: e.scalar_tensor_tensor(
                SMF[:, B + 4:B + 5], SMF[:, B + 1:B + 2], 1.0 / D, SMF[:, B + 3:B + 4],
                ALU.mult, ALU.subtract),
                reads=[("SMF", B + 1), ("SMF", B + 3)], writes=[("SMF", B + 4)])
            em.s(lambda e, B=B: e.activation(SMF[:, B + 5:B + 6], SMF[:, B + 4:B + 5], AF.Ln, bias=f32(1e-5)),
                 reads=[("SMF", B + 4)], writes=[("SMF", B + 5)])
            em.s(lambda e, st=st, B=B: e.activation(
                STASH[:, 32 + st:32 + st + 1], SMF[:, B + 5:B + 6], AF.Exp, scale=f32(-0.5)),
                reads=[("SMF", B + 5)], writes=[("STH", "r", st)])
            if kind == "ffn":
                em.v(lambda e, st=st, B=B: e.tensor_mul(
                    SMF[:, B + 8:B + 9], STASH[:, 32 + st:32 + st + 1], STASH[:, 32 + st:32 + st + 1]),
                    reads=[("STH", "r", st)], writes=[("SMF", B + 8)])
                em.v(lambda e, B=B: e.tensor_mul(SMF[:, B + 9:B + 10], SMF[:, B + 4:B + 5], SMF[:, B + 8:B + 9]),
                     reads=[("SMF", B + 4), ("SMF", B + 8)], writes=[("SMF", B + 9)])
                em.s(lambda e, B=B: e.activation(SMF[:, B + 10:B + 11], SMF[:, B + 9:B + 10], AF.Ln,
                                                 bias=f32(1e-6)),
                     reads=[("SMF", B + 9)], writes=[("SMF", B + 10)])
                em.s(lambda e, B=B: e.activation(SMF[:, B + 7:B + 8], SMF[:, B + 10:B + 11], AF.Exp,
                                                 scale=f32(-0.5)),
                     reads=[("SMF", B + 10)], writes=[("SMF", B + 7)])
                em.v(lambda e, st=st, B=B: e.tensor_mul(
                    SMF[:, B + 6:B + 7], STASH[:, 32 + st:32 + st + 1], SMF[:, B + 7:B + 8]),
                    reads=[("STH", "r", st), ("SMF", B + 7)], writes=[("SMF", B + 6)])
                em.v(lambda e, st=st, B=B: e.tensor_scalar(
                    STASH[:, 48 + st:48 + st + 1], SMF[:, B + 2:B + 3],
                    STASH[:, 32 + st:32 + st + 1], -1.0, ALU.mult, ALU.mult),
                    reads=[("SMF", B + 2), ("STH", "r", st)], writes=[("STH", "bln", st)])
            else:
                em.v(lambda e, st=st, B=B: e.tensor_copy(SMF[:, B + 6:B + 7], STASH[:, 32 + st:32 + st + 1]),
                     reads=[("STH", "r", st)], writes=[("SMF", B + 6)])
            em.v(lambda e, st=st, B=B: e.tensor_scalar(
                STASH[:, 16 + st:16 + st + 1], SMF[:, B + 2:B + 3], SMF[:, B + 6:B + 7], -1.0,
                ALU.mult, ALU.mult),
                reads=[("SMF", B + 2), ("SMF", B + 6)], writes=[("STH", "b", st)])
            em.v(lambda e, xs=xs, ys=ys, st=st, B=B: e.tensor_scalar(
                STB[:, ys:ys + 2048], STF[:, xs:xs + 2048], SMF[:, B + 6:B + 7],
                STASH[:, 16 + st:16 + st + 1], ALU.mult, ALU.add),
                reads=stf(xs) + [("SMF", B + 6), ("STH", "b", st)], writes=stb(ys))
            em.dsync(lambda e, ys=ys, r0=r0: e.dma_start(
                out=y_dr[r0:r0 + 128, :], in_=STB[:, ys:ys + 2048]),
                reads=stb(ys), writes=[("ydr", r0 // 128)])

    def load_yT(kind):
        ally = [("ydr", t) for t in range(NT)]
        for dc in range(16):
            em.dscal(lambda e, dc=dc: e.dma_start(
                out=BIG1[:, dc * 2048:(dc + 1) * 2048],
                in_=y_dr[:, dc * 128:(dc + 1) * 128], transpose=True),
                reads=ally, writes=b1(dc * 2048, (dc + 1) * 2048))
            if kind == "ffn":
                em.v(lambda e, dc=dc: e.tensor_scalar(
                    BIG1[:, dc * 2048:(dc + 1) * 2048],
                    BIG1[:, dc * 2048:(dc + 1) * 2048],
                    LNC[:, dc:dc + 1], None, ALU.mult),
                    reads=b1(dc * 2048, (dc + 1) * 2048) + [("LNC", 1)],
                    writes=b1(dc * 2048, (dc + 1) * 2048))

    def rope_natural(qf_off, out_off, st, nheads, mode, reads, writes):
        if mode == "full":
            half, coff, soff, tw, rot = 64, 0, 1024, 64, 128
        else:
            half, coff, soff, tw, rot = 32, 2048, 2560, 32, 64
        qv = STF[:, qf_off:qf_off + nheads * 128].rearrange("p (h e) -> p h e", h=nheads)
        ov = STB[:, out_off:out_off + nheads * 128].rearrange("p (h e) -> p h e", h=nheads)
        cosb = CON[:, coff + st * tw:coff + st * tw + half]
        sinb = CON[:, soff + st * tw:soff + st * tw + half]
        cosB = bass.AP(cosb.tensor, cosb.offset, [cosb.ap[0], [0, nheads], cosb.ap[1]])
        sinB = bass.AP(sinb.tensor, sinb.offset, [sinb.ap[0], [0, nheads], sinb.ap[1]])
        t1 = STF[:, 6144:6144 + nheads * half].rearrange("p (h e) -> p h e", h=nheads)
        t2 = STF[:, 7168:7168 + nheads * half].rearrange("p (h e) -> p h e", h=nheads)
        lo = (slice(None), slice(None), slice(0, half))
        hi = (slice(None), slice(None), slice(half, 2 * half))
        rd = list(reads) + [("CON", "tab")]
        wrt = list(writes)
        t3 = stf(6144, 8192)
        em.v(lambda e: e.tensor_mul(t1, qv[lo], cosB), reads=rd, writes=t3)
        em.v(lambda e: e.tensor_mul(t2, qv[hi], sinB), reads=rd, writes=t3)
        em.v(lambda e: e.tensor_sub(ov[lo], t1, t2), reads=t3, writes=wrt)
        em.v(lambda e: e.tensor_mul(t1, qv[hi], cosB), reads=rd, writes=t3)
        em.v(lambda e: e.tensor_mul(t2, qv[lo], sinB), reads=rd, writes=t3)
        em.v(lambda e: e.tensor_add(ov[hi], t1, t2), reads=t3, writes=wrt)
        if mode == "part":
            pas = (slice(None), slice(None), slice(rot, 128))
            em.v(lambda e: e.tensor_copy(ov[pas], qv[pas]), reads=rd, writes=wrt)

    def phase1_qkv(kind):
        wq_d, wk_d, wv_d = (aq, ak, av) if kind == "attn_a" else (fwq, fwk, fwv)
        gq_rd = [("CON", "gq")] if kind == "ffn" else []
        gk_rd = [("CON", "gk")] if kind == "ffn" else []
        mode = "full" if kind == "attn_a" else "part"
        for dc in range(16):
            em.dpool(lambda e, dc=dc: e.dma_start(
                out=BIG2[:, dc * 2048:(dc + 1) * 2048],
                in_=wq_d[dc * 128:(dc + 1) * 128, :]),
                reads=[], writes=b2(dc * 2048, (dc + 1) * 2048))
        for st in range(NT):
            pa = (st % 2) * 2048
            for dc in range(16):
                for n in range(4):
                    em.t(lambda e, st=st, dc=dc, n=n, pa=pa: e.matmul(
                        PS[:, pa + n * 512:pa + n * 512 + 512],
                        BIG1[:, dc * 2048 + st * 128:dc * 2048 + st * 128 + 128],
                        BIG2[:, dc * 2048 + n * 512:dc * 2048 + n * 512 + 512],
                        start=(dc == 0), stop=(dc == 15)),
                        reads=b1(dc * 2048 + st * 128) + b2(dc * 2048, (dc + 1) * 2048),
                        writes=pst(pa + n * 512))
            pb_rd = pst(pa, pa + 2048)
            B1 = 192 + (st % 4) * 4
            sqs = 4096 + (st % 2) * 1024
            em.s(lambda e, pa=pa, B1=B1, sqs=sqs: e.activation(
                STF[:, sqs:sqs + 1024], PS[:, pa:pa + 1024], AF.Square,
                accum_out=SMF[:, B1:B1 + 1]),
                reads=pb_rd, writes=stf(4096) + [("SMF", B1)])
            em.s(lambda e, pa=pa, B1=B1, sqs=sqs: e.activation(
                STF[:, sqs:sqs + 1024], PS[:, pa + 1024:pa + 2048], AF.Square,
                accum_out=SMF[:, B1 + 3:B1 + 4]),
                reads=pb_rd, writes=stf(4096) + [("SMF", B1 + 3)])
            em.v(lambda e, B1=B1: e.tensor_add(SMF[:, B1:B1 + 1], SMF[:, B1:B1 + 1],
                                               SMF[:, B1 + 3:B1 + 4]),
                 reads=[("SMF", B1), ("SMF", B1 + 3)], writes=[("SMF", B1)])
            em.v(lambda e, B1=B1: e.tensor_scalar_mul(SMF[:, B1 + 1:B1 + 2], SMF[:, B1:B1 + 1], 1.0 / D),
                 reads=[("SMF", B1)], writes=[("SMF", B1 + 1)])
            em.s(lambda e, B1=B1: e.activation(SMF[:, B1 + 2:B1 + 3], SMF[:, B1 + 1:B1 + 2], AF.Ln, bias=f32(1e-6)),
                 reads=[("SMF", B1 + 1)], writes=[("SMF", B1 + 2)])
            em.s(lambda e, B1=B1: e.activation(SMF[:, B1 + 1:B1 + 2], SMF[:, B1 + 2:B1 + 3], AF.Exp, scale=f32(-0.5)),
                 reads=[("SMF", B1 + 2)], writes=[("SMF", B1 + 1)])
            if kind == "attn_a":
                em.v(lambda e, pa=pa, B1=B1: e.tensor_scalar(
                    STF[:, 0:2048], PS[:, pa:pa + 2048], SMF[:, B1 + 1:B1 + 2], float(SCALE),
                    ALU.mult, ALU.mult),
                    reads=pb_rd + [("SMF", B1 + 1)], writes=stf(0))
            else:
                em.v(lambda e, pa=pa, B1=B1: e.scalar_tensor_tensor(
                    STF[:, 0:2048], PS[:, pa:pa + 2048], SMF[:, B1 + 1:B1 + 2],
                    CONB[:, 0:2048], ALU.mult, ALU.mult),
                    reads=pb_rd + [("SMF", B1 + 1)] + gq_rd, writes=stf(0))
            rope_natural(0, 0, st, 16, mode, reads=stf(0), writes=stb(0))
            em.dsync(lambda e, st=st: e.dma_start(
                out=q_dr[st * 128:st * 128 + 128, :], in_=STB[:, 0:2048]),
                reads=stb(0), writes=[("qdr", st)])
        for dc in range(16):
            em.dpool(lambda e, dc=dc: e.dma_start(
                out=BIG2[:, dc * 512:(dc + 1) * 512],
                in_=wk_d[dc * 128:(dc + 1) * 128, :]),
                reads=[], writes=b2(dc * 512, (dc + 1) * 512))
        for st in range(NT):
            pa = (st % 2) * 2048
            for dc in range(16):
                em.t(lambda e, st=st, dc=dc, pa=pa: e.matmul(
                    PS[:, pa:pa + 512],
                    BIG1[:, dc * 2048 + st * 128:dc * 2048 + st * 128 + 128],
                    BIG2[:, dc * 512:dc * 512 + 512],
                    start=(dc == 0), stop=(dc == 15)),
                    reads=b1(dc * 2048 + st * 128) + b2(dc * 512),
                    writes=pst(pa))
            B1 = 192 + (st % 4) * 4
            sqs = 4096 + (st % 2) * 1024
            em.s(lambda e, pa=pa, B1=B1, sqs=sqs: e.activation(
                STF[:, sqs:sqs + 512], PS[:, pa:pa + 512], AF.Square,
                accum_out=SMF[:, B1:B1 + 1]),
                reads=pst(pa), writes=stf(4096) + [("SMF", B1)])
            em.v(lambda e, B1=B1: e.tensor_scalar_mul(SMF[:, B1 + 1:B1 + 2], SMF[:, B1:B1 + 1], 1.0 / 512),
                 reads=[("SMF", B1)], writes=[("SMF", B1 + 1)])
            em.s(lambda e, B1=B1: e.activation(SMF[:, B1 + 2:B1 + 3], SMF[:, B1 + 1:B1 + 2], AF.Ln, bias=f32(1e-6)),
                 reads=[("SMF", B1 + 1)], writes=[("SMF", B1 + 2)])
            em.s(lambda e, B1=B1: e.activation(SMF[:, B1 + 1:B1 + 2], SMF[:, B1 + 2:B1 + 3], AF.Exp, scale=f32(-0.5)),
                 reads=[("SMF", B1 + 2)], writes=[("SMF", B1 + 1)])
            if kind == "attn_a":
                em.v(lambda e, pa=pa, B1=B1: e.tensor_scalar(
                    STF[:, 0:512], PS[:, pa:pa + 512], SMF[:, B1 + 1:B1 + 2], None, ALU.mult),
                    reads=pst(pa) + [("SMF", B1 + 1)], writes=stf(0))
            else:
                em.v(lambda e, pa=pa, B1=B1: e.scalar_tensor_tensor(
                    STF[:, 0:512], PS[:, pa:pa + 512], SMF[:, B1 + 1:B1 + 2],
                    CONB[:, 2048:2560], ALU.mult, ALU.mult),
                    reads=pst(pa) + [("SMF", B1 + 1)] + gk_rd, writes=stf(0))
            rope_natural(0, 0, st, 4, mode, reads=stf(0), writes=stb(0))
            em.dsync(lambda e, st=st: e.dma_start(
                out=k_dr[st * 128:st * 128 + 128, :], in_=STB[:, 0:512]),
                reads=stb(0), writes=[("kdr", st)])
        for dc in range(16):
            em.dpool(lambda e, dc=dc: e.dma_start(
                out=BIG2[:, dc * 512:(dc + 1) * 512],
                in_=wv_d[dc * 128:(dc + 1) * 128, :]),
                reads=[], writes=b2(dc * 512, (dc + 1) * 512))
        for st in range(NT):
            pa = (st % 2) * 2048
            for dc in range(16):
                em.t(lambda e, st=st, dc=dc, pa=pa: e.matmul(
                    PS[:, pa:pa + 512],
                    BIG1[:, dc * 2048 + st * 128:dc * 2048 + st * 128 + 128],
                    BIG2[:, dc * 512:dc * 512 + 512],
                    start=(dc == 0), stop=(dc == 15)),
                    reads=b1(dc * 2048 + st * 128) + b2(dc * 512),
                    writes=pst(pa))
            em.v(lambda e, st=st, pa=pa: e.tensor_copy(
                BIG2[:, 8192 + st * 512:8192 + st * 512 + 512], PS[:, pa:pa + 512]),
                reads=pst(pa), writes=b2(8192 + st * 512))
            if "spill" in os.environ.get("DEV_TAP", ""):
                em.dsync(lambda e, st=st: e.dma_start(
                    out=v_dr[st * 128:st * 128 + 128, 0:512],
                    in_=BIG2[:, 8192 + st * 512:8192 + st * 512 + 512]),
                    reads=b2(8192 + st * 512), writes=[("vdr", st, 0)])

    def phase1_b():
        for which, wsrc, dst, dtag in (("wqs", wqT_dr, qT_dr, "qTd"),
                                       ("wks", wkT_dr, kT_dr, "kTd")):
            for h in range(NH):
                ws = (h % 2) * 2048
                em.dpool(lambda e, h=h, ws=ws, wsrc=wsrc: e.dma_start(
                    out=STB[:, ws:ws + 2048], in_=wsrc[h]),
                    reads=[(which, h)], writes=stb(ws))
                pa = (h % 2) * 2048
                for dc in range(16):
                    for n in range(4):
                        em.t(lambda e, dc=dc, n=n, pa=pa, ws=ws: e.matmul(
                            PS[:, pa + n * 512:pa + n * 512 + 512],
                            STB[:, ws + dc * 128:ws + dc * 128 + 128],
                            BIG1[:, dc * 2048 + n * 512:dc * 2048 + n * 512 + 512],
                            start=(dc == 0), stop=(dc == 15)),
                            reads=stb(ws) + b1(dc * 2048, (dc + 1) * 2048),
                            writes=pst(pa + n * 512))
                em.v(lambda e, pa=pa: e.tensor_copy(STB[:, 4096:6144], PS[:, pa:pa + 2048]),
                     reads=pst(pa, pa + 2048), writes=stb(4096))
                em.dsync(lambda e, h=h, dst=dst: e.dma_start(out=dst[h], in_=STB[:, 4096:6144]),
                         reads=stb(4096), writes=[(dtag, h)])
        for eh in range(2):
            em.dpool(lambda e, eh=eh: e.dma_start(
                out=BIG2[:, 16384:32768].rearrange("p (c e) -> p c e", c=16),
                in_=wvT_dr[:, :, eh * 1024:eh * 1024 + 1024].rearrange("c p e -> p c e")),
                reads=[("wv_scr",)], writes=b2(16384, 32768))
            for st in range(NT):
                pa = (st % 2) * 2048
                for dc in range(16):
                    for n in range(2):
                        em.t(lambda e, st=st, dc=dc, n=n, pa=pa: e.matmul(
                            PS[:, pa + n * 512:pa + n * 512 + 512],
                            BIG1[:, dc * 2048 + st * 128:dc * 2048 + st * 128 + 128],
                            BIG2[:, 16384 + dc * 1024 + n * 512:16384 + dc * 1024 + n * 512 + 512],
                            start=(dc == 0), stop=(dc == 15)),
                            reads=b1(dc * 2048 + st * 128) + b2(16384 + dc * 1024, 16384 + (dc + 1) * 1024),
                            writes=pst(pa + n * 512))
                em.v(lambda e, pa=pa, eh=eh: e.tensor_copy(
                    STB[:, 4096 + eh * 1024:4096 + eh * 1024 + 1024], PS[:, pa:pa + 1024]),
                    reads=pst(pa, pa + 1024), writes=stb(4096))
                em.dsync(lambda e, st=st, eh=eh: e.dma_start(
                    out=v_dr[st * 128:st * 128 + 128, eh * 1024:eh * 1024 + 1024],
                    in_=STB[:, 4096 + eh * 1024:4096 + eh * 1024 + 1024]),
                    reads=stb(4096), writes=[("vdr", st, eh)])

    def phase2(sub, kind):
        causal = kind != "attn_a" and "nomask" not in os.environ.get("DEV_TAP", "")
        is_b = kind == "attn_b"
        xdst = out_d if sub == N_SUB - 1 else xbuf
        xtag = "ob" if sub == N_SUB - 1 else "xb"
        wo_bf = aob_dr if kind == "attn_a" else (fob_dr if kind == "ffn" else None)
        if not is_b:
            for kv in range(4):
                em.dscal(lambda e, kv=kv: e.dma_start(
                    out=BIG2[:, kv * 2048:kv * 2048 + 2048],
                    in_=k_dr[:, kv * 128:kv * 128 + 128], transpose=True),
                    reads=[("kdr", t) for t in range(NT)],
                    writes=b2(kv * 2048, (kv + 1) * 2048))
        QG0, PB0, WO0, ZA0, KT0, VH0 = 0, 1536, 4096, 8192, 16384, 20480
        for g in range(4):
            qt0 = g * 4
            nch = (qt0 + 4) if causal else NT
            def emit_q_load(h):
                qgs = QG0 + (h % 3) * 512
                if is_b:
                    em.dscal(lambda e, h=h, qgs=qgs, g=g: e.dma_start(
                        out=BIG1[:, qgs:qgs + 512], in_=qT_dr[h][:, g * 512:g * 512 + 512]),
                        reads=[("qTd", h)], writes=b1(qgs))
                else:
                    em.dscal(lambda e, h=h, qgs=qgs, g=g: e.dma_start(
                        out=BIG1[:, qgs:qgs + 512],
                        in_=q_dr[g * 512:g * 512 + 512, h * 128:h * 128 + 128],
                        transpose=True),
                        reads=[("qdr", t) for t in range(qt0, qt0 + 4)], writes=b1(qgs))

            def emit_kv_load(h):
                if not is_b:
                    return
                kts = KT0 + (h % 2) * 2048
                em.dscal(lambda e, h=h, kts=kts, nch=nch: e.dma_start(
                    out=BIG1[:, kts:kts + nch * 128], in_=kT_dr[h][:, 0:nch * 128]),
                    reads=[("kTd", h)], writes=b1(kts, kts + 2048))
                vhs = VH0 + (h % 2) * 2048
                em.dscal(lambda e, h=h, vhs=vhs, nch=nch: e.dma_start(
                    out=BIG1[:, vhs:vhs + nch * 128].rearrange("p (c e) -> p c e", c=nch),
                    in_=v_dr[0:nch * 128, h * 128:h * 128 + 128].rearrange(
                        "(c p) e -> p c e", p=128)),
                    reads=[("vdr", t, eh) for t in range(nch) for eh in range(2)],
                    writes=b1(vhs, vhs + 2048))

            emit_q_load(0)
            if NH > 1:
                emit_q_load(1)
            emit_kv_load(0)
            for h in range(NH):
                if h + 2 < NH:
                    emit_q_load(h + 2)
                if h + 1 < NH:
                    emit_kv_load(h + 1)
                kvh = h // 4
                qgs = QG0 + (h % 3) * 512
                if is_b:
                    kts = KT0 + (h % 2) * 2048
                    vhs = VH0 + (h % 2) * 2048
                    kt_ap = (lambda c, kts=kts: BIG1[:, kts + c * 128:kts + c * 128 + 128])
                    kt_rd = (lambda c, kts=kts: b1(kts + c * 128))
                    v_ap = (lambda c, vhs=vhs: BIG1[:, vhs + c * 128:vhs + c * 128 + 128])
                    v_rd = (lambda c, vhs=vhs: b1(vhs + c * 128))
                else:
                    kt_ap = (lambda c, kvh=kvh: BIG2[:, kvh * 2048 + c * 128:kvh * 2048 + c * 128 + 128])
                    kt_rd = (lambda c, kvh=kvh: b2(kvh * 2048 + c * 128))
                    v_ap = (lambda c, kvh=kvh: BIG2[:, 8192 + c * 512 + kvh * 128:8192 + c * 512 + kvh * 128 + 128])
                    v_rd = (lambda c: b2(8192 + c * 512))
                dgb = 2048 if h % 2 == 0 else 3584
                dnb = 1536
                ztb = 1024
                bcb = 2560 + (h % 2) * 512
                mc = (h % 2) * 4
                mb = (h % 2) * 512
                bcs_s = (h % 2) * 512
                for j in range(4):
                    qt = qt0 + j
                    em.t(lambda e, j=j, qt=qt, qgs=qgs, kt_ap=kt_ap, dgb=dgb: e.matmul(
                        PS[:, dgb + j * 128:dgb + j * 128 + 128],
                        BIG1[:, qgs + j * 128:qgs + j * 128 + 128],
                        kt_ap(qt), start=True, stop=True),
                        reads=b1(qgs) + kt_rd(qt), writes=pst(dgb))
                    em.v(lambda e, j=j, dgb=dgb, mc=mc: e.reduce_max(
                        MCOL[:, mc + j:mc + j + 1], PS[:, dgb + j * 128:dgb + j * 128 + 128],
                        axis=AX.X),
                        reads=pst(dgb), writes=[("MCOL", mc + j)])
                    em.t(lambda e, j=j, dnb=dnb, mc=mc: e.transpose(
                        PS[0:1, dnb + j * 128:dnb + j * 128 + 128],
                        MCOL[:, mc + j:mc + j + 1], IDE[:, :]),
                        reads=[("MCOL", mc + j), ("IDE",)], writes=pst(dnb))
                em.s(lambda e, dnb=dnb, mb=mb: e.activation(
                    MROW[0:1, mb:mb + 512], PS[0:1, dnb:dnb + 512], AF.Copy,
                    scale=f32(-1.0)),
                    reads=pst(dnb), writes=[("MROW", mb)])
                for c in range(nch):
                    pb = (c % 2) * 512
                    pbs = PB0 + (c % 3) * 512
                    em.t(lambda e, c=c, pb=pb, qgs=qgs, kt_ap=kt_ap: e.matmul(
                        PS[:, pb:pb + 512], kt_ap(c), BIG1[:, qgs:qgs + 512],
                        start=True, stop=False),
                        reads=kt_rd(c) + b1(qgs), writes=pst(pb))
                    em.t(lambda e, pb=pb, mb=mb: e.matmul(
                        PS[:, pb:pb + 512], ONERB[0:1, :], MROW[0:1, mb:mb + 512],
                        start=False, stop=True),
                        reads=[("ONERB",), ("MROW", mb)], writes=pst(pb))
                    em.s(lambda e, pb=pb, pbs=pbs: e.activation(
                        BIG1[:, pbs:pbs + 512], PS[:, pb:pb + 512], AF.Exp),
                        reads=pst(pb), writes=b1(pbs))
                    if causal and c >= qt0:
                        r = c - qt0
                        em.v(lambda e, r=r, pbs=pbs: e.tensor_mul(
                            BIG1[:, pbs:pbs + 512], BIG1[:, pbs:pbs + 512],
                            MSK[:, r * 512:r * 512 + 512]),
                            reads=b1(pbs) + [("MSK",)], writes=b1(pbs))
                    em.t(lambda e, c=c, pbs=pbs, dnb=dnb: e.matmul(
                        PS[0:1, dnb:dnb + 512], ONEC[:, :], BIG1[:, pbs:pbs + 512],
                        start=(c == 0), stop=(c == nch - 1)),
                        reads=[("ONEC",)] + b1(pbs), writes=pst(dnb))
                    em.t(lambda e, c=c, pbs=pbs, v_ap=v_ap, ztb=ztb: e.matmul(
                        PS[:, ztb:ztb + 512], v_ap(c), BIG1[:, pbs:pbs + 512],
                        start=(c == 0), stop=(c == nch - 1)),
                        reads=v_rd(c) + b1(pbs), writes=pst(ztb))
                em.v(lambda e, dnb=dnb: e.reciprocal(RCROW[0:1, :], PS[0:1, dnb:dnb + 512]),
                     reads=pst(dnb), writes=[("RCROW",)])
                em.t(lambda e, bcb=bcb: e.matmul(PS[:, bcb:bcb + 512], ONERF[0:1, :],
                                                 RCROW[0:1, :], start=True, stop=True),
                     reads=[("ONERF",), ("RCROW",)], writes=pst(bcb))
                em.s(lambda e, bcb=bcb, bcs_s=bcs_s: e.activation(
                    BCS[:, bcs_s:bcs_s + 512], PS[:, bcb:bcb + 512], AF.Copy,
                    scale=f32(1.0)),
                    reads=pst(bcb), writes=[("BCS", bcs_s)])
                em.v(lambda e, h=h, ztb=ztb, bcs_s=bcs_s: e.tensor_mul(
                    BIG1[:, ZA0 + h * 512:ZA0 + h * 512 + 512], PS[:, ztb:ztb + 512],
                    BCS[:, bcs_s:bcs_s + 512]),
                    reads=pst(ztb) + [("BCS", bcs_s)], writes=b1(ZA0 + h * 512))
            for jp in range(2):
                for h in range(NH):
                    ws = WO0 + (h % 2) * 2048
                    if is_b:
                        em.dpool(lambda e, h=h, ws=ws: e.dma_start(
                            out=BIG1[:, ws:ws + 2048], in_=woT_dr[h]),
                            reads=[("wos", h)], writes=b1(ws, ws + 2048))
                    else:
                        em.dpool(lambda e, h=h, ws=ws: e.dma_start(
                            out=BIG1[:, ws:ws + 2048],
                            in_=wo_bf[h * 128:h * 128 + 128, :]),
                            reads=[("aob" if kind == "attn_a" else "fob", h)],
                            writes=b1(ws, ws + 2048))
                    for jj in range(2):
                        j = jp * 2 + jj
                        for n in range(4):
                            em.t(lambda e, h=h, j=j, jj=jj, n=n, ws=ws: e.matmul(
                                PS[:, jj * 2048 + n * 512:jj * 2048 + n * 512 + 512],
                                BIG1[:, ZA0 + h * 512 + j * 128:ZA0 + h * 512 + j * 128 + 128],
                                BIG1[:, ws + n * 512:ws + n * 512 + 512],
                                start=(h == 0), stop=(h == NH - 1)),
                                reads=b1(ZA0 + h * 512) + b1(ws + n * 512),
                                writes=pst(jj * 2048 + n * 512))
                for jj in range(2):
                    j = jp * 2 + jj
                    st = qt0 + j
                    obanks = pst(jj * 2048, jj * 2048 + 2048)
                    if kind != "ffn" or "plainffn" in os.environ.get("DEV_TAP", ""):
                        osl = 4096 + jj * 2048
                        em.v(lambda e, jj=jj, osl=osl: e.tensor_copy(
                            STF[:, osl:osl + 2048], PS[:, jj * 2048:jj * 2048 + 2048]),
                            reads=obanks, writes=stf(osl))
                        em.dpool(lambda e, st=st, osl=osl: e.dma_start(
                            out=xdst[st * 128:st * 128 + 128, :],
                            in_=STF[:, osl:osl + 2048], accum_op=ALU.add),
                            reads=stf(osl), writes=[(xtag, st)])
                    else:
                        ffn_epilogue(sub, st, jj, xdst, xtag)

    def ffn_epilogue(sub, st, jj, xdst, xtag):
        xs = (st % 2) * 2048
        EB = 208 + jj * 8
        TB = jj * 56
        obanks = pst(jj * 2048, jj * 2048 + 2048)
        em.dsync(lambda e, xs=xs, st=st: e.dma_start(
            out=STF[:, xs:xs + 2048], in_=xbuf[st * 128:st * 128 + 128, :]),
            reads=[("xb", st)], writes=stf(xs))
        em.s(lambda e, xs=xs, st=st: e.activation(
            STF[:, 4096:6144], STF[:, xs:xs + 2048], AF.Identity,
            bias=STASH[:, 48 + st:48 + st + 1], scale=STASH[:, 32 + st:32 + st + 1]),
            reads=stf(xs) + [("STH", "bln", st), ("STH", "r", st)],
            writes=stf(4096))
        em.v(lambda e, jj=jj: e.tensor_add(
            STF[:, 6144:8192], STF[:, 4096:6144], PS[:, jj * 2048:jj * 2048 + 2048]),
            reads=stf(4096) + obanks, writes=stf(6144))
        em.dpool(lambda e, st=st: e.dma_start(
            out=xdst[st * 128:st * 128 + 128, :], in_=STF[:, 6144:8192],
            accum_op=ALU.add),
            reads=stf(6144), writes=[(xtag, st)])
        em.s(lambda e, xs=xs: e.activation(
            STF[:, xs:xs + 2048], STF[:, 6144:8192], AF.Square, accum_out=SMF[:, EB:EB + 1]),
            reads=stf(6144), writes=stf(xs) + [("SMF", EB + 0)])
        em.v(lambda e: e.tensor_scalar_mul(SMF[:, EB + 1:EB + 2], SMF[:, EB:EB + 1], 1.0 / D),
             reads=[("SMF", EB + 0)], writes=[("SMF", EB + 1)])
        em.s(lambda e: e.activation(SMF[:, EB + 2:EB + 3], SMF[:, EB + 1:EB + 2], AF.Ln, bias=f32(1e-6)),
             reads=[("SMF", EB + 1)], writes=[("SMF", EB + 2)])
        em.s(lambda e: e.activation(SMF[:, EB + 3:EB + 4], SMF[:, EB + 2:EB + 3], AF.Exp, scale=f32(-0.5)),
             reads=[("SMF", EB + 2)], writes=[("SMF", EB + 3)])
        em.v(lambda e: e.tensor_scalar(
            STF[:, 4096:6144], STF[:, 6144:8192], SMF[:, EB + 3:EB + 4], None, ALU.mult),
            reads=stf(6144) + [("SMF", EB + 3)], writes=stf(4096))
        em.v(lambda e: e.tensor_copy(STB[:, 0:2048], STF[:, 4096:6144]),
             reads=stf(4096), writes=stb(0))
        em.dsync(lambda e, st=st: e.dma_start(
            out=t_dr[st * 128:st * 128 + 128, :], in_=STB[:, 0:2048]),
            reads=stb(0), writes=[("tdr", st)])
        rb0 = jj * 2048
        for dc in range(16):
            pb = rb0 + (dc % 2) * 512
            em.t(lambda e, dc=dc, pb=pb: e.transpose(
                PS[:, pb:pb + 128], STF[:, 4096 + dc * 128:4096 + dc * 128 + 128],
                IDE[:, :]),
                reads=stf(4096) + [("IDE",)], writes=pst(pb))
            em.v(lambda e, dc=dc, pb=pb: e.tensor_scalar(
                BCS[:, 0:128], PS[:, pb:pb + 128], LNC[:, 16 + dc:16 + dc + 1],
                None, ALU.mult),
                reads=pst(pb) + [("LNC", 2)], writes=[("BCS",)])
            em.t(lambda e, dc=dc, rb0=rb0: e.matmul(
                PS[:, rb0 + 1024:rb0 + 1032], BCS[:, 0:128], WR8[:, dc * 8:dc * 8 + 8],
                start=(dc == 0), stop=(dc == 15)),
                reads=[("BCS",), ("WR8",)], writes=pst(rb0 + 1024))
        em.s(lambda e, rb0=rb0: e.activation(T8[:, TB + 0:TB + 8], PS[:, rb0 + 1024:rb0 + 1032], AF.Sigmoid),
             reads=pst(rb0 + 1024), writes=[("T8", jj, 0)])
        em.v(lambda e: e.tensor_add(T8[:, TB + 8:TB + 16], T8[:, TB + 0:TB + 8], T8[:, 112:120]),
             reads=[("T8", jj, 0), ("T8", "bias")], writes=[("T8", jj, 1)])
        em.v(lambda e: e.reduce_max(SMF[:, EB + 4:EB + 5], T8[:, TB + 8:TB + 16], axis=AX.X),
             reads=[("T8", jj, 1)], writes=[("SMF", EB + 4)])
        em.v(lambda e: e.tensor_scalar(T8[:, TB + 16:TB + 24], T8[:, TB + 8:TB + 16], SMF[:, EB + 4:EB + 5],
                                       None, ALU.is_ge),
             reads=[("T8", jj, 1), ("SMF", EB + 4)], writes=[("T8", jj, 2)])
        em.v(lambda e: e.scalar_tensor_tensor(T8[:, TB + 24:TB + 32], T8[:, TB + 16:TB + 24], -1e30,
                                              T8[:, TB + 8:TB + 16], ALU.mult, ALU.add),
             reads=[("T8", jj, 2), ("T8", jj, 1)], writes=[("T8", jj, 3)])
        em.v(lambda e: e.reduce_max(SMF[:, EB + 5:EB + 6], T8[:, TB + 24:TB + 32], axis=AX.X),
             reads=[("T8", jj, 3)], writes=[("SMF", EB + 5)])
        em.v(lambda e: e.tensor_scalar(T8[:, TB + 32:TB + 40], T8[:, TB + 24:TB + 32], SMF[:, EB + 5:EB + 6],
                                       None, ALU.is_ge),
             reads=[("T8", jj, 3), ("SMF", EB + 5)], writes=[("T8", jj, 4)])
        em.v(lambda e: e.tensor_add(T8[:, TB + 40:TB + 48], T8[:, TB + 16:TB + 24], T8[:, TB + 32:TB + 40]),
             reads=[("T8", jj, 2), ("T8", jj, 4)], writes=[("T8", jj, 5)])
        em.v(lambda e: e.tensor_mul(T8[:, TB + 48:TB + 56], T8[:, TB + 0:TB + 8], T8[:, TB + 40:TB + 48]),
             reads=[("T8", jj, 0), ("T8", jj, 5)], writes=[("T8", jj, 6)])
        em.v(lambda e: e.reduce_sum(SMF[:, EB + 6:EB + 7], T8[:, TB + 48:TB + 56], axis=AX.X),
             reads=[("T8", jj, 6)], writes=[("SMF", EB + 6)])
        em.v(lambda e: e.reciprocal(SMF[:, EB + 7:EB + 8], SMF[:, EB + 6:EB + 7]),
             reads=[("SMF", EB + 6)], writes=[("SMF", EB + 7)])
        em.v(lambda e, st=st: e.tensor_scalar(
            CMB[:, st * 8:st * 8 + 8], T8[:, TB + 48:TB + 56], SMF[:, EB + 7:EB + 8], None, ALU.mult),
            reads=[("T8", jj, 6), ("SMF", EB + 7)], writes=[("CMB", st)])

    def moe(sub):
        if "nomoe" in os.environ.get("DEV_TAP", ""):
            return
        xdst = out_d if sub == N_SUB - 1 else xbuf
        xtag = "ob" if sub == N_SUB - 1 else "xb"
        WA, WB, TT0, WD0 = 0, 16384, 16384, 8192
        if "spill" in os.environ.get("DEV_TAP", ""):
            em.dpool(lambda e: e.dma_start(out=zall_dr[0, :, 0:128], in_=CMB[:, :]),
                     reads=[("CMB", t) for t in range(NT)], writes=[("zspill", 0)])
        for ex in range(8):
            for dc in range(16):
                em.dpool(lambda e, ex=ex, dc=dc: e.dma_start(
                    out=BIG1[:, WA + dc * 1024:WA + (dc + 1) * 1024],
                    in_=wg[ex, dc * 128:(dc + 1) * 128, :]),
                    reads=[], writes=b1(WA + dc * 1024, WA + (dc + 1) * 1024))
                em.dpool(lambda e, ex=ex, dc=dc: e.dma_start(
                    out=BIG1[:, WB + dc * 1024:WB + (dc + 1) * 1024],
                    in_=wu[ex, dc * 128:(dc + 1) * 128, :]),
                    reads=[], writes=b1(WB + dc * 1024, WB + (dc + 1) * 1024))
            for sh in range(2):
                for dc in range(16):
                    em.dscal(lambda e, dc=dc, sh=sh: e.dma_start(
                        out=BIG2[:, TT0 + dc * 1024:TT0 + dc * 1024 + 1024],
                        in_=t_dr[sh * 1024:sh * 1024 + 1024, dc * 128:dc * 128 + 128],
                        transpose=True),
                        reads=[("tdr", t) for t in range(sh * 8, sh * 8 + 8)],
                        writes=b2(TT0 + dc * 1024, TT0 + (dc + 1) * 1024))
                    em.v(lambda e, dc=dc: e.tensor_scalar(
                        BIG2[:, TT0 + dc * 1024:TT0 + dc * 1024 + 1024],
                        BIG2[:, TT0 + dc * 1024:TT0 + dc * 1024 + 1024],
                        LNC[:, 16 + dc:16 + dc + 1], None, ALU.mult),
                        reads=b2(TT0 + dc * 1024, TT0 + (dc + 1) * 1024) + [("LNC", 2)],
                        writes=b2(TT0 + dc * 1024, TT0 + (dc + 1) * 1024))
                for ic in range(8):
                    for dc in range(16):
                        for n in range(2):
                            em.t(lambda e, ic=ic, dc=dc, n=n: e.matmul(
                                PS[:, n * 512:n * 512 + 512],
                                BIG1[:, WA + dc * 1024 + ic * 128:WA + dc * 1024 + ic * 128 + 128],
                                BIG2[:, TT0 + dc * 1024 + n * 512:TT0 + dc * 1024 + n * 512 + 512],
                                start=(dc == 0), stop=(dc == 15)),
                                reads=b1(WA + dc * 1024 + ic * 128) + b2(TT0 + dc * 1024, TT0 + (dc + 1) * 1024),
                                writes=pst(n * 512))
                            em.t(lambda e, ic=ic, dc=dc, n=n: e.matmul(
                                PS[:, 2048 + n * 512:2048 + n * 512 + 512],
                                BIG1[:, WB + dc * 1024 + ic * 128:WB + dc * 1024 + ic * 128 + 128],
                                BIG2[:, TT0 + dc * 1024 + n * 512:TT0 + dc * 1024 + n * 512 + 512],
                                start=(dc == 0), stop=(dc == 15)),
                                reads=b1(WB + dc * 1024 + ic * 128) + b2(TT0 + dc * 1024, TT0 + (dc + 1) * 1024),
                                writes=pst(2048 + n * 512))
                    em.s(lambda e: e.activation(STF[:, 4096:5120], PS[:, 0:1024], AF.Silu),
                         reads=pst(0, 1024), writes=stf(4096))
                    em.v(lambda e, ic=ic: e.tensor_mul(
                        BIG2[:, ic * 1024:ic * 1024 + 1024], STF[:, 4096:5120],
                        PS[:, 2048:3072]),
                        reads=stf(4096) + pst(2048, 3072), writes=b2(ic * 1024, (ic + 1) * 1024))
                for dh in range(2):
                    wdb = WD0 if (sh * 2 + dh) % 2 == 0 or 'nowddb' in os.environ.get('DEV_TAP','') else 16384
                    for ic in range(8):
                        em.dpool(lambda e, ex=ex, dh=dh, ic=ic, wdb=wdb: e.dma_start(
                            out=BIG2[:, wdb + ic * 1024:wdb + (ic + 1) * 1024],
                            in_=wd[ex, ic * 128:(ic + 1) * 128, dh * 1024:dh * 1024 + 1024]),
                            reads=[], writes=b2(wdb + ic * 1024, wdb + (ic + 1) * 1024))
                    for st_l in range(8):
                        st = sh * 8 + st_l
                        pa = 2048 * (st_l % 2)
                        for ic in range(8):
                            for n in range(2):
                                em.t(lambda e, ic=ic, st_l=st_l, n=n, pa=pa, wdb=wdb: e.matmul(
                                    PS[:, pa + n * 512:pa + n * 512 + 512],
                                    BIG2[:, ic * 1024 + st_l * 128:ic * 1024 + st_l * 128 + 128],
                                    BIG2[:, wdb + ic * 1024 + n * 512:wdb + ic * 1024 + n * 512 + 512],
                                    start=(ic == 0), stop=(ic == 7)),
                                    reads=b2(ic * 1024 + st_l * 128) + b2(wdb + ic * 1024, wdb + (ic + 1) * 1024),
                                    writes=pst(pa + n * 512))
                        osl = 4096 + (st_l % 2) * 2048
                        em.v(lambda e, st=st, pa=pa, osl=osl, ex=ex: e.tensor_scalar(
                            STF[:, osl:osl + 1024], PS[:, pa:pa + 1024],
                            CMB[:, st * 8 + ex:st * 8 + ex + 1], None, ALU.mult),
                            reads=pst(pa, pa + 1024) + [("CMB", st)], writes=stf(osl))
                        em.dpool(lambda e, st=st, osl=osl, dh=dh: e.dma_start(
                            out=xdst[st * 128:st * 128 + 128, dh * 1024:dh * 1024 + 1024],
                            in_=STF[:, osl:osl + 1024], accum_op=ALU.add),
                            reads=stf(osl), writes=[(xtag, st)])

    kinds = ["attn_a", "ffn", "attn_b", "ffn"]
    for sub in range(N_SUB):
        kind = kinds[sub]
        phase0(sub, kind)
        load_yT(kind)
        if kind == "attn_b":
            phase1_b()
        else:
            phase1_qkv(kind)
        phase2(sub, kind)
        if kind == "ffn":
            moe(sub)

    em.build(nc, block, sems)
    st_ctx.close()
    return nc


def kernel(**inputs):
    consts = host_consts()
    nc = build_program()
    x_full = np.ascontiguousarray(np.asarray(inputs["L0_attn_a_x"]))
    base = dict(
        aq=inputs["L0_attn_a_q_kernel"], ak=inputs["L0_attn_a_k_kernel"],
        av=inputs["L0_attn_a_v_kernel"], ao=inputs["L0_attn_a_out_kernel"],
        bwq=inputs["L0_attn_b_W_Q"], bwk=inputs["L0_attn_b_W_K"],
        bwv=inputs["L0_attn_b_W_V"], bwo=inputs["L0_attn_b_W_O"],
        ln1=inputs["L0_ffn_ln1_w"], ln2=inputs["L0_ffn_ln2_w"],
        fwq=inputs["L0_ffn_wq"], fwk=inputs["L0_ffn_wk"],
        fwv=inputs["L0_ffn_wv"], fwo=inputs["L0_ffn_wo"],
        qnw=inputs["L0_ffn_qnorm_w"], knw=inputs["L0_ffn_knorm_w"],
        rw=inputs["L0_ffn_router_w"], rb=inputs["L0_ffn_router_bias"],
        wg=inputs["L0_ffn_wg"], wu=inputs["L0_ffn_wu"], wd=inputs["L0_ffn_wd"],
        **consts,
    )
    base = {k: np.ascontiguousarray(np.asarray(v)) for k, v in base.items()}
    in_maps = []
    for c in range(NCORES):
        m = dict(base)
        m["x_in"] = np.ascontiguousarray(x_full[c])
        in_maps.append(m)
    res = run_bass_kernel_spmd(nc, in_maps, core_ids=list(range(NCORES)))
    if "spill" in os.environ.get("DEV_TAP", ""):
        import pickle
        with open("/root/problem/_dev_spill.pkl", "wb") as f:
            pickle.dump({k: np.asarray(res.results[0][k]).astype(np.float32)
                         for k in ("y_dr", "q_dr", "k_dr", "t_dr", "v_dr", "zall_dr")
                         if k in res.results[0]}, f)
    return np.stack([r["out"] for r in res.results], axis=0)


# version 04
# speedup vs baseline: 1.0251x; 1.0172x over previous
import os
import numpy as np
import ml_dtypes
import concourse.bass as bass
import concourse.mybir as mybir
from concourse.bass_utils import run_bass_kernel_spmd

F32 = mybir.dt.float32
BF16 = mybir.dt.bfloat16
AF = mybir.ActivationFunctionType
ALU = mybir.AluOpType
AX = mybir.AxisListType
BF = ml_dtypes.bfloat16
f32 = np.float32

NCORES = 8
S = 2048
D = 2048
NT = 16
NH = 16
SCALE = f32(1.0 / np.sqrt(128.0))
N_SUB = int(os.environ.get("DEV_NSUB", "4"))


class Op:
    __slots__ = ("idx", "stream", "fn", "deps", "semkey", "is_dma", "milestone", "waits")

    def __init__(self, idx, stream, fn, deps, semkey, is_dma):
        self.idx = idx
        self.stream = stream
        self.fn = fn
        self.deps = deps
        self.semkey = semkey
        self.is_dma = is_dma
        self.milestone = None
        self.waits = None


class Em:
    STREAMS = ("tensor", "vector", "scalar", "pool", "sync")

    def __init__(self):
        self.ops = []
        self.last_write = {}
        self.readers = {}
        self.dsem_map = {}
        self.last_dma_sem = {}

    NDSEM = 64

    def _emit(self, stream, semkey, is_dma, fn, reads, writes):
        idx = len(self.ops)
        deps = set()
        for t in reads:
            w = self.last_write.get(t)
            if w is not None:
                deps.add(w)
        for t in writes:
            w = self.last_write.get(t)
            if w is not None:
                deps.add(w)
            for r in self.readers.get(t, ()):
                deps.add(r)
        deps.discard(idx)
        if is_dma:
            tag = writes[0] if writes else ("q", semkey)
            slot = self.dsem_map.setdefault(tag, len(self.dsem_map) % self.NDSEM)
            semkey = ("d", slot)
            prev = self.last_dma_sem.get(slot)
            if prev is not None:
                deps.add(prev)
            self.last_dma_sem[slot] = idx
        self.ops.append(Op(idx, stream, fn, deps, semkey, is_dma))
        for t in reads:
            self.readers.setdefault(t, []).append(idx)
        for t in writes:
            self.last_write[t] = idx
            self.readers[t] = []
        return idx

    def t(self, fn, reads=(), writes=()):
        return self._emit("tensor", "tensor", False, fn, reads, writes)

    def v(self, fn, reads=(), writes=()):
        return self._emit("vector", "vector", False, fn, reads, writes)

    def s(self, fn, reads=(), writes=()):
        return self._emit("scalar", "scalar", False, fn, reads, writes)

    def p(self, fn, reads=(), writes=()):
        return self._emit("pool", "pool", False, fn, reads, writes)

    def dsync(self, fn, reads=(), writes=()):
        return self._emit("sync", "q_sync", True, fn, reads, writes)

    def dscal(self, fn, reads=(), writes=()):
        return self._emit("scalar", "q_scal", True, fn, reads, writes)

    def dpool(self, fn, reads=(), writes=()):
        return self._emit("pool", "q_pool", True, fn, reads, writes)

    def finalize(self):
        needed = set()
        for o in self.ops:
            if o.is_dma:
                needed.add(o.idx)
            for d in o.deps:
                p = self.ops[d]
                if p.stream == "tensor" and o.stream == "tensor":
                    continue
                needed.add(d)
        counts = {}
        for o in self.ops:
            if o.idx in needed:
                c = counts.get(o.semkey, 0) + 1
                counts[o.semkey] = c
                o.milestone = c
        hi = {s: {} for s in self.STREAMS}
        for o in self.ops:
            waits = {}
            for d in o.deps:
                p = self.ops[d]
                if p.stream == "tensor" and o.stream == "tensor":
                    continue
                v = p.milestone * (16 if p.is_dma else 1)
                if waits.get(p.semkey, 0) < v:
                    waits[p.semkey] = v
            h = hi[o.stream]
            o.waits = []
            for k, v in waits.items():
                if h.get(k, 0) < v:
                    h[k] = v
                    o.waits.append((k, v))
        self.final_counts = {k: v * (16 if (isinstance(k, tuple) and k[0] == "d")
                                      else 1)
                             for k, v in counts.items()}

    def build(self, nc, block, sems):
        self.finalize()
        streams = {s: [o for o in self.ops if o.stream == s] for s in self.STREAMS}

        def runner(stream):
            def run(eng):
                for o in streams[stream]:
                    for k, v in o.waits:
                        eng.wait_ge(sems[k], v)
                    inst = o.fn(eng)
                    if o.milestone is not None:
                        inst.then_inc(sems[o.semkey], 16 if o.is_dma else 1)
                if stream == "sync":
                    for k, v in self.final_counts.items():
                        eng.wait_ge(sems[k], v)
            return run

        block.tensor(runner("tensor"))
        block.vector(runner("vector"))
        block.scalar(runner("scalar"))
        block.gpsimd(runner("pool"))
        block.sync(runner("sync"))


def host_consts():
    pos = np.arange(S, dtype=np.float64)
    fr_a = np.arange(0, 128, 2, dtype=np.float64) / 128.0
    inv_a = 1.0 / (1e4 ** fr_a)
    ang_a = pos[:, None] * inv_a[None, :]
    cos_a = np.cos(ang_a).astype(np.float32)
    sin_a = np.sin(ang_a).astype(np.float32)
    fr_f = np.arange(0, 64, 2, dtype=np.float64) / 64.0
    inv_f = 1.0 / (1e6 ** fr_f)
    ang_f = pos[:, None] * inv_f[None, :]
    cos_f = np.cos(ang_f).astype(np.float32)
    sin_f = np.sin(ang_f).astype(np.float32)
    ident = np.eye(128, dtype=np.float32)
    masks = np.zeros((4, 128, 512), dtype=np.float32)
    sk = np.arange(128)[:, None]
    sq = np.arange(128)[None, :]
    tri = (sk <= sq).astype(np.float32)
    for r in range(4):
        for j in range(4):
            if r < j:
                masks[r, :, j * 128:(j + 1) * 128] = 1.0
            elif r == j:
                masks[r, :, j * 128:(j + 1) * 128] = tri
    return dict(cos_a=cos_a, sin_a=sin_a, cos_f=cos_f, sin_f=sin_f,
                ident=ident, trimask=masks.astype(BF))


def build_program():
    nc = bass.Bass("TRN2", target_bir_lowering=False, debug=False)
    em = Em()

    def din(name, shape, dt=F32):
        return nc.dram_tensor(name, list(shape), dt, kind="ExternalInput").ap()

    x_in = din("x_in", [S, D])
    aq = din("aq", [D, D])
    ak = din("ak", [D, 512])
    av = din("av", [D, 512])
    ao = din("ao", [D, D])
    bwq = din("bwq", [NH, 128, D])
    bwk = din("bwk", [NH, 128, D])
    bwv = din("bwv", [NH, 128, D])
    bwo = din("bwo", [NH, D, 128])
    ln1 = din("ln1", [D])
    ln2 = din("ln2", [D])
    fwq = din("fwq", [D, D])
    fwk = din("fwk", [D, 512])
    fwv = din("fwv", [D, 512])
    fwo = din("fwo", [D, D])
    qnw = din("qnw", [D])
    knw = din("knw", [512])
    rw = din("rw", [D, 8])
    rb = din("rb", [8])
    wg = din("wg", [8, D, 1024])
    wu = din("wu", [8, D, 1024])
    wd = din("wd", [8, 1024, D])
    cos_a = din("cos_a", [S, 64])
    sin_a = din("sin_a", [S, 64])
    cos_f = din("cos_f", [S, 32])
    sin_f = din("sin_f", [S, 32])
    ident_d = din("ident", [128, 128])
    trimask_d = din("trimask", [4, 128, 512], BF16)

    out_d = nc.dram_tensor("out", [S, D], F32, kind="ExternalOutput").ap()

    spill_kind = "ExternalOutput" if "spill" in os.environ.get("DEV_TAP", "") else "Internal"
    xbuf = nc.dram_tensor("xbuf", [S, D], F32, kind="Internal").ap()
    y_dr = nc.dram_tensor("y_dr", [S, D], BF16, kind=spill_kind).ap()
    q_dr = nc.dram_tensor("q_dr", [S, D], BF16, kind=spill_kind).ap()
    k_dr = nc.dram_tensor("k_dr", [S, 512], BF16, kind=spill_kind).ap()
    t_dr = nc.dram_tensor("t_dr", [S, D], BF16, kind=spill_kind).ap()
    v_dr = nc.dram_tensor("v_dr", [S, D], BF16, kind=spill_kind).ap()
    qT_dr = nc.dram_tensor("qT_dr", [NH, 128, S], BF16, kind="Internal").ap()
    kT_dr = nc.dram_tensor("kT_dr", [NH, 128, S], BF16, kind="Internal").ap()
    wqT_dr = nc.dram_tensor("wqT_dr", [NH, 128, D], BF16, kind="Internal").ap()
    wkT_dr = nc.dram_tensor("wkT_dr", [NH, 128, D], BF16, kind="Internal").ap()
    wvT_dr = nc.dram_tensor("wvT_dr", [16, 128, D], BF16, kind="Internal").ap()
    woT_dr = nc.dram_tensor("woT_dr", [NH, 128, D], BF16, kind="Internal").ap()
    zall_dr = nc.dram_tensor("zall_dr", [4, 128, 8192], BF16, kind=spill_kind).ap()
    aob_dr = nc.dram_tensor("aob_dr", [D, D], BF16, kind="Internal").ap()
    fob_dr = nc.dram_tensor("fob_dr", [D, D], BF16, kind="Internal").ap()

    from contextlib import ExitStack
    st_ctx = ExitStack()

    def sb(name, shape, dt, **kw):
        return st_ctx.enter_context(nc.sbuf_tensor(name, shape, dt, **kw))

    BIG1 = sb("BIG1", [128, 32768], BF16, is_transpose_dest=True)
    BIG2 = sb("BIG2", [128, 32768], BF16, is_transpose_dest=True)
    STF = sb("STF", [128, 8192], F32)
    STB = sb("STB", [128, 6144], BF16)
    CON = sb("CON", [128, 3072], F32)
    CONB = sb("CONB", [128, 2560], BF16)
    MSK = sb("MSK", [128, 2048], BF16)
    IDE = sb("IDE", [128, 128], F32)
    ONEC = sb("ONEC", [128, 1], BF16)
    ONERB = sb("ONERB", [1, 128], BF16)
    ONERF = sb("ONERF", [1, 128], F32)
    SMF = sb("SMF", [128, 224], F32)
    STASH = sb("STASH", [128, 64], F32)
    MCOL = sb("MCOL", [128, 8], F32)
    MROW = sb("MROW", [1, 1024], BF16)
    RCROW = sb("RCROW", [1, 512], F32)
    BCS = sb("BCS", [128, 1024], F32)
    CMB = sb("CMB", [128, 128], F32)
    T8 = sb("T8", [128, 128], F32)
    WR8 = sb("WR8", [128, 128], F32)
    LNC = sb("LNC", [128, 32], F32)
    PS = st_ctx.enter_context(nc.psum_tensor("PS", [128, 4096], F32))

    sems = {}
    for k in ("tensor", "vector", "scalar", "pool"):
        sems[k] = st_ctx.enter_context(nc.semaphore("sem_" + k))
    for i in range(Em.NDSEM):
        sems[("d", i)] = st_ctx.enter_context(nc.semaphore("dsem_%d" % i))
    block = st_ctx.enter_context(nc.Block())

    def b1(c0, c1=None):
        c1 = c0 + 512 if c1 is None else c1
        return [("B1", i) for i in range(c0 // 512, (c1 + 511) // 512)]

    def b2(c0, c1=None):
        c1 = c0 + 512 if c1 is None else c1
        return [("B2", i) for i in range(c0 // 512, (c1 + 511) // 512)]

    def stf(c0, c1=None):
        c1 = c0 + 2048 if c1 is None else c1
        return [("STF", i) for i in range(c0 // 2048, (c1 + 2047) // 2048)]

    def stb(c0, c1=None):
        c1 = c0 + 2048 if c1 is None else c1
        return [("STB", i) for i in range(c0 // 2048, (c1 + 2047) // 2048)]

    def pst(c0, c1=None):
        c1 = c0 + 512 if c1 is None else c1
        return [("PS", i) for i in range(c0 // 512, (c1 + 511) // 512)]

    em.dsync(lambda e: e.dma_start(out=IDE[:, :], in_=ident_d[:, :]), writes=[("IDE",)])
    em.v(lambda e: e.memset(ONEC[:, :], 1.0), writes=[("ONEC",)])
    em.v(lambda e: e.memset(ONERB[:, :], 1.0), writes=[("ONERB",)])
    em.v(lambda e: e.memset(ONERF[:, :], 1.0), writes=[("ONERF",)])
    em.dsync(lambda e: e.dma_start(out=MSK[:, :].rearrange("p (r c) -> p r c", r=4),
                                   in_=trimask_d[:, :, :].rearrange("r p c -> p r c")),
             writes=[("MSK",)])
    for dram, w, off in ((cos_a, 64, 0), (sin_a, 64, 1024),
                         (cos_f, 32, 2048), (sin_f, 32, 2560)):
        em.dsync(lambda e, dram=dram, w=w, off=off: e.dma_start(
            out=CON[:, off:off + NT * w].rearrange("p (t w) -> p t w", t=NT),
            in_=dram.rearrange("(t p) w -> p t w", p=128)),
            writes=[("CON", "tab")])

    def bcast_gain(dram_vec, width, dst_off, scale, tagname):
        em.dsync(lambda e: e.dma_start(out=STF[0:1, 2048:2048 + width], in_=dram_vec[None, :]),
                 writes=stf(2048))
        for i in range((width + 511) // 512):
            w0 = i * 512
            w1 = min(width, w0 + 512)
            em.t(lambda e, w0=w0, w1=w1: e.matmul(
                PS[:, 0:w1 - w0], ONERF[0:1, :], STF[0:1, 2048 + w0:2048 + w1],
                start=True, stop=True),
                reads=[("ONERF",)] + stf(2048), writes=pst(0))
            em.s(lambda e, w0=w0, w1=w1: e.activation(
                CONB[:, dst_off + w0:dst_off + w1], PS[:, 0:w1 - w0], AF.Copy, scale=scale),
                reads=pst(0), writes=[("CON", tagname)])

    bcast_gain(qnw, D, 0, SCALE, "gq")
    bcast_gain(knw, 512, 2048, f32(1.0), "gk")
    for vec, cbase, tag in ((ln1, 0, ("LNC", 1)), (ln2, 16, ("LNC", 2))):
        em.dsync(lambda e, vec=vec: e.dma_start(out=STF[0:1, 2048:4096], in_=vec[None, :]),
                 writes=stf(2048))
        for c in range(16):
            em.t(lambda e, c=c: e.transpose(
                PS[:, 1024 + (c % 2) * 512:1024 + (c % 2) * 512 + 1],
                STF[0:1, 2048 + c * 128:2048 + c * 128 + 128], IDE[0:1, 0:1]),
                reads=stf(2048) + [("IDE",)], writes=pst(1024 + (c % 2) * 512))
            em.s(lambda e, c=c, cbase=cbase: e.activation(
                LNC[:, cbase + c:cbase + c + 1],
                PS[:, 1024 + (c % 2) * 512:1024 + (c % 2) * 512 + 1],
                AF.Copy, scale=f32(1.0)),
                reads=pst(1024 + (c % 2) * 512), writes=[tag])
    em.dsync(lambda e: e.dma_start(out=WR8[:, :].rearrange("p (c w) -> p c w", c=16),
                                   in_=rw.rearrange("(c p) w -> p c w", p=128)),
             writes=[("WR8",)])
    em.dsync(lambda e: e.dma_start(out=STF[0:1, 2048:2056], in_=rb[None, :]), writes=stf(2048))
    em.t(lambda e: e.matmul(PS[:, 0:8], ONERF[0:1, :], STF[0:1, 2048:2056], start=True, stop=True),
         reads=[("ONERF",)] + stf(2048), writes=pst(0))
    em.s(lambda e: e.activation(T8[:, 112:120], PS[:, 0:8], AF.Copy, scale=f32(1.0)),
         reads=pst(0), writes=[("T8", "bias")])

    def prep_wo_bf(src, dst, tag):
        for dc in range(16):
            em.dsync(lambda e, dc=dc: e.dma_start(
                out=STF[:, 2048:4096], in_=src[dc * 128:dc * 128 + 128, :]),
                writes=stf(2048))
            em.v(lambda e, dc=dc: e.tensor_copy(STB[:, 0:2048], STF[:, 2048:4096]),
                 reads=stf(2048), writes=stb(0))
            em.dscal(lambda e, dc=dc: e.dma_start(
                out=dst[dc * 128:dc * 128 + 128, :], in_=STB[:, 0:2048]),
                reads=stb(0), writes=[(tag, dc)])

    prep_wo_bf(ao, aob_dr, "aob")
    if N_SUB >= 2:
        prep_wo_bf(fwo, fob_dr, "fob")

    def prep_bw(src3, dst, scale, vmode=False, tagbase="w"):
        for h in range(NH):
            ws = 2048 + (h % 2) * 2048
            em.dsync(lambda e, h=h, ws=ws: e.dma_start(
                out=STF[:, ws:ws + 2048], in_=src3[h]),
                writes=stf(ws))
            for dc in range(16):
                pb = 2048 + (dc % 4) * 512
                em.t(lambda e, h=h, dc=dc, pb=pb, ws=ws: e.transpose(
                    PS[:, pb:pb + 128], STF[:, ws + dc * 128:ws + dc * 128 + 128],
                    IDE[:, :]),
                    reads=stf(ws) + [("IDE",)], writes=pst(pb))
                if scale != 1.0:
                    em.v(lambda e, dc=dc, pb=pb: e.tensor_scalar(
                        STB[:, dc * 128:dc * 128 + 128], PS[:, pb:pb + 128],
                        float(scale), None, ALU.mult),
                        reads=pst(pb), writes=stb(0))
                else:
                    em.v(lambda e, dc=dc, pb=pb: e.tensor_copy(
                        STB[:, dc * 128:dc * 128 + 128], PS[:, pb:pb + 128]),
                        reads=pst(pb), writes=stb(0))
            if vmode:
                em.dscal(lambda e, h=h: e.dma_start(
                    out=dst[:, :, h * 128:(h + 1) * 128].rearrange("c p e -> p c e"),
                    in_=STB[:, 0:2048].rearrange("p (c e) -> p c e", c=16)),
                    reads=stb(0), writes=[("wv_scr",)])
            else:
                em.dscal(lambda e, h=h: e.dma_start(out=dst[h], in_=STB[:, 0:2048]),
                         reads=stb(0), writes=[(tagbase, h)])

    if N_SUB >= 3:
        prep_bw(bwq, wqT_dr, SCALE, tagbase="wqs")
        prep_bw(bwk, wkT_dr, 1.0, tagbase="wks")
        prep_bw(bwv, wvT_dr, 1.0, vmode=True)
        for h in range(NH):
            em.dsync(lambda e, h=h: e.dma_start(
                out=STF[:, 2048:4096].rearrange("p (c e) -> p c e", c=16),
                in_=bwo[h].rearrange("(c p) e -> p c e", p=128)),
                writes=stf(2048))
            for dc in range(16):
                pb = 2048 + (dc % 4) * 512
                em.t(lambda e, dc=dc, pb=pb: e.transpose(
                    PS[:, pb:pb + 128], STF[:, 2048 + dc * 128:2048 + dc * 128 + 128],
                    IDE[:, :]),
                    reads=stf(2048) + [("IDE",)], writes=pst(pb))
                em.v(lambda e, dc=dc, pb=pb: e.tensor_copy(
                    STB[:, dc * 128:dc * 128 + 128], PS[:, pb:pb + 128]),
                    reads=pst(pb), writes=stb(0))
            em.dscal(lambda e, h=h: e.dma_start(out=woT_dr[h], in_=STB[:, 0:2048]),
                     reads=stb(0), writes=[("wos", h)])

    def phase0(sub, kind):
        xsrc = x_in if sub == 0 else xbuf
        for st in range(NT):
            B = st * 12
            xs = (st % 2) * 2048
            r0 = st * 128
            em.dsync(lambda e, xs=xs, r0=r0: e.dma_start(
                out=STF[:, xs:xs + 2048], in_=xsrc[r0:r0 + 128, :]),
                reads=[("xb", st)] if sub > 0 else [], writes=stf(xs))
            if sub == 0:
                em.dscal(lambda e, xs=xs, r0=r0: e.dma_start(
                    out=xbuf[r0:r0 + 128, :], in_=STF[:, xs:xs + 2048]),
                    reads=stf(xs), writes=[("xb", st)])
            if sub == N_SUB - 1:
                em.dscal(lambda e, xs=xs, r0=r0: e.dma_start(
                    out=out_d[r0:r0 + 128, :], in_=STF[:, xs:xs + 2048]),
                    reads=stf(xs), writes=[("ob", st)])
            em.v(lambda e, xs=xs, B=B: e.reduce_sum(SMF[:, B:B + 1], STF[:, xs:xs + 2048], axis=AX.X),
                 reads=stf(xs), writes=[("SMF", B)])
            ys = (st % 2) * 2048
            em.s(lambda e, xs=xs, B=B, ys=ys: e.activation(
                STB[:, ys:ys + 2048], STF[:, xs:xs + 2048], AF.Square,
                accum_out=SMF[:, B + 1:B + 2]),
                reads=stf(xs), writes=stb(ys) + [("SMF", B + 1)])
            em.v(lambda e, B=B: e.tensor_scalar_mul(SMF[:, B + 2:B + 3], SMF[:, B:B + 1], 1.0 / D),
                 reads=[("SMF", B)], writes=[("SMF", B + 2)])
            em.v(lambda e, st=st, B=B: e.tensor_copy(STASH[:, st:st + 1], SMF[:, B + 2:B + 3]),
                 reads=[("SMF", B + 2)], writes=[("STH", "m", st)])
            em.v(lambda e, B=B: e.tensor_mul(SMF[:, B + 3:B + 4], SMF[:, B + 2:B + 3], SMF[:, B + 2:B + 3]),
                 reads=[("SMF", B + 2)], writes=[("SMF", B + 3)])
            em.v(lambda e, B=B: e.scalar_tensor_tensor(
                SMF[:, B + 4:B + 5], SMF[:, B + 1:B + 2], 1.0 / D, SMF[:, B + 3:B + 4],
                ALU.mult, ALU.subtract),
                reads=[("SMF", B + 1), ("SMF", B + 3)], writes=[("SMF", B + 4)])
            em.s(lambda e, B=B: e.activation(SMF[:, B + 5:B + 6], SMF[:, B + 4:B + 5], AF.Ln, bias=f32(1e-5)),
                 reads=[("SMF", B + 4)], writes=[("SMF", B + 5)])
            em.s(lambda e, st=st, B=B: e.activation(
                STASH[:, 32 + st:32 + st + 1], SMF[:, B + 5:B + 6], AF.Exp, scale=f32(-0.5)),
                reads=[("SMF", B + 5)], writes=[("STH", "r", st)])
            if kind == "ffn":
                em.v(lambda e, st=st, B=B: e.tensor_mul(
                    SMF[:, B + 8:B + 9], STASH[:, 32 + st:32 + st + 1], STASH[:, 32 + st:32 + st + 1]),
                    reads=[("STH", "r", st)], writes=[("SMF", B + 8)])
                em.v(lambda e, B=B: e.tensor_mul(SMF[:, B + 9:B + 10], SMF[:, B + 4:B + 5], SMF[:, B + 8:B + 9]),
                     reads=[("SMF", B + 4), ("SMF", B + 8)], writes=[("SMF", B + 9)])
                em.s(lambda e, B=B: e.activation(SMF[:, B + 10:B + 11], SMF[:, B + 9:B + 10], AF.Ln,
                                                 bias=f32(1e-6)),
                     reads=[("SMF", B + 9)], writes=[("SMF", B + 10)])
                em.s(lambda e, B=B: e.activation(SMF[:, B + 7:B + 8], SMF[:, B + 10:B + 11], AF.Exp,
                                                 scale=f32(-0.5)),
                     reads=[("SMF", B + 10)], writes=[("SMF", B + 7)])
                em.v(lambda e, st=st, B=B: e.tensor_mul(
                    SMF[:, B + 6:B + 7], STASH[:, 32 + st:32 + st + 1], SMF[:, B + 7:B + 8]),
                    reads=[("STH", "r", st), ("SMF", B + 7)], writes=[("SMF", B + 6)])
                em.v(lambda e, st=st, B=B: e.tensor_scalar(
                    STASH[:, 48 + st:48 + st + 1], SMF[:, B + 2:B + 3],
                    STASH[:, 32 + st:32 + st + 1], -1.0, ALU.mult, ALU.mult),
                    reads=[("SMF", B + 2), ("STH", "r", st)], writes=[("STH", "bln", st)])
            else:
                em.v(lambda e, st=st, B=B: e.tensor_copy(SMF[:, B + 6:B + 7], STASH[:, 32 + st:32 + st + 1]),
                     reads=[("STH", "r", st)], writes=[("SMF", B + 6)])
            em.v(lambda e, st=st, B=B: e.tensor_scalar(
                STASH[:, 16 + st:16 + st + 1], SMF[:, B + 2:B + 3], SMF[:, B + 6:B + 7], -1.0,
                ALU.mult, ALU.mult),
                reads=[("SMF", B + 2), ("SMF", B + 6)], writes=[("STH", "b", st)])
            em.v(lambda e, xs=xs, ys=ys, st=st, B=B: e.tensor_scalar(
                STB[:, ys:ys + 2048], STF[:, xs:xs + 2048], SMF[:, B + 6:B + 7],
                STASH[:, 16 + st:16 + st + 1], ALU.mult, ALU.add),
                reads=stf(xs) + [("SMF", B + 6), ("STH", "b", st)], writes=stb(ys))
            em.dsync(lambda e, ys=ys, r0=r0: e.dma_start(
                out=y_dr[r0:r0 + 128, :], in_=STB[:, ys:ys + 2048]),
                reads=stb(ys), writes=[("ydr", r0 // 128)])

    def load_yT(kind):
        ally = [("ydr", t) for t in range(NT)]
        for dc in range(16):
            em.dscal(lambda e, dc=dc: e.dma_start(
                out=BIG1[:, dc * 2048:(dc + 1) * 2048],
                in_=y_dr[:, dc * 128:(dc + 1) * 128], transpose=True),
                reads=ally, writes=b1(dc * 2048, (dc + 1) * 2048))
            if kind == "ffn":
                em.v(lambda e, dc=dc: e.tensor_scalar(
                    BIG1[:, dc * 2048:(dc + 1) * 2048],
                    BIG1[:, dc * 2048:(dc + 1) * 2048],
                    LNC[:, dc:dc + 1], None, ALU.mult),
                    reads=b1(dc * 2048, (dc + 1) * 2048) + [("LNC", 1)],
                    writes=b1(dc * 2048, (dc + 1) * 2048))

    def rope_natural(qf_off, out_off, st, nheads, mode, reads, writes):
        if mode == "full":
            half, coff, soff, tw, rot = 64, 0, 1024, 64, 128
        else:
            half, coff, soff, tw, rot = 32, 2048, 2560, 32, 64
        qv = STF[:, qf_off:qf_off + nheads * 128].rearrange("p (h e) -> p h e", h=nheads)
        ov = STB[:, out_off:out_off + nheads * 128].rearrange("p (h e) -> p h e", h=nheads)
        cosb = CON[:, coff + st * tw:coff + st * tw + half]
        sinb = CON[:, soff + st * tw:soff + st * tw + half]
        cosB = bass.AP(cosb.tensor, cosb.offset, [cosb.ap[0], [0, nheads], cosb.ap[1]])
        sinB = bass.AP(sinb.tensor, sinb.offset, [sinb.ap[0], [0, nheads], sinb.ap[1]])
        t1 = STF[:, 6144:6144 + nheads * half].rearrange("p (h e) -> p h e", h=nheads)
        t2 = STF[:, 7168:7168 + nheads * half].rearrange("p (h e) -> p h e", h=nheads)
        lo = (slice(None), slice(None), slice(0, half))
        hi = (slice(None), slice(None), slice(half, 2 * half))
        rd = list(reads) + [("CON", "tab")]
        wrt = list(writes)
        t3 = stf(6144, 8192)
        em.v(lambda e: e.tensor_mul(t1, qv[lo], cosB), reads=rd, writes=t3)
        em.v(lambda e: e.tensor_mul(t2, qv[hi], sinB), reads=rd, writes=t3)
        em.v(lambda e: e.tensor_sub(ov[lo], t1, t2), reads=t3, writes=wrt)
        em.v(lambda e: e.tensor_mul(t1, qv[hi], cosB), reads=rd, writes=t3)
        em.v(lambda e: e.tensor_mul(t2, qv[lo], sinB), reads=rd, writes=t3)
        em.v(lambda e: e.tensor_add(ov[hi], t1, t2), reads=t3, writes=wrt)
        if mode == "part":
            pas = (slice(None), slice(None), slice(rot, 128))
            em.v(lambda e: e.tensor_copy(ov[pas], qv[pas]), reads=rd, writes=wrt)

    def phase1_qkv(kind):
        wq_d, wk_d, wv_d = (aq, ak, av) if kind == "attn_a" else (fwq, fwk, fwv)
        gq_rd = [("CON", "gq")] if kind == "ffn" else []
        gk_rd = [("CON", "gk")] if kind == "ffn" else []
        mode = "full" if kind == "attn_a" else "part"
        for dc in range(16):
            em.dpool(lambda e, dc=dc: e.dma_start(
                out=BIG2[:, dc * 2048:(dc + 1) * 2048],
                in_=wq_d[dc * 128:(dc + 1) * 128, :]),
                reads=[], writes=b2(dc * 2048, (dc + 1) * 2048))
        for st in range(NT):
            pa = (st % 2) * 2048
            for dc in range(16):
                for n in range(4):
                    em.t(lambda e, st=st, dc=dc, n=n, pa=pa: e.matmul(
                        PS[:, pa + n * 512:pa + n * 512 + 512],
                        BIG1[:, dc * 2048 + st * 128:dc * 2048 + st * 128 + 128],
                        BIG2[:, dc * 2048 + n * 512:dc * 2048 + n * 512 + 512],
                        start=(dc == 0), stop=(dc == 15)),
                        reads=b1(dc * 2048 + st * 128) + b2(dc * 2048, (dc + 1) * 2048),
                        writes=pst(pa + n * 512))
            pb_rd = pst(pa, pa + 2048)
            B1 = 192 + (st % 4) * 4
            sqs = 4096 + (st % 2) * 1024
            em.s(lambda e, pa=pa, B1=B1, sqs=sqs: e.activation(
                STF[:, sqs:sqs + 1024], PS[:, pa:pa + 1024], AF.Square,
                accum_out=SMF[:, B1:B1 + 1]),
                reads=pb_rd, writes=stf(4096) + [("SMF", B1)])
            em.s(lambda e, pa=pa, B1=B1, sqs=sqs: e.activation(
                STF[:, sqs:sqs + 1024], PS[:, pa + 1024:pa + 2048], AF.Square,
                accum_out=SMF[:, B1 + 3:B1 + 4]),
                reads=pb_rd, writes=stf(4096) + [("SMF", B1 + 3)])
            em.v(lambda e, B1=B1: e.tensor_add(SMF[:, B1:B1 + 1], SMF[:, B1:B1 + 1],
                                               SMF[:, B1 + 3:B1 + 4]),
                 reads=[("SMF", B1), ("SMF", B1 + 3)], writes=[("SMF", B1)])
            em.v(lambda e, B1=B1: e.tensor_scalar_mul(SMF[:, B1 + 1:B1 + 2], SMF[:, B1:B1 + 1], 1.0 / D),
                 reads=[("SMF", B1)], writes=[("SMF", B1 + 1)])
            em.s(lambda e, B1=B1: e.activation(SMF[:, B1 + 2:B1 + 3], SMF[:, B1 + 1:B1 + 2], AF.Ln, bias=f32(1e-6)),
                 reads=[("SMF", B1 + 1)], writes=[("SMF", B1 + 2)])
            em.s(lambda e, B1=B1: e.activation(SMF[:, B1 + 1:B1 + 2], SMF[:, B1 + 2:B1 + 3], AF.Exp, scale=f32(-0.5)),
                 reads=[("SMF", B1 + 2)], writes=[("SMF", B1 + 1)])
            if kind == "attn_a":
                em.v(lambda e, pa=pa, B1=B1: e.tensor_scalar(
                    STF[:, 0:2048], PS[:, pa:pa + 2048], SMF[:, B1 + 1:B1 + 2], float(SCALE),
                    ALU.mult, ALU.mult),
                    reads=pb_rd + [("SMF", B1 + 1)], writes=stf(0))
            else:
                em.v(lambda e, pa=pa, B1=B1: e.scalar_tensor_tensor(
                    STF[:, 0:2048], PS[:, pa:pa + 2048], SMF[:, B1 + 1:B1 + 2],
                    CONB[:, 0:2048], ALU.mult, ALU.mult),
                    reads=pb_rd + [("SMF", B1 + 1)] + gq_rd, writes=stf(0))
            rope_natural(0, 0, st, 16, mode, reads=stf(0), writes=stb(0))
            em.dsync(lambda e, st=st: e.dma_start(
                out=q_dr[st * 128:st * 128 + 128, :], in_=STB[:, 0:2048]),
                reads=stb(0), writes=[("qdr", st)])
        for dc in range(16):
            em.dpool(lambda e, dc=dc: e.dma_start(
                out=BIG2[:, dc * 512:(dc + 1) * 512],
                in_=wk_d[dc * 128:(dc + 1) * 128, :]),
                reads=[], writes=b2(dc * 512, (dc + 1) * 512))
        for st in range(NT):
            pa = (st % 2) * 2048
            for dc in range(16):
                em.t(lambda e, st=st, dc=dc, pa=pa: e.matmul(
                    PS[:, pa:pa + 512],
                    BIG1[:, dc * 2048 + st * 128:dc * 2048 + st * 128 + 128],
                    BIG2[:, dc * 512:dc * 512 + 512],
                    start=(dc == 0), stop=(dc == 15)),
                    reads=b1(dc * 2048 + st * 128) + b2(dc * 512),
                    writes=pst(pa))
            B1 = 192 + (st % 4) * 4
            sqs = 4096 + (st % 2) * 1024
            em.s(lambda e, pa=pa, B1=B1, sqs=sqs: e.activation(
                STF[:, sqs:sqs + 512], PS[:, pa:pa + 512], AF.Square,
                accum_out=SMF[:, B1:B1 + 1]),
                reads=pst(pa), writes=stf(4096) + [("SMF", B1)])
            em.v(lambda e, B1=B1: e.tensor_scalar_mul(SMF[:, B1 + 1:B1 + 2], SMF[:, B1:B1 + 1], 1.0 / 512),
                 reads=[("SMF", B1)], writes=[("SMF", B1 + 1)])
            em.s(lambda e, B1=B1: e.activation(SMF[:, B1 + 2:B1 + 3], SMF[:, B1 + 1:B1 + 2], AF.Ln, bias=f32(1e-6)),
                 reads=[("SMF", B1 + 1)], writes=[("SMF", B1 + 2)])
            em.s(lambda e, B1=B1: e.activation(SMF[:, B1 + 1:B1 + 2], SMF[:, B1 + 2:B1 + 3], AF.Exp, scale=f32(-0.5)),
                 reads=[("SMF", B1 + 2)], writes=[("SMF", B1 + 1)])
            if kind == "attn_a":
                em.v(lambda e, pa=pa, B1=B1: e.tensor_scalar(
                    STF[:, 0:512], PS[:, pa:pa + 512], SMF[:, B1 + 1:B1 + 2], None, ALU.mult),
                    reads=pst(pa) + [("SMF", B1 + 1)], writes=stf(0))
            else:
                em.v(lambda e, pa=pa, B1=B1: e.scalar_tensor_tensor(
                    STF[:, 0:512], PS[:, pa:pa + 512], SMF[:, B1 + 1:B1 + 2],
                    CONB[:, 2048:2560], ALU.mult, ALU.mult),
                    reads=pst(pa) + [("SMF", B1 + 1)] + gk_rd, writes=stf(0))
            rope_natural(0, 0, st, 4, mode, reads=stf(0), writes=stb(0))
            em.dsync(lambda e, st=st: e.dma_start(
                out=k_dr[st * 128:st * 128 + 128, :], in_=STB[:, 0:512]),
                reads=stb(0), writes=[("kdr", st)])
        for dc in range(16):
            em.dpool(lambda e, dc=dc: e.dma_start(
                out=BIG2[:, dc * 512:(dc + 1) * 512],
                in_=wv_d[dc * 128:(dc + 1) * 128, :]),
                reads=[], writes=b2(dc * 512, (dc + 1) * 512))
        for st in range(NT):
            pa = (st % 2) * 2048
            for dc in range(16):
                em.t(lambda e, st=st, dc=dc, pa=pa: e.matmul(
                    PS[:, pa:pa + 512],
                    BIG1[:, dc * 2048 + st * 128:dc * 2048 + st * 128 + 128],
                    BIG2[:, dc * 512:dc * 512 + 512],
                    start=(dc == 0), stop=(dc == 15)),
                    reads=b1(dc * 2048 + st * 128) + b2(dc * 512),
                    writes=pst(pa))
            em.v(lambda e, st=st, pa=pa: e.tensor_copy(
                BIG2[:, 8192 + st * 512:8192 + st * 512 + 512], PS[:, pa:pa + 512]),
                reads=pst(pa), writes=b2(8192 + st * 512))
            if "spill" in os.environ.get("DEV_TAP", ""):
                em.dsync(lambda e, st=st: e.dma_start(
                    out=v_dr[st * 128:st * 128 + 128, 0:512],
                    in_=BIG2[:, 8192 + st * 512:8192 + st * 512 + 512]),
                    reads=b2(8192 + st * 512), writes=[("vdr", st, 0)])

    def phase1_b():
        for which, wsrc, dst, dtag in (("wqs", wqT_dr, qT_dr, "qTd"),
                                       ("wks", wkT_dr, kT_dr, "kTd")):
            for h in range(NH):
                ws = (h % 2) * 2048
                em.dpool(lambda e, h=h, ws=ws, wsrc=wsrc: e.dma_start(
                    out=STB[:, ws:ws + 2048], in_=wsrc[h]),
                    reads=[(which, h)], writes=stb(ws))
                pa = (h % 2) * 2048
                for dc in range(16):
                    for n in range(4):
                        em.t(lambda e, dc=dc, n=n, pa=pa, ws=ws: e.matmul(
                            PS[:, pa + n * 512:pa + n * 512 + 512],
                            STB[:, ws + dc * 128:ws + dc * 128 + 128],
                            BIG1[:, dc * 2048 + n * 512:dc * 2048 + n * 512 + 512],
                            start=(dc == 0), stop=(dc == 15)),
                            reads=stb(ws) + b1(dc * 2048, (dc + 1) * 2048),
                            writes=pst(pa + n * 512))
                em.v(lambda e, pa=pa: e.tensor_copy(STB[:, 4096:6144], PS[:, pa:pa + 2048]),
                     reads=pst(pa, pa + 2048), writes=stb(4096))
                em.dsync(lambda e, h=h, dst=dst: e.dma_start(out=dst[h], in_=STB[:, 4096:6144]),
                         reads=stb(4096), writes=[(dtag, h)])
        for eh in range(2):
            em.dpool(lambda e, eh=eh: e.dma_start(
                out=BIG2[:, 16384:32768].rearrange("p (c e) -> p c e", c=16),
                in_=wvT_dr[:, :, eh * 1024:eh * 1024 + 1024].rearrange("c p e -> p c e")),
                reads=[("wv_scr",)], writes=b2(16384, 32768))
            for st in range(NT):
                pa = (st % 2) * 2048
                for dc in range(16):
                    for n in range(2):
                        em.t(lambda e, st=st, dc=dc, n=n, pa=pa: e.matmul(
                            PS[:, pa + n * 512:pa + n * 512 + 512],
                            BIG1[:, dc * 2048 + st * 128:dc * 2048 + st * 128 + 128],
                            BIG2[:, 16384 + dc * 1024 + n * 512:16384 + dc * 1024 + n * 512 + 512],
                            start=(dc == 0), stop=(dc == 15)),
                            reads=b1(dc * 2048 + st * 128) + b2(16384 + dc * 1024, 16384 + (dc + 1) * 1024),
                            writes=pst(pa + n * 512))
                em.v(lambda e, pa=pa, eh=eh: e.tensor_copy(
                    STB[:, 4096 + eh * 1024:4096 + eh * 1024 + 1024], PS[:, pa:pa + 1024]),
                    reads=pst(pa, pa + 1024), writes=stb(4096))
                em.dsync(lambda e, st=st, eh=eh: e.dma_start(
                    out=v_dr[st * 128:st * 128 + 128, eh * 1024:eh * 1024 + 1024],
                    in_=STB[:, 4096 + eh * 1024:4096 + eh * 1024 + 1024]),
                    reads=stb(4096), writes=[("vdr", st, eh)])

    def phase2(sub, kind):
        causal = kind != "attn_a" and "nomask" not in os.environ.get("DEV_TAP", "")
        is_b = kind == "attn_b"
        xdst = out_d if sub == N_SUB - 1 else xbuf
        xtag = "ob" if sub == N_SUB - 1 else "xb"
        wo_bf = aob_dr if kind == "attn_a" else (fob_dr if kind == "ffn" else None)
        if not is_b:
            for kv in range(4):
                em.dscal(lambda e, kv=kv: e.dma_start(
                    out=BIG2[:, kv * 2048:kv * 2048 + 2048],
                    in_=k_dr[:, kv * 128:kv * 128 + 128], transpose=True),
                    reads=[("kdr", t) for t in range(NT)],
                    writes=b2(kv * 2048, (kv + 1) * 2048))
        QG0, PB0, WO0, ZA0, KT0, VH0 = 0, 1536, 4096, 8192, 16384, 20480
        for g in range(4):
            qt0 = g * 4
            nch = (qt0 + 4) if causal else NT
            def emit_q_load(h):
                qgs = QG0 + (h % 3) * 512
                if is_b:
                    em.dscal(lambda e, h=h, qgs=qgs, g=g: e.dma_start(
                        out=BIG1[:, qgs:qgs + 512], in_=qT_dr[h][:, g * 512:g * 512 + 512]),
                        reads=[("qTd", h)], writes=b1(qgs))
                else:
                    em.dscal(lambda e, h=h, qgs=qgs, g=g: e.dma_start(
                        out=BIG1[:, qgs:qgs + 512],
                        in_=q_dr[g * 512:g * 512 + 512, h * 128:h * 128 + 128],
                        transpose=True),
                        reads=[("qdr", t) for t in range(qt0, qt0 + 4)], writes=b1(qgs))

            def emit_kv_load(h):
                if not is_b:
                    return
                kts = KT0 + (h % 2) * 2048
                em.dscal(lambda e, h=h, kts=kts, nch=nch: e.dma_start(
                    out=BIG1[:, kts:kts + nch * 128], in_=kT_dr[h][:, 0:nch * 128]),
                    reads=[("kTd", h)], writes=b1(kts, kts + 2048))
                vhs = VH0 + (h % 2) * 2048
                em.dscal(lambda e, h=h, vhs=vhs, nch=nch: e.dma_start(
                    out=BIG1[:, vhs:vhs + nch * 128].rearrange("p (c e) -> p c e", c=nch),
                    in_=v_dr[0:nch * 128, h * 128:h * 128 + 128].rearrange(
                        "(c p) e -> p c e", p=128)),
                    reads=[("vdr", t, eh) for t in range(nch) for eh in range(2)],
                    writes=b1(vhs, vhs + 2048))

            emit_q_load(0)
            if NH > 1:
                emit_q_load(1)
            emit_kv_load(0)
            for h in range(NH):
                if h + 2 < NH:
                    emit_q_load(h + 2)
                if h + 1 < NH:
                    emit_kv_load(h + 1)
                kvh = h // 4
                qgs = QG0 + (h % 3) * 512
                if is_b:
                    kts = KT0 + (h % 2) * 2048
                    vhs = VH0 + (h % 2) * 2048
                    kt_ap = (lambda c, kts=kts: BIG1[:, kts + c * 128:kts + c * 128 + 128])
                    kt_rd = (lambda c, kts=kts: b1(kts + c * 128))
                    v_ap = (lambda c, vhs=vhs: BIG1[:, vhs + c * 128:vhs + c * 128 + 128])
                    v_rd = (lambda c, vhs=vhs: b1(vhs + c * 128))
                else:
                    kt_ap = (lambda c, kvh=kvh: BIG2[:, kvh * 2048 + c * 128:kvh * 2048 + c * 128 + 128])
                    kt_rd = (lambda c, kvh=kvh: b2(kvh * 2048 + c * 128))
                    v_ap = (lambda c, kvh=kvh: BIG2[:, 8192 + c * 512 + kvh * 128:8192 + c * 512 + kvh * 128 + 128])
                    v_rd = (lambda c: b2(8192 + c * 512))
                dgb = 2048 if h % 2 == 0 else 3584
                dnb = 1536
                ztb = 1024
                bcb = 2560 + (h % 2) * 512
                mc = (h % 2) * 4
                mb = (h % 2) * 512
                bcs_s = (h % 2) * 512
                for j in range(4):
                    qt = qt0 + j
                    em.t(lambda e, j=j, qt=qt, qgs=qgs, kt_ap=kt_ap, dgb=dgb: e.matmul(
                        PS[:, dgb + j * 128:dgb + j * 128 + 128],
                        BIG1[:, qgs + j * 128:qgs + j * 128 + 128],
                        kt_ap(qt), start=True, stop=True),
                        reads=b1(qgs) + kt_rd(qt), writes=pst(dgb))
                    em.v(lambda e, j=j, dgb=dgb, mc=mc: e.reduce_max(
                        MCOL[:, mc + j:mc + j + 1], PS[:, dgb + j * 128:dgb + j * 128 + 128],
                        axis=AX.X),
                        reads=pst(dgb), writes=[("MCOL", mc + j)])
                    em.t(lambda e, j=j, dnb=dnb, mc=mc: e.transpose(
                        PS[0:1, dnb + j * 128:dnb + j * 128 + 128],
                        MCOL[:, mc + j:mc + j + 1], IDE[:, :]),
                        reads=[("MCOL", mc + j), ("IDE",)], writes=pst(dnb))
                em.s(lambda e, dnb=dnb, mb=mb: e.activation(
                    MROW[0:1, mb:mb + 512], PS[0:1, dnb:dnb + 512], AF.Copy,
                    scale=f32(-1.0)),
                    reads=pst(dnb), writes=[("MROW", mb)])
                for c in range(nch):
                    pb = (c % 2) * 512
                    pbs = PB0 + (c % 3) * 512
                    em.t(lambda e, c=c, pb=pb, qgs=qgs, kt_ap=kt_ap: e.matmul(
                        PS[:, pb:pb + 512], kt_ap(c), BIG1[:, qgs:qgs + 512],
                        start=True, stop=False),
                        reads=kt_rd(c) + b1(qgs), writes=pst(pb))
                    em.t(lambda e, pb=pb, mb=mb: e.matmul(
                        PS[:, pb:pb + 512], ONERB[0:1, :], MROW[0:1, mb:mb + 512],
                        start=False, stop=True),
                        reads=[("ONERB",), ("MROW", mb)], writes=pst(pb))
                    em.s(lambda e, pb=pb, pbs=pbs: e.activation(
                        BIG1[:, pbs:pbs + 512], PS[:, pb:pb + 512], AF.Exp),
                        reads=pst(pb), writes=b1(pbs))
                    if causal and c >= qt0:
                        r = c - qt0
                        em.v(lambda e, r=r, pbs=pbs: e.tensor_mul(
                            BIG1[:, pbs:pbs + 512], BIG1[:, pbs:pbs + 512],
                            MSK[:, r * 512:r * 512 + 512]),
                            reads=b1(pbs) + [("MSK",)], writes=b1(pbs))
                    em.t(lambda e, c=c, pbs=pbs, dnb=dnb: e.matmul(
                        PS[0:1, dnb:dnb + 512], ONEC[:, :], BIG1[:, pbs:pbs + 512],
                        start=(c == 0), stop=(c == nch - 1)),
                        reads=[("ONEC",)] + b1(pbs), writes=pst(dnb))
                    em.t(lambda e, c=c, pbs=pbs, v_ap=v_ap, ztb=ztb: e.matmul(
                        PS[:, ztb:ztb + 512], v_ap(c), BIG1[:, pbs:pbs + 512],
                        start=(c == 0), stop=(c == nch - 1)),
                        reads=v_rd(c) + b1(pbs), writes=pst(ztb))
                em.v(lambda e, dnb=dnb: e.reciprocal(RCROW[0:1, :], PS[0:1, dnb:dnb + 512]),
                     reads=pst(dnb), writes=[("RCROW",)])
                em.t(lambda e, bcb=bcb: e.matmul(PS[:, bcb:bcb + 512], ONERF[0:1, :],
                                                 RCROW[0:1, :], start=True, stop=True),
                     reads=[("ONERF",), ("RCROW",)], writes=pst(bcb))
                em.s(lambda e, bcb=bcb, bcs_s=bcs_s: e.activation(
                    BCS[:, bcs_s:bcs_s + 512], PS[:, bcb:bcb + 512], AF.Copy,
                    scale=f32(1.0)),
                    reads=pst(bcb), writes=[("BCS", bcs_s)])
                em.v(lambda e, h=h, ztb=ztb, bcs_s=bcs_s: e.tensor_mul(
                    BIG1[:, ZA0 + h * 512:ZA0 + h * 512 + 512], PS[:, ztb:ztb + 512],
                    BCS[:, bcs_s:bcs_s + 512]),
                    reads=pst(ztb) + [("BCS", bcs_s)], writes=b1(ZA0 + h * 512))
            for jp in range(2):
                for h in range(NH):
                    ws = WO0 + (h % 2) * 2048
                    if is_b:
                        em.dpool(lambda e, h=h, ws=ws: e.dma_start(
                            out=BIG1[:, ws:ws + 2048], in_=woT_dr[h]),
                            reads=[("wos", h)], writes=b1(ws, ws + 2048))
                    else:
                        em.dpool(lambda e, h=h, ws=ws: e.dma_start(
                            out=BIG1[:, ws:ws + 2048],
                            in_=wo_bf[h * 128:h * 128 + 128, :]),
                            reads=[("aob" if kind == "attn_a" else "fob", h)],
                            writes=b1(ws, ws + 2048))
                    for jj in range(2):
                        j = jp * 2 + jj
                        for n in range(4):
                            em.t(lambda e, h=h, j=j, jj=jj, n=n, ws=ws: e.matmul(
                                PS[:, jj * 2048 + n * 512:jj * 2048 + n * 512 + 512],
                                BIG1[:, ZA0 + h * 512 + j * 128:ZA0 + h * 512 + j * 128 + 128],
                                BIG1[:, ws + n * 512:ws + n * 512 + 512],
                                start=(h == 0), stop=(h == NH - 1)),
                                reads=b1(ZA0 + h * 512) + b1(ws + n * 512),
                                writes=pst(jj * 2048 + n * 512))
                for jj in range(2):
                    j = jp * 2 + jj
                    st = qt0 + j
                    obanks = pst(jj * 2048, jj * 2048 + 2048)
                    if kind != "ffn" or "plainffn" in os.environ.get("DEV_TAP", ""):
                        osl = 4096 + jj * 2048
                        em.v(lambda e, jj=jj, osl=osl: e.tensor_copy(
                            STF[:, osl:osl + 2048], PS[:, jj * 2048:jj * 2048 + 2048]),
                            reads=obanks, writes=stf(osl))
                        em.dpool(lambda e, st=st, osl=osl: e.dma_start(
                            out=xdst[st * 128:st * 128 + 128, :],
                            in_=STF[:, osl:osl + 2048], accum_op=ALU.add),
                            reads=stf(osl), writes=[(xtag, st)])
                    else:
                        ffn_epilogue(sub, st, jj, xdst, xtag)

    def ffn_epilogue(sub, st, jj, xdst, xtag):
        xs = (st % 2) * 2048
        EB = 208 + jj * 8
        TB = jj * 56
        obanks = pst(jj * 2048, jj * 2048 + 2048)
        em.dsync(lambda e, xs=xs, st=st: e.dma_start(
            out=STF[:, xs:xs + 2048], in_=xbuf[st * 128:st * 128 + 128, :]),
            reads=[("xb", st)], writes=stf(xs))
        em.s(lambda e, xs=xs, st=st: e.activation(
            STF[:, 4096:6144], STF[:, xs:xs + 2048], AF.Identity,
            bias=STASH[:, 48 + st:48 + st + 1], scale=STASH[:, 32 + st:32 + st + 1]),
            reads=stf(xs) + [("STH", "bln", st), ("STH", "r", st)],
            writes=stf(4096))
        em.v(lambda e, jj=jj: e.tensor_add(
            STF[:, 6144:8192], STF[:, 4096:6144], PS[:, jj * 2048:jj * 2048 + 2048]),
            reads=stf(4096) + obanks, writes=stf(6144))
        em.dpool(lambda e, st=st: e.dma_start(
            out=xdst[st * 128:st * 128 + 128, :], in_=STF[:, 6144:8192],
            accum_op=ALU.add),
            reads=stf(6144), writes=[(xtag, st)])
        em.s(lambda e, xs=xs: e.activation(
            STF[:, xs:xs + 2048], STF[:, 6144:8192], AF.Square, accum_out=SMF[:, EB:EB + 1]),
            reads=stf(6144), writes=stf(xs) + [("SMF", EB + 0)])
        em.v(lambda e: e.tensor_scalar_mul(SMF[:, EB + 1:EB + 2], SMF[:, EB:EB + 1], 1.0 / D),
             reads=[("SMF", EB + 0)], writes=[("SMF", EB + 1)])
        em.s(lambda e: e.activation(SMF[:, EB + 2:EB + 3], SMF[:, EB + 1:EB + 2], AF.Ln, bias=f32(1e-6)),
             reads=[("SMF", EB + 1)], writes=[("SMF", EB + 2)])
        em.s(lambda e: e.activation(SMF[:, EB + 3:EB + 4], SMF[:, EB + 2:EB + 3], AF.Exp, scale=f32(-0.5)),
             reads=[("SMF", EB + 2)], writes=[("SMF", EB + 3)])
        em.v(lambda e: e.tensor_scalar(
            STF[:, 4096:6144], STF[:, 6144:8192], SMF[:, EB + 3:EB + 4], None, ALU.mult),
            reads=stf(6144) + [("SMF", EB + 3)], writes=stf(4096))
        em.v(lambda e: e.tensor_copy(STB[:, 0:2048], STF[:, 4096:6144]),
             reads=stf(4096), writes=stb(0))
        em.dsync(lambda e, st=st: e.dma_start(
            out=t_dr[st * 128:st * 128 + 128, :], in_=STB[:, 0:2048]),
            reads=stb(0), writes=[("tdr", st)])
        rb0 = jj * 2048
        for dc in range(16):
            pb = rb0 + (dc % 2) * 512
            em.t(lambda e, dc=dc, pb=pb: e.transpose(
                PS[:, pb:pb + 128], STF[:, 4096 + dc * 128:4096 + dc * 128 + 128],
                IDE[:, :]),
                reads=stf(4096) + [("IDE",)], writes=pst(pb))
            em.v(lambda e, dc=dc, pb=pb: e.tensor_scalar(
                BCS[:, 0:128], PS[:, pb:pb + 128], LNC[:, 16 + dc:16 + dc + 1],
                None, ALU.mult),
                reads=pst(pb) + [("LNC", 2)], writes=[("BCS",)])
            em.t(lambda e, dc=dc, rb0=rb0: e.matmul(
                PS[:, rb0 + 1024:rb0 + 1032], BCS[:, 0:128], WR8[:, dc * 8:dc * 8 + 8],
                start=(dc == 0), stop=(dc == 15)),
                reads=[("BCS",), ("WR8",)], writes=pst(rb0 + 1024))
        em.s(lambda e, rb0=rb0: e.activation(T8[:, TB + 0:TB + 8], PS[:, rb0 + 1024:rb0 + 1032], AF.Sigmoid),
             reads=pst(rb0 + 1024), writes=[("T8", jj, 0)])
        em.v(lambda e: e.tensor_add(T8[:, TB + 8:TB + 16], T8[:, TB + 0:TB + 8], T8[:, 112:120]),
             reads=[("T8", jj, 0), ("T8", "bias")], writes=[("T8", jj, 1)])
        em.v(lambda e: e.reduce_max(SMF[:, EB + 4:EB + 5], T8[:, TB + 8:TB + 16], axis=AX.X),
             reads=[("T8", jj, 1)], writes=[("SMF", EB + 4)])
        em.v(lambda e: e.tensor_scalar(T8[:, TB + 16:TB + 24], T8[:, TB + 8:TB + 16], SMF[:, EB + 4:EB + 5],
                                       None, ALU.is_ge),
             reads=[("T8", jj, 1), ("SMF", EB + 4)], writes=[("T8", jj, 2)])
        em.v(lambda e: e.scalar_tensor_tensor(T8[:, TB + 24:TB + 32], T8[:, TB + 16:TB + 24], -1e30,
                                              T8[:, TB + 8:TB + 16], ALU.mult, ALU.add),
             reads=[("T8", jj, 2), ("T8", jj, 1)], writes=[("T8", jj, 3)])
        em.v(lambda e: e.reduce_max(SMF[:, EB + 5:EB + 6], T8[:, TB + 24:TB + 32], axis=AX.X),
             reads=[("T8", jj, 3)], writes=[("SMF", EB + 5)])
        em.v(lambda e: e.tensor_scalar(T8[:, TB + 32:TB + 40], T8[:, TB + 24:TB + 32], SMF[:, EB + 5:EB + 6],
                                       None, ALU.is_ge),
             reads=[("T8", jj, 3), ("SMF", EB + 5)], writes=[("T8", jj, 4)])
        em.v(lambda e: e.tensor_add(T8[:, TB + 40:TB + 48], T8[:, TB + 16:TB + 24], T8[:, TB + 32:TB + 40]),
             reads=[("T8", jj, 2), ("T8", jj, 4)], writes=[("T8", jj, 5)])
        em.v(lambda e: e.tensor_mul(T8[:, TB + 48:TB + 56], T8[:, TB + 0:TB + 8], T8[:, TB + 40:TB + 48]),
             reads=[("T8", jj, 0), ("T8", jj, 5)], writes=[("T8", jj, 6)])
        em.v(lambda e: e.reduce_sum(SMF[:, EB + 6:EB + 7], T8[:, TB + 48:TB + 56], axis=AX.X),
             reads=[("T8", jj, 6)], writes=[("SMF", EB + 6)])
        em.v(lambda e: e.reciprocal(SMF[:, EB + 7:EB + 8], SMF[:, EB + 6:EB + 7]),
             reads=[("SMF", EB + 6)], writes=[("SMF", EB + 7)])
        em.v(lambda e, st=st: e.tensor_scalar(
            CMB[:, st * 8:st * 8 + 8], T8[:, TB + 48:TB + 56], SMF[:, EB + 7:EB + 8], None, ALU.mult),
            reads=[("T8", jj, 6), ("SMF", EB + 7)], writes=[("CMB", st)])

    def moe(sub):
        if "nomoe" in os.environ.get("DEV_TAP", ""):
            return
        xdst = out_d if sub == N_SUB - 1 else xbuf
        xtag = "ob" if sub == N_SUB - 1 else "xb"
        WA, WB, TT0, WD0 = 0, 16384, 16384, 8192
        if "spill" in os.environ.get("DEV_TAP", ""):
            em.dpool(lambda e: e.dma_start(out=zall_dr[0, :, 0:128], in_=CMB[:, :]),
                     reads=[("CMB", t) for t in range(NT)], writes=[("zspill", 0)])
        for ex in range(8):
            for dc in range(16):
                em.dpool(lambda e, ex=ex, dc=dc: e.dma_start(
                    out=BIG1[:, WA + dc * 1024:WA + (dc + 1) * 1024],
                    in_=wg[ex, dc * 128:(dc + 1) * 128, :]),
                    reads=[], writes=b1(WA + dc * 1024, WA + (dc + 1) * 1024))
                em.dpool(lambda e, ex=ex, dc=dc: e.dma_start(
                    out=BIG1[:, WB + dc * 1024:WB + (dc + 1) * 1024],
                    in_=wu[ex, dc * 128:(dc + 1) * 128, :]),
                    reads=[], writes=b1(WB + dc * 1024, WB + (dc + 1) * 1024))
            for sh in range(2):
                for dc in range(16):
                    em.dscal(lambda e, dc=dc, sh=sh: e.dma_start(
                        out=BIG2[:, TT0 + dc * 1024:TT0 + dc * 1024 + 1024],
                        in_=t_dr[sh * 1024:sh * 1024 + 1024, dc * 128:dc * 128 + 128],
                        transpose=True),
                        reads=[("tdr", t) for t in range(sh * 8, sh * 8 + 8)],
                        writes=b2(TT0 + dc * 1024, TT0 + (dc + 1) * 1024))
                    em.v(lambda e, dc=dc: e.tensor_scalar(
                        BIG2[:, TT0 + dc * 1024:TT0 + dc * 1024 + 1024],
                        BIG2[:, TT0 + dc * 1024:TT0 + dc * 1024 + 1024],
                        LNC[:, 16 + dc:16 + dc + 1], None, ALU.mult),
                        reads=b2(TT0 + dc * 1024, TT0 + (dc + 1) * 1024) + [("LNC", 2)],
                        writes=b2(TT0 + dc * 1024, TT0 + (dc + 1) * 1024))
                for ic in range(8):
                    gb = (ic % 2) * 1024
                    ub = 2048 + (ic % 2) * 1024
                    sls = 4096 + (ic % 2) * 1024
                    for dc in range(16):
                        for n in range(2):
                            em.t(lambda e, ic=ic, dc=dc, n=n, gb=gb: e.matmul(
                                PS[:, gb + n * 512:gb + n * 512 + 512],
                                BIG1[:, WA + dc * 1024 + ic * 128:WA + dc * 1024 + ic * 128 + 128],
                                BIG2[:, TT0 + dc * 1024 + n * 512:TT0 + dc * 1024 + n * 512 + 512],
                                start=(dc == 0), stop=(dc == 15)),
                                reads=b1(WA + dc * 1024 + ic * 128) + b2(TT0 + dc * 1024, TT0 + (dc + 1) * 1024),
                                writes=pst(gb + n * 512))
                            em.t(lambda e, ic=ic, dc=dc, n=n, ub=ub: e.matmul(
                                PS[:, ub + n * 512:ub + n * 512 + 512],
                                BIG1[:, WB + dc * 1024 + ic * 128:WB + dc * 1024 + ic * 128 + 128],
                                BIG2[:, TT0 + dc * 1024 + n * 512:TT0 + dc * 1024 + n * 512 + 512],
                                start=(dc == 0), stop=(dc == 15)),
                                reads=b1(WB + dc * 1024 + ic * 128) + b2(TT0 + dc * 1024, TT0 + (dc + 1) * 1024),
                                writes=pst(ub + n * 512))
                    em.s(lambda e, gb=gb, sls=sls: e.activation(
                        STF[:, sls:sls + 1024], PS[:, gb:gb + 1024], AF.Silu),
                        reads=pst(gb, gb + 1024), writes=stf(sls))
                    em.v(lambda e, ic=ic, ub=ub, sls=sls: e.tensor_mul(
                        BIG2[:, ic * 1024:ic * 1024 + 1024], STF[:, sls:sls + 1024],
                        PS[:, ub:ub + 1024]),
                        reads=stf(sls) + pst(ub, ub + 1024), writes=b2(ic * 1024, (ic + 1) * 1024))
                for dh in range(2):
                    wdb = WD0 if (sh * 2 + dh) % 2 == 0 or 'nowddb' in os.environ.get('DEV_TAP','') else 16384
                    for ic in range(8):
                        em.dpool(lambda e, ex=ex, dh=dh, ic=ic, wdb=wdb: e.dma_start(
                            out=BIG2[:, wdb + ic * 1024:wdb + (ic + 1) * 1024],
                            in_=wd[ex, ic * 128:(ic + 1) * 128, dh * 1024:dh * 1024 + 1024]),
                            reads=[], writes=b2(wdb + ic * 1024, wdb + (ic + 1) * 1024))
                    for st_l in range(8):
                        st = sh * 8 + st_l
                        pa = 2048 * (st_l % 2)
                        for ic in range(8):
                            for n in range(2):
                                em.t(lambda e, ic=ic, st_l=st_l, n=n, pa=pa, wdb=wdb: e.matmul(
                                    PS[:, pa + n * 512:pa + n * 512 + 512],
                                    BIG2[:, ic * 1024 + st_l * 128:ic * 1024 + st_l * 128 + 128],
                                    BIG2[:, wdb + ic * 1024 + n * 512:wdb + ic * 1024 + n * 512 + 512],
                                    start=(ic == 0), stop=(ic == 7)),
                                    reads=b2(ic * 1024 + st_l * 128) + b2(wdb + ic * 1024, wdb + (ic + 1) * 1024),
                                    writes=pst(pa + n * 512))
                        osl = 4096 + (st_l % 2) * 2048
                        em.v(lambda e, st=st, pa=pa, osl=osl, ex=ex: e.tensor_scalar(
                            STF[:, osl:osl + 1024], PS[:, pa:pa + 1024],
                            CMB[:, st * 8 + ex:st * 8 + ex + 1], None, ALU.mult),
                            reads=pst(pa, pa + 1024) + [("CMB", st)], writes=stf(osl))
                        em.dpool(lambda e, st=st, osl=osl, dh=dh: e.dma_start(
                            out=xdst[st * 128:st * 128 + 128, dh * 1024:dh * 1024 + 1024],
                            in_=STF[:, osl:osl + 1024], accum_op=ALU.add),
                            reads=stf(osl), writes=[(xtag, st)])

    kinds = ["attn_a", "ffn", "attn_b", "ffn"]
    for sub in range(N_SUB):
        kind = kinds[sub]
        phase0(sub, kind)
        load_yT(kind)
        if kind == "attn_b":
            phase1_b()
        else:
            phase1_qkv(kind)
        phase2(sub, kind)
        if kind == "ffn":
            moe(sub)

    em.build(nc, block, sems)
    st_ctx.close()
    return nc


def kernel(**inputs):
    consts = host_consts()
    nc = build_program()
    x_full = np.ascontiguousarray(np.asarray(inputs["L0_attn_a_x"]))
    base = dict(
        aq=inputs["L0_attn_a_q_kernel"], ak=inputs["L0_attn_a_k_kernel"],
        av=inputs["L0_attn_a_v_kernel"], ao=inputs["L0_attn_a_out_kernel"],
        bwq=inputs["L0_attn_b_W_Q"], bwk=inputs["L0_attn_b_W_K"],
        bwv=inputs["L0_attn_b_W_V"], bwo=inputs["L0_attn_b_W_O"],
        ln1=inputs["L0_ffn_ln1_w"], ln2=inputs["L0_ffn_ln2_w"],
        fwq=inputs["L0_ffn_wq"], fwk=inputs["L0_ffn_wk"],
        fwv=inputs["L0_ffn_wv"], fwo=inputs["L0_ffn_wo"],
        qnw=inputs["L0_ffn_qnorm_w"], knw=inputs["L0_ffn_knorm_w"],
        rw=inputs["L0_ffn_router_w"], rb=inputs["L0_ffn_router_bias"],
        wg=inputs["L0_ffn_wg"], wu=inputs["L0_ffn_wu"], wd=inputs["L0_ffn_wd"],
        **consts,
    )
    base = {k: np.ascontiguousarray(np.asarray(v)) for k, v in base.items()}
    in_maps = []
    for c in range(NCORES):
        m = dict(base)
        m["x_in"] = np.ascontiguousarray(x_full[c])
        in_maps.append(m)
    res = run_bass_kernel_spmd(nc, in_maps, core_ids=list(range(NCORES)))
    if "spill" in os.environ.get("DEV_TAP", ""):
        import pickle
        with open("/root/problem/_dev_spill.pkl", "wb") as f:
            pickle.dump({k: np.asarray(res.results[0][k]).astype(np.float32)
                         for k in ("y_dr", "q_dr", "k_dr", "t_dr", "v_dr", "zall_dr")
                         if k in res.results[0]}, f)
    return np.stack([r["out"] for r in res.results], axis=0)


# version 05 (final)
# speedup vs baseline: 1.0258x; 1.0007x over previous
import os
import numpy as np
import ml_dtypes
import concourse.bass as bass
import concourse.mybir as mybir
from concourse.bass_utils import run_bass_kernel_spmd

F32 = mybir.dt.float32
BF16 = mybir.dt.bfloat16
AF = mybir.ActivationFunctionType
ALU = mybir.AluOpType
AX = mybir.AxisListType
BF = ml_dtypes.bfloat16
f32 = np.float32

NCORES = 8
S = 2048
D = 2048
NT = 16
NH = 16
SCALE = f32(1.0 / np.sqrt(128.0))
N_SUB = int(os.environ.get("DEV_NSUB", "4"))


class Op:
    __slots__ = ("idx", "stream", "fn", "deps", "semkey", "is_dma", "milestone", "waits")

    def __init__(self, idx, stream, fn, deps, semkey, is_dma):
        self.idx = idx
        self.stream = stream
        self.fn = fn
        self.deps = deps
        self.semkey = semkey
        self.is_dma = is_dma
        self.milestone = None
        self.waits = None


class Em:
    STREAMS = ("tensor", "vector", "scalar", "pool", "sync")

    def __init__(self):
        self.ops = []
        self.last_write = {}
        self.readers = {}
        self.dsem_map = {}
        self.last_dma_sem = {}

    NDSEM = 64

    def _emit(self, stream, semkey, is_dma, fn, reads, writes):
        idx = len(self.ops)
        deps = set()
        for t in reads:
            w = self.last_write.get(t)
            if w is not None:
                deps.add(w)
        for t in writes:
            w = self.last_write.get(t)
            if w is not None:
                deps.add(w)
            for r in self.readers.get(t, ()):
                deps.add(r)
        deps.discard(idx)
        if is_dma:
            sw = semkey == "q_pool"
            tag = (sw, writes[0] if writes else ("q", semkey))
            if sw:
                slot = self.dsem_map.setdefault(tag, 44 + hash(tag) % 20)
            else:
                slot = self.dsem_map.setdefault(tag, hash(tag) % 44)
            semkey = ("d", slot)
            prev = self.last_dma_sem.get(slot)
            if prev is not None:
                deps.add(prev)
            self.last_dma_sem[slot] = idx
        self.ops.append(Op(idx, stream, fn, deps, semkey, is_dma))
        for t in reads:
            self.readers.setdefault(t, []).append(idx)
        for t in writes:
            self.last_write[t] = idx
            self.readers[t] = []
        return idx

    def t(self, fn, reads=(), writes=()):
        return self._emit("tensor", "tensor", False, fn, reads, writes)

    def v(self, fn, reads=(), writes=()):
        return self._emit("vector", "vector", False, fn, reads, writes)

    def s(self, fn, reads=(), writes=()):
        return self._emit("scalar", "scalar", False, fn, reads, writes)

    def p(self, fn, reads=(), writes=()):
        return self._emit("pool", "pool", False, fn, reads, writes)

    def dsync(self, fn, reads=(), writes=()):
        return self._emit("sync", "q_sync", True, fn, reads, writes)

    def dscal(self, fn, reads=(), writes=()):
        return self._emit("scalar", "q_scal", True, fn, reads, writes)

    def dpool(self, fn, reads=(), writes=()):
        return self._emit("pool", "q_pool", True, fn, reads, writes)

    def finalize(self):
        needed = set()
        for o in self.ops:
            if o.is_dma:
                needed.add(o.idx)
            for d in o.deps:
                p = self.ops[d]
                if p.stream == "tensor" and o.stream == "tensor":
                    continue
                needed.add(d)
        counts = {}
        for o in self.ops:
            if o.idx in needed:
                c = counts.get(o.semkey, 0) + 1
                counts[o.semkey] = c
                o.milestone = c
        hi = {s: {} for s in self.STREAMS}
        for o in self.ops:
            waits = {}
            for d in o.deps:
                p = self.ops[d]
                if p.stream == "tensor" and o.stream == "tensor":
                    continue
                v = p.milestone * (16 if p.is_dma else 1)
                if waits.get(p.semkey, 0) < v:
                    waits[p.semkey] = v
            h = hi[o.stream]
            o.waits = []
            for k, v in waits.items():
                if h.get(k, 0) < v:
                    h[k] = v
                    o.waits.append((k, v))
        self.final_counts = {k: v * (16 if (isinstance(k, tuple) and k[0] == "d")
                                      else 1)
                             for k, v in counts.items()}

    def build(self, nc, block, sems):
        self.finalize()
        streams = {s: [o for o in self.ops if o.stream == s] for s in self.STREAMS}

        def runner(stream):
            def run(eng):
                for o in streams[stream]:
                    for k, v in o.waits:
                        eng.wait_ge(sems[k], v)
                    inst = o.fn(eng)
                    if o.milestone is not None:
                        inst.then_inc(sems[o.semkey], 16 if o.is_dma else 1)
                if stream == "sync":
                    for k, v in self.final_counts.items():
                        eng.wait_ge(sems[k], v)
            return run

        block.tensor(runner("tensor"))
        block.vector(runner("vector"))
        block.scalar(runner("scalar"))
        block.gpsimd(runner("pool"))
        block.sync(runner("sync"))


def host_consts():
    pos = np.arange(S, dtype=np.float64)
    fr_a = np.arange(0, 128, 2, dtype=np.float64) / 128.0
    inv_a = 1.0 / (1e4 ** fr_a)
    ang_a = pos[:, None] * inv_a[None, :]
    cos_a = np.cos(ang_a).astype(np.float32)
    sin_a = np.sin(ang_a).astype(np.float32)
    fr_f = np.arange(0, 64, 2, dtype=np.float64) / 64.0
    inv_f = 1.0 / (1e6 ** fr_f)
    ang_f = pos[:, None] * inv_f[None, :]
    cos_f = np.cos(ang_f).astype(np.float32)
    sin_f = np.sin(ang_f).astype(np.float32)
    ident = np.eye(128, dtype=np.float32)
    masks = np.zeros((4, 128, 512), dtype=np.float32)
    sk = np.arange(128)[:, None]
    sq = np.arange(128)[None, :]
    tri = (sk <= sq).astype(np.float32)
    for r in range(4):
        for j in range(4):
            if r < j:
                masks[r, :, j * 128:(j + 1) * 128] = 1.0
            elif r == j:
                masks[r, :, j * 128:(j + 1) * 128] = tri
    return dict(cos_a=cos_a, sin_a=sin_a, cos_f=cos_f, sin_f=sin_f,
                ident=ident, trimask=masks.astype(BF))


def build_program():
    nc = bass.Bass("TRN2", target_bir_lowering=False, debug=False)
    em = Em()

    def din(name, shape, dt=F32):
        return nc.dram_tensor(name, list(shape), dt, kind="ExternalInput").ap()

    x_in = din("x_in", [S, D])
    aq = din("aq", [D, D])
    ak = din("ak", [D, 512])
    av = din("av", [D, 512])
    ao = din("ao", [D, D])
    bwq = din("bwq", [NH, 128, D])
    bwk = din("bwk", [NH, 128, D])
    bwv = din("bwv", [NH, 128, D])
    bwo = din("bwo", [NH, D, 128])
    ln1 = din("ln1", [D])
    ln2 = din("ln2", [D])
    fwq = din("fwq", [D, D])
    fwk = din("fwk", [D, 512])
    fwv = din("fwv", [D, 512])
    fwo = din("fwo", [D, D])
    qnw = din("qnw", [D])
    knw = din("knw", [512])
    rw = din("rw", [D, 8])
    rb = din("rb", [8])
    wg = din("wg", [8, D, 1024])
    wu = din("wu", [8, D, 1024])
    wd = din("wd", [8, 1024, D])
    cos_a = din("cos_a", [S, 64])
    sin_a = din("sin_a", [S, 64])
    cos_f = din("cos_f", [S, 32])
    sin_f = din("sin_f", [S, 32])
    ident_d = din("ident", [128, 128])
    trimask_d = din("trimask", [4, 128, 512], BF16)

    out_d = nc.dram_tensor("out", [S, D], F32, kind="ExternalOutput").ap()

    spill_kind = "ExternalOutput" if "spill" in os.environ.get("DEV_TAP", "") else "Internal"
    xbuf = nc.dram_tensor("xbuf", [S, D], F32, kind="Internal").ap()
    y_dr = nc.dram_tensor("y_dr", [S, D], BF16, kind=spill_kind).ap()
    q_dr = nc.dram_tensor("q_dr", [S, D], BF16, kind=spill_kind).ap()
    k_dr = nc.dram_tensor("k_dr", [S, 512], BF16, kind=spill_kind).ap()
    t_dr = nc.dram_tensor("t_dr", [S, D], BF16, kind=spill_kind).ap()
    v_dr = nc.dram_tensor("v_dr", [S, D], BF16, kind=spill_kind).ap()
    qT_dr = nc.dram_tensor("qT_dr", [NH, 128, S], BF16, kind="Internal").ap()
    kT_dr = nc.dram_tensor("kT_dr", [NH, 128, S], BF16, kind="Internal").ap()
    wqT_dr = nc.dram_tensor("wqT_dr", [NH, 128, D], BF16, kind="Internal").ap()
    wkT_dr = nc.dram_tensor("wkT_dr", [NH, 128, D], BF16, kind="Internal").ap()
    wvT_dr = nc.dram_tensor("wvT_dr", [16, 128, D], BF16, kind="Internal").ap()
    woT_dr = nc.dram_tensor("woT_dr", [NH, 128, D], BF16, kind="Internal").ap()
    zall_dr = nc.dram_tensor("zall_dr", [4, 128, 8192], BF16, kind=spill_kind).ap()
    aob_dr = nc.dram_tensor("aob_dr", [D, D], BF16, kind="Internal").ap()
    fob_dr = nc.dram_tensor("fob_dr", [D, D], BF16, kind="Internal").ap()

    from contextlib import ExitStack
    st_ctx = ExitStack()

    def sb(name, shape, dt, **kw):
        return st_ctx.enter_context(nc.sbuf_tensor(name, shape, dt, **kw))

    BIG1 = sb("BIG1", [128, 32768], BF16, is_transpose_dest=True)
    BIG2 = sb("BIG2", [128, 32768], BF16, is_transpose_dest=True)
    STF = sb("STF", [128, 8192], F32)
    STB = sb("STB", [128, 6144], BF16)
    CON = sb("CON", [128, 3072], F32)
    CONB = sb("CONB", [128, 2560], BF16)
    MSK = sb("MSK", [128, 2048], BF16)
    IDE = sb("IDE", [128, 128], F32)
    ONEC = sb("ONEC", [128, 1], BF16)
    ONERB = sb("ONERB", [1, 128], BF16)
    ONERF = sb("ONERF", [1, 128], F32)
    SMF = sb("SMF", [128, 224], F32)
    STASH = sb("STASH", [128, 64], F32)
    MCOL = sb("MCOL", [128, 8], F32)
    MROW = sb("MROW", [1, 1024], BF16)
    RCROW = sb("RCROW", [1, 512], F32)
    BCS = sb("BCS", [128, 1024], F32)
    CMB = sb("CMB", [128, 128], F32)
    T8 = sb("T8", [128, 128], F32)
    WR8 = sb("WR8", [128, 128], F32)
    LNC = sb("LNC", [128, 32], F32)
    PS = st_ctx.enter_context(nc.psum_tensor("PS", [128, 4096], F32))

    sems = {}
    for k in ("tensor", "vector", "scalar", "pool"):
        sems[k] = st_ctx.enter_context(nc.semaphore("sem_" + k))
    for i in range(Em.NDSEM):
        sems[("d", i)] = st_ctx.enter_context(nc.semaphore("dsem_%d" % i))
    block = st_ctx.enter_context(nc.Block())

    def b1(c0, c1=None):
        c1 = c0 + 512 if c1 is None else c1
        return [("B1", i) for i in range(c0 // 512, (c1 + 511) // 512)]

    def b2(c0, c1=None):
        c1 = c0 + 512 if c1 is None else c1
        return [("B2", i) for i in range(c0 // 512, (c1 + 511) // 512)]

    def stf(c0, c1=None):
        c1 = c0 + 2048 if c1 is None else c1
        return [("STF", i) for i in range(c0 // 2048, (c1 + 2047) // 2048)]

    def stb(c0, c1=None):
        c1 = c0 + 2048 if c1 is None else c1
        return [("STB", i) for i in range(c0 // 2048, (c1 + 2047) // 2048)]

    def pst(c0, c1=None):
        c1 = c0 + 512 if c1 is None else c1
        return [("PS", i) for i in range(c0 // 512, (c1 + 511) // 512)]

    em.dsync(lambda e: e.dma_start(out=IDE[:, :], in_=ident_d[:, :]), writes=[("IDE",)])
    em.v(lambda e: e.memset(ONEC[:, :], 1.0), writes=[("ONEC",)])
    em.v(lambda e: e.memset(ONERB[:, :], 1.0), writes=[("ONERB",)])
    em.v(lambda e: e.memset(ONERF[:, :], 1.0), writes=[("ONERF",)])
    em.dsync(lambda e: e.dma_start(out=MSK[:, :].rearrange("p (r c) -> p r c", r=4),
                                   in_=trimask_d[:, :, :].rearrange("r p c -> p r c")),
             writes=[("MSK",)])
    for dram, w, off in ((cos_a, 64, 0), (sin_a, 64, 1024),
                         (cos_f, 32, 2048), (sin_f, 32, 2560)):
        em.dsync(lambda e, dram=dram, w=w, off=off: e.dma_start(
            out=CON[:, off:off + NT * w].rearrange("p (t w) -> p t w", t=NT),
            in_=dram.rearrange("(t p) w -> p t w", p=128)),
            writes=[("CON", "tab")])

    def bcast_gain(dram_vec, width, dst_off, scale, tagname):
        em.dsync(lambda e: e.dma_start(out=STF[0:1, 2048:2048 + width], in_=dram_vec[None, :]),
                 writes=stf(2048))
        for i in range((width + 511) // 512):
            w0 = i * 512
            w1 = min(width, w0 + 512)
            em.t(lambda e, w0=w0, w1=w1: e.matmul(
                PS[:, 0:w1 - w0], ONERF[0:1, :], STF[0:1, 2048 + w0:2048 + w1],
                start=True, stop=True),
                reads=[("ONERF",)] + stf(2048), writes=pst(0))
            em.s(lambda e, w0=w0, w1=w1: e.activation(
                CONB[:, dst_off + w0:dst_off + w1], PS[:, 0:w1 - w0], AF.Copy, scale=scale),
                reads=pst(0), writes=[("CON", tagname)])

    bcast_gain(qnw, D, 0, SCALE, "gq")
    bcast_gain(knw, 512, 2048, f32(1.0), "gk")
    for vec, cbase, tag in ((ln1, 0, ("LNC", 1)), (ln2, 16, ("LNC", 2))):
        em.dsync(lambda e, vec=vec: e.dma_start(out=STF[0:1, 2048:4096], in_=vec[None, :]),
                 writes=stf(2048))
        for c in range(16):
            em.t(lambda e, c=c: e.transpose(
                PS[:, 1024 + (c % 2) * 512:1024 + (c % 2) * 512 + 1],
                STF[0:1, 2048 + c * 128:2048 + c * 128 + 128], IDE[0:1, 0:1]),
                reads=stf(2048) + [("IDE",)], writes=pst(1024 + (c % 2) * 512))
            em.s(lambda e, c=c, cbase=cbase: e.activation(
                LNC[:, cbase + c:cbase + c + 1],
                PS[:, 1024 + (c % 2) * 512:1024 + (c % 2) * 512 + 1],
                AF.Copy, scale=f32(1.0)),
                reads=pst(1024 + (c % 2) * 512), writes=[tag])
    em.dsync(lambda e: e.dma_start(out=WR8[:, :].rearrange("p (c w) -> p c w", c=16),
                                   in_=rw.rearrange("(c p) w -> p c w", p=128)),
             writes=[("WR8",)])
    em.dsync(lambda e: e.dma_start(out=STF[0:1, 2048:2056], in_=rb[None, :]), writes=stf(2048))
    em.t(lambda e: e.matmul(PS[:, 0:8], ONERF[0:1, :], STF[0:1, 2048:2056], start=True, stop=True),
         reads=[("ONERF",)] + stf(2048), writes=pst(0))
    em.s(lambda e: e.activation(T8[:, 112:120], PS[:, 0:8], AF.Copy, scale=f32(1.0)),
         reads=pst(0), writes=[("T8", "bias")])

    def prep_wo_bf(src, dst, tag):
        for dc in range(16):
            em.dsync(lambda e, dc=dc: e.dma_start(
                out=STF[:, 2048:4096], in_=src[dc * 128:dc * 128 + 128, :]),
                writes=stf(2048))
            em.v(lambda e, dc=dc: e.tensor_copy(STB[:, 0:2048], STF[:, 2048:4096]),
                 reads=stf(2048), writes=stb(0))
            em.dscal(lambda e, dc=dc: e.dma_start(
                out=dst[dc * 128:dc * 128 + 128, :], in_=STB[:, 0:2048]),
                reads=stb(0), writes=[(tag, dc)])

    prep_wo_bf(ao, aob_dr, "aob")
    if N_SUB >= 2:
        prep_wo_bf(fwo, fob_dr, "fob")

    def prep_bw(src3, dst, scale, vmode=False, tagbase="w"):
        for h in range(NH):
            ws = 2048 + (h % 2) * 2048
            em.dsync(lambda e, h=h, ws=ws: e.dma_start(
                out=STF[:, ws:ws + 2048], in_=src3[h]),
                writes=stf(ws))
            for dc in range(16):
                pb = 2048 + (dc % 4) * 512
                em.t(lambda e, h=h, dc=dc, pb=pb, ws=ws: e.transpose(
                    PS[:, pb:pb + 128], STF[:, ws + dc * 128:ws + dc * 128 + 128],
                    IDE[:, :]),
                    reads=stf(ws) + [("IDE",)], writes=pst(pb))
                if scale != 1.0:
                    em.v(lambda e, dc=dc, pb=pb: e.tensor_scalar(
                        STB[:, dc * 128:dc * 128 + 128], PS[:, pb:pb + 128],
                        float(scale), None, ALU.mult),
                        reads=pst(pb), writes=stb(0))
                else:
                    em.v(lambda e, dc=dc, pb=pb: e.tensor_copy(
                        STB[:, dc * 128:dc * 128 + 128], PS[:, pb:pb + 128]),
                        reads=pst(pb), writes=stb(0))
            if vmode:
                em.dscal(lambda e, h=h: e.dma_start(
                    out=dst[:, :, h * 128:(h + 1) * 128].rearrange("c p e -> p c e"),
                    in_=STB[:, 0:2048].rearrange("p (c e) -> p c e", c=16)),
                    reads=stb(0), writes=[("wv_scr",)])
            else:
                em.dscal(lambda e, h=h: e.dma_start(out=dst[h], in_=STB[:, 0:2048]),
                         reads=stb(0), writes=[(tagbase, h)])

    if N_SUB >= 3:
        prep_bw(bwq, wqT_dr, SCALE, tagbase="wqs")
        prep_bw(bwk, wkT_dr, 1.0, tagbase="wks")
        prep_bw(bwv, wvT_dr, 1.0, vmode=True)
        for h in range(NH):
            em.dsync(lambda e, h=h: e.dma_start(
                out=STF[:, 2048:4096].rearrange("p (c e) -> p c e", c=16),
                in_=bwo[h].rearrange("(c p) e -> p c e", p=128)),
                writes=stf(2048))
            for dc in range(16):
                pb = 2048 + (dc % 4) * 512
                em.t(lambda e, dc=dc, pb=pb: e.transpose(
                    PS[:, pb:pb + 128], STF[:, 2048 + dc * 128:2048 + dc * 128 + 128],
                    IDE[:, :]),
                    reads=stf(2048) + [("IDE",)], writes=pst(pb))
                em.v(lambda e, dc=dc, pb=pb: e.tensor_copy(
                    STB[:, dc * 128:dc * 128 + 128], PS[:, pb:pb + 128]),
                    reads=pst(pb), writes=stb(0))
            em.dscal(lambda e, h=h: e.dma_start(out=woT_dr[h], in_=STB[:, 0:2048]),
                     reads=stb(0), writes=[("wos", h)])

    def phase0(sub, kind):
        xsrc = x_in if sub == 0 else xbuf
        for st in range(NT):
            B = st * 12
            xs = (st % 2) * 2048
            r0 = st * 128
            em.dsync(lambda e, xs=xs, r0=r0: e.dma_start(
                out=STF[:, xs:xs + 2048], in_=xsrc[r0:r0 + 128, :]),
                reads=[("xb", st)] if sub > 0 else [], writes=stf(xs))
            if sub == 0:
                em.dscal(lambda e, xs=xs, r0=r0: e.dma_start(
                    out=xbuf[r0:r0 + 128, :], in_=STF[:, xs:xs + 2048]),
                    reads=stf(xs), writes=[("xb", st)])
            if sub == N_SUB - 1:
                em.dscal(lambda e, xs=xs, r0=r0: e.dma_start(
                    out=out_d[r0:r0 + 128, :], in_=STF[:, xs:xs + 2048]),
                    reads=stf(xs), writes=[("ob", st)])
            em.v(lambda e, xs=xs, B=B: e.reduce_sum(SMF[:, B:B + 1], STF[:, xs:xs + 2048], axis=AX.X),
                 reads=stf(xs), writes=[("SMF", B)])
            ys = (st % 2) * 2048
            em.s(lambda e, xs=xs, B=B, ys=ys: e.activation(
                STB[:, ys:ys + 2048], STF[:, xs:xs + 2048], AF.Square,
                accum_out=SMF[:, B + 1:B + 2]),
                reads=stf(xs), writes=stb(ys) + [("SMF", B + 1)])
            em.v(lambda e, B=B: e.tensor_scalar_mul(SMF[:, B + 2:B + 3], SMF[:, B:B + 1], 1.0 / D),
                 reads=[("SMF", B)], writes=[("SMF", B + 2)])
            em.v(lambda e, st=st, B=B: e.tensor_copy(STASH[:, st:st + 1], SMF[:, B + 2:B + 3]),
                 reads=[("SMF", B + 2)], writes=[("STH", "m", st)])
            em.v(lambda e, B=B: e.tensor_mul(SMF[:, B + 3:B + 4], SMF[:, B + 2:B + 3], SMF[:, B + 2:B + 3]),
                 reads=[("SMF", B + 2)], writes=[("SMF", B + 3)])
            em.v(lambda e, B=B: e.scalar_tensor_tensor(
                SMF[:, B + 4:B + 5], SMF[:, B + 1:B + 2], 1.0 / D, SMF[:, B + 3:B + 4],
                ALU.mult, ALU.subtract),
                reads=[("SMF", B + 1), ("SMF", B + 3)], writes=[("SMF", B + 4)])
            em.s(lambda e, B=B: e.activation(SMF[:, B + 5:B + 6], SMF[:, B + 4:B + 5], AF.Ln, bias=f32(1e-5)),
                 reads=[("SMF", B + 4)], writes=[("SMF", B + 5)])
            em.s(lambda e, st=st, B=B: e.activation(
                STASH[:, 32 + st:32 + st + 1], SMF[:, B + 5:B + 6], AF.Exp, scale=f32(-0.5)),
                reads=[("SMF", B + 5)], writes=[("STH", "r", st)])
            if kind == "ffn":
                em.v(lambda e, st=st, B=B: e.tensor_mul(
                    SMF[:, B + 8:B + 9], STASH[:, 32 + st:32 + st + 1], STASH[:, 32 + st:32 + st + 1]),
                    reads=[("STH", "r", st)], writes=[("SMF", B + 8)])
                em.v(lambda e, B=B: e.tensor_mul(SMF[:, B + 9:B + 10], SMF[:, B + 4:B + 5], SMF[:, B + 8:B + 9]),
                     reads=[("SMF", B + 4), ("SMF", B + 8)], writes=[("SMF", B + 9)])
                em.s(lambda e, B=B: e.activation(SMF[:, B + 10:B + 11], SMF[:, B + 9:B + 10], AF.Ln,
                                                 bias=f32(1e-6)),
                     reads=[("SMF", B + 9)], writes=[("SMF", B + 10)])
                em.s(lambda e, B=B: e.activation(SMF[:, B + 7:B + 8], SMF[:, B + 10:B + 11], AF.Exp,
                                                 scale=f32(-0.5)),
                     reads=[("SMF", B + 10)], writes=[("SMF", B + 7)])
                em.v(lambda e, st=st, B=B: e.tensor_mul(
                    SMF[:, B + 6:B + 7], STASH[:, 32 + st:32 + st + 1], SMF[:, B + 7:B + 8]),
                    reads=[("STH", "r", st), ("SMF", B + 7)], writes=[("SMF", B + 6)])
                em.v(lambda e, st=st, B=B: e.tensor_scalar(
                    STASH[:, 48 + st:48 + st + 1], SMF[:, B + 2:B + 3],
                    STASH[:, 32 + st:32 + st + 1], -1.0, ALU.mult, ALU.mult),
                    reads=[("SMF", B + 2), ("STH", "r", st)], writes=[("STH", "bln", st)])
            else:
                em.v(lambda e, st=st, B=B: e.tensor_copy(SMF[:, B + 6:B + 7], STASH[:, 32 + st:32 + st + 1]),
                     reads=[("STH", "r", st)], writes=[("SMF", B + 6)])
            em.v(lambda e, st=st, B=B: e.tensor_scalar(
                STASH[:, 16 + st:16 + st + 1], SMF[:, B + 2:B + 3], SMF[:, B + 6:B + 7], -1.0,
                ALU.mult, ALU.mult),
                reads=[("SMF", B + 2), ("SMF", B + 6)], writes=[("STH", "b", st)])
            em.v(lambda e, xs=xs, ys=ys, st=st, B=B: e.tensor_scalar(
                STB[:, ys:ys + 2048], STF[:, xs:xs + 2048], SMF[:, B + 6:B + 7],
                STASH[:, 16 + st:16 + st + 1], ALU.mult, ALU.add),
                reads=stf(xs) + [("SMF", B + 6), ("STH", "b", st)], writes=stb(ys))
            em.dsync(lambda e, ys=ys, r0=r0: e.dma_start(
                out=y_dr[r0:r0 + 128, :], in_=STB[:, ys:ys + 2048]),
                reads=stb(ys), writes=[("ydr", r0 // 128)])

    def load_yT(kind):
        ally = [("ydr", t) for t in range(NT)]
        for dc in range(16):
            em.dscal(lambda e, dc=dc: e.dma_start(
                out=BIG1[:, dc * 2048:(dc + 1) * 2048],
                in_=y_dr[:, dc * 128:(dc + 1) * 128], transpose=True),
                reads=ally, writes=b1(dc * 2048, (dc + 1) * 2048))
            if kind == "ffn":
                em.v(lambda e, dc=dc: e.tensor_scalar(
                    BIG1[:, dc * 2048:(dc + 1) * 2048],
                    BIG1[:, dc * 2048:(dc + 1) * 2048],
                    LNC[:, dc:dc + 1], None, ALU.mult),
                    reads=b1(dc * 2048, (dc + 1) * 2048) + [("LNC", 1)],
                    writes=b1(dc * 2048, (dc + 1) * 2048))

    def rope_natural(qf_off, out_off, st, nheads, mode, reads, writes):
        if mode == "full":
            half, coff, soff, tw, rot = 64, 0, 1024, 64, 128
        else:
            half, coff, soff, tw, rot = 32, 2048, 2560, 32, 64
        qv = STF[:, qf_off:qf_off + nheads * 128].rearrange("p (h e) -> p h e", h=nheads)
        ov = STB[:, out_off:out_off + nheads * 128].rearrange("p (h e) -> p h e", h=nheads)
        cosb = CON[:, coff + st * tw:coff + st * tw + half]
        sinb = CON[:, soff + st * tw:soff + st * tw + half]
        cosB = bass.AP(cosb.tensor, cosb.offset, [cosb.ap[0], [0, nheads], cosb.ap[1]])
        sinB = bass.AP(sinb.tensor, sinb.offset, [sinb.ap[0], [0, nheads], sinb.ap[1]])
        t1 = STF[:, 6144:6144 + nheads * half].rearrange("p (h e) -> p h e", h=nheads)
        t2 = STF[:, 7168:7168 + nheads * half].rearrange("p (h e) -> p h e", h=nheads)
        lo = (slice(None), slice(None), slice(0, half))
        hi = (slice(None), slice(None), slice(half, 2 * half))
        rd = list(reads) + [("CON", "tab")]
        wrt = list(writes)
        t3 = stf(6144, 8192)
        em.v(lambda e: e.tensor_mul(t1, qv[lo], cosB), reads=rd, writes=t3)
        em.v(lambda e: e.tensor_mul(t2, qv[hi], sinB), reads=rd, writes=t3)
        em.v(lambda e: e.tensor_sub(ov[lo], t1, t2), reads=t3, writes=wrt)
        em.v(lambda e: e.tensor_mul(t1, qv[hi], cosB), reads=rd, writes=t3)
        em.v(lambda e: e.tensor_mul(t2, qv[lo], sinB), reads=rd, writes=t3)
        em.v(lambda e: e.tensor_add(ov[hi], t1, t2), reads=t3, writes=wrt)
        if mode == "part":
            pas = (slice(None), slice(None), slice(rot, 128))
            em.v(lambda e: e.tensor_copy(ov[pas], qv[pas]), reads=rd, writes=wrt)

    def phase1_qkv(kind):
        wq_d, wk_d, wv_d = (aq, ak, av) if kind == "attn_a" else (fwq, fwk, fwv)
        gq_rd = [("CON", "gq")] if kind == "ffn" else []
        gk_rd = [("CON", "gk")] if kind == "ffn" else []
        mode = "full" if kind == "attn_a" else "part"
        for dc in range(16):
            em.dpool(lambda e, dc=dc: e.dma_start(
                out=BIG2[:, dc * 2048:(dc + 1) * 2048],
                in_=wq_d[dc * 128:(dc + 1) * 128, :]),
                reads=[], writes=b2(dc * 2048, (dc + 1) * 2048))
        for st in range(NT):
            pa = (st % 2) * 2048
            for dc in range(16):
                for n in range(4):
                    em.t(lambda e, st=st, dc=dc, n=n, pa=pa: e.matmul(
                        PS[:, pa + n * 512:pa + n * 512 + 512],
                        BIG1[:, dc * 2048 + st * 128:dc * 2048 + st * 128 + 128],
                        BIG2[:, dc * 2048 + n * 512:dc * 2048 + n * 512 + 512],
                        start=(dc == 0), stop=(dc == 15)),
                        reads=b1(dc * 2048 + st * 128) + b2(dc * 2048, (dc + 1) * 2048),
                        writes=pst(pa + n * 512))
            pb_rd = pst(pa, pa + 2048)
            B1 = 192 + (st % 4) * 4
            sqs = 4096 + (st % 2) * 1024
            em.s(lambda e, pa=pa, B1=B1, sqs=sqs: e.activation(
                STF[:, sqs:sqs + 1024], PS[:, pa:pa + 1024], AF.Square,
                accum_out=SMF[:, B1:B1 + 1]),
                reads=pb_rd, writes=stf(4096) + [("SMF", B1)])
            em.s(lambda e, pa=pa, B1=B1, sqs=sqs: e.activation(
                STF[:, sqs:sqs + 1024], PS[:, pa + 1024:pa + 2048], AF.Square,
                accum_out=SMF[:, B1 + 3:B1 + 4]),
                reads=pb_rd, writes=stf(4096) + [("SMF", B1 + 3)])
            em.v(lambda e, B1=B1: e.tensor_add(SMF[:, B1:B1 + 1], SMF[:, B1:B1 + 1],
                                               SMF[:, B1 + 3:B1 + 4]),
                 reads=[("SMF", B1), ("SMF", B1 + 3)], writes=[("SMF", B1)])
            em.v(lambda e, B1=B1: e.tensor_scalar_mul(SMF[:, B1 + 1:B1 + 2], SMF[:, B1:B1 + 1], 1.0 / D),
                 reads=[("SMF", B1)], writes=[("SMF", B1 + 1)])
            em.s(lambda e, B1=B1: e.activation(SMF[:, B1 + 2:B1 + 3], SMF[:, B1 + 1:B1 + 2], AF.Ln, bias=f32(1e-6)),
                 reads=[("SMF", B1 + 1)], writes=[("SMF", B1 + 2)])
            em.s(lambda e, B1=B1: e.activation(SMF[:, B1 + 1:B1 + 2], SMF[:, B1 + 2:B1 + 3], AF.Exp, scale=f32(-0.5)),
                 reads=[("SMF", B1 + 2)], writes=[("SMF", B1 + 1)])
            if kind == "attn_a":
                em.v(lambda e, pa=pa, B1=B1: e.tensor_scalar(
                    STF[:, 0:2048], PS[:, pa:pa + 2048], SMF[:, B1 + 1:B1 + 2], float(SCALE),
                    ALU.mult, ALU.mult),
                    reads=pb_rd + [("SMF", B1 + 1)], writes=stf(0))
            else:
                em.v(lambda e, pa=pa, B1=B1: e.scalar_tensor_tensor(
                    STF[:, 0:2048], PS[:, pa:pa + 2048], SMF[:, B1 + 1:B1 + 2],
                    CONB[:, 0:2048], ALU.mult, ALU.mult),
                    reads=pb_rd + [("SMF", B1 + 1)] + gq_rd, writes=stf(0))
            rope_natural(0, 0, st, 16, mode, reads=stf(0), writes=stb(0))
            em.dsync(lambda e, st=st: e.dma_start(
                out=q_dr[st * 128:st * 128 + 128, :], in_=STB[:, 0:2048]),
                reads=stb(0), writes=[("qdr", st)])
        for dc in range(16):
            em.dpool(lambda e, dc=dc: e.dma_start(
                out=BIG2[:, dc * 512:(dc + 1) * 512],
                in_=wk_d[dc * 128:(dc + 1) * 128, :]),
                reads=[], writes=b2(dc * 512, (dc + 1) * 512))
        for st in range(NT):
            pa = (st % 2) * 2048
            for dc in range(16):
                em.t(lambda e, st=st, dc=dc, pa=pa: e.matmul(
                    PS[:, pa:pa + 512],
                    BIG1[:, dc * 2048 + st * 128:dc * 2048 + st * 128 + 128],
                    BIG2[:, dc * 512:dc * 512 + 512],
                    start=(dc == 0), stop=(dc == 15)),
                    reads=b1(dc * 2048 + st * 128) + b2(dc * 512),
                    writes=pst(pa))
            B1 = 192 + (st % 4) * 4
            sqs = 4096 + (st % 2) * 1024
            em.s(lambda e, pa=pa, B1=B1, sqs=sqs: e.activation(
                STF[:, sqs:sqs + 512], PS[:, pa:pa + 512], AF.Square,
                accum_out=SMF[:, B1:B1 + 1]),
                reads=pst(pa), writes=stf(4096) + [("SMF", B1)])
            em.v(lambda e, B1=B1: e.tensor_scalar_mul(SMF[:, B1 + 1:B1 + 2], SMF[:, B1:B1 + 1], 1.0 / 512),
                 reads=[("SMF", B1)], writes=[("SMF", B1 + 1)])
            em.s(lambda e, B1=B1: e.activation(SMF[:, B1 + 2:B1 + 3], SMF[:, B1 + 1:B1 + 2], AF.Ln, bias=f32(1e-6)),
                 reads=[("SMF", B1 + 1)], writes=[("SMF", B1 + 2)])
            em.s(lambda e, B1=B1: e.activation(SMF[:, B1 + 1:B1 + 2], SMF[:, B1 + 2:B1 + 3], AF.Exp, scale=f32(-0.5)),
                 reads=[("SMF", B1 + 2)], writes=[("SMF", B1 + 1)])
            if kind == "attn_a":
                em.v(lambda e, pa=pa, B1=B1: e.tensor_scalar(
                    STF[:, 0:512], PS[:, pa:pa + 512], SMF[:, B1 + 1:B1 + 2], None, ALU.mult),
                    reads=pst(pa) + [("SMF", B1 + 1)], writes=stf(0))
            else:
                em.v(lambda e, pa=pa, B1=B1: e.scalar_tensor_tensor(
                    STF[:, 0:512], PS[:, pa:pa + 512], SMF[:, B1 + 1:B1 + 2],
                    CONB[:, 2048:2560], ALU.mult, ALU.mult),
                    reads=pst(pa) + [("SMF", B1 + 1)] + gk_rd, writes=stf(0))
            rope_natural(0, 0, st, 4, mode, reads=stf(0), writes=stb(0))
            em.dsync(lambda e, st=st: e.dma_start(
                out=k_dr[st * 128:st * 128 + 128, :], in_=STB[:, 0:512]),
                reads=stb(0), writes=[("kdr", st)])
        for dc in range(16):
            em.dpool(lambda e, dc=dc: e.dma_start(
                out=BIG2[:, dc * 512:(dc + 1) * 512],
                in_=wv_d[dc * 128:(dc + 1) * 128, :]),
                reads=[], writes=b2(dc * 512, (dc + 1) * 512))
        for st in range(NT):
            pa = (st % 2) * 2048
            for dc in range(16):
                em.t(lambda e, st=st, dc=dc, pa=pa: e.matmul(
                    PS[:, pa:pa + 512],
                    BIG1[:, dc * 2048 + st * 128:dc * 2048 + st * 128 + 128],
                    BIG2[:, dc * 512:dc * 512 + 512],
                    start=(dc == 0), stop=(dc == 15)),
                    reads=b1(dc * 2048 + st * 128) + b2(dc * 512),
                    writes=pst(pa))
            em.v(lambda e, st=st, pa=pa: e.tensor_copy(
                BIG2[:, 8192 + st * 512:8192 + st * 512 + 512], PS[:, pa:pa + 512]),
                reads=pst(pa), writes=b2(8192 + st * 512))
            if "spill" in os.environ.get("DEV_TAP", ""):
                em.dsync(lambda e, st=st: e.dma_start(
                    out=v_dr[st * 128:st * 128 + 128, 0:512],
                    in_=BIG2[:, 8192 + st * 512:8192 + st * 512 + 512]),
                    reads=b2(8192 + st * 512), writes=[("vdr", st, 0)])

    def phase1_b():
        for which, wsrc, dst, dtag in (("wqs", wqT_dr, qT_dr, "qTd"),
                                       ("wks", wkT_dr, kT_dr, "kTd")):
            for h in range(NH):
                ws = (h % 2) * 2048
                em.dpool(lambda e, h=h, ws=ws, wsrc=wsrc: e.dma_start(
                    out=STB[:, ws:ws + 2048], in_=wsrc[h]),
                    reads=[(which, h)], writes=stb(ws))
                pa = (h % 2) * 2048
                for dc in range(16):
                    for n in range(4):
                        em.t(lambda e, dc=dc, n=n, pa=pa, ws=ws: e.matmul(
                            PS[:, pa + n * 512:pa + n * 512 + 512],
                            STB[:, ws + dc * 128:ws + dc * 128 + 128],
                            BIG1[:, dc * 2048 + n * 512:dc * 2048 + n * 512 + 512],
                            start=(dc == 0), stop=(dc == 15)),
                            reads=stb(ws) + b1(dc * 2048, (dc + 1) * 2048),
                            writes=pst(pa + n * 512))
                em.v(lambda e, pa=pa: e.tensor_copy(STB[:, 4096:6144], PS[:, pa:pa + 2048]),
                     reads=pst(pa, pa + 2048), writes=stb(4096))
                em.dsync(lambda e, h=h, dst=dst: e.dma_start(out=dst[h], in_=STB[:, 4096:6144]),
                         reads=stb(4096), writes=[(dtag, h)])
        for eh in range(2):
            em.dpool(lambda e, eh=eh: e.dma_start(
                out=BIG2[:, 16384:32768].rearrange("p (c e) -> p c e", c=16),
                in_=wvT_dr[:, :, eh * 1024:eh * 1024 + 1024].rearrange("c p e -> p c e")),
                reads=[("wv_scr",)], writes=b2(16384, 32768))
            for st in range(NT):
                pa = (st % 2) * 2048
                for dc in range(16):
                    for n in range(2):
                        em.t(lambda e, st=st, dc=dc, n=n, pa=pa: e.matmul(
                            PS[:, pa + n * 512:pa + n * 512 + 512],
                            BIG1[:, dc * 2048 + st * 128:dc * 2048 + st * 128 + 128],
                            BIG2[:, 16384 + dc * 1024 + n * 512:16384 + dc * 1024 + n * 512 + 512],
                            start=(dc == 0), stop=(dc == 15)),
                            reads=b1(dc * 2048 + st * 128) + b2(16384 + dc * 1024, 16384 + (dc + 1) * 1024),
                            writes=pst(pa + n * 512))
                em.v(lambda e, pa=pa, eh=eh: e.tensor_copy(
                    STB[:, 4096 + eh * 1024:4096 + eh * 1024 + 1024], PS[:, pa:pa + 1024]),
                    reads=pst(pa, pa + 1024), writes=stb(4096))
                em.dsync(lambda e, st=st, eh=eh: e.dma_start(
                    out=v_dr[st * 128:st * 128 + 128, eh * 1024:eh * 1024 + 1024],
                    in_=STB[:, 4096 + eh * 1024:4096 + eh * 1024 + 1024]),
                    reads=stb(4096), writes=[("vdr", st, eh)])

    def phase2(sub, kind):
        causal = kind != "attn_a" and "nomask" not in os.environ.get("DEV_TAP", "")
        is_b = kind == "attn_b"
        xdst = out_d if sub == N_SUB - 1 else xbuf
        xtag = "ob" if sub == N_SUB - 1 else "xb"
        wo_bf = aob_dr if kind == "attn_a" else (fob_dr if kind == "ffn" else None)
        if not is_b:
            for kv in range(4):
                em.dscal(lambda e, kv=kv: e.dma_start(
                    out=BIG2[:, kv * 2048:kv * 2048 + 2048],
                    in_=k_dr[:, kv * 128:kv * 128 + 128], transpose=True),
                    reads=[("kdr", t) for t in range(NT)],
                    writes=b2(kv * 2048, (kv + 1) * 2048))
        QG0, PB0, WO0, ZA0, KT0, VH0 = 0, 1536, 4096, 8192, 16384, 20480
        for g in range(4):
            qt0 = g * 4
            nch = (qt0 + 4) if causal else NT
            def emit_q_load(h):
                qgs = QG0 + (h % 3) * 512
                if is_b:
                    em.dscal(lambda e, h=h, qgs=qgs, g=g: e.dma_start(
                        out=BIG1[:, qgs:qgs + 512], in_=qT_dr[h][:, g * 512:g * 512 + 512]),
                        reads=[("qTd", h)], writes=b1(qgs))
                else:
                    em.dscal(lambda e, h=h, qgs=qgs, g=g: e.dma_start(
                        out=BIG1[:, qgs:qgs + 512],
                        in_=q_dr[g * 512:g * 512 + 512, h * 128:h * 128 + 128],
                        transpose=True),
                        reads=[("qdr", t) for t in range(qt0, qt0 + 4)], writes=b1(qgs))

            def emit_kv_load(h):
                if not is_b:
                    return
                kts = KT0 + (h % 2) * 2048
                em.dscal(lambda e, h=h, kts=kts, nch=nch: e.dma_start(
                    out=BIG1[:, kts:kts + nch * 128], in_=kT_dr[h][:, 0:nch * 128]),
                    reads=[("kTd", h)], writes=b1(kts, kts + 2048))
                vhs = VH0 + (h % 2) * 2048
                em.dscal(lambda e, h=h, vhs=vhs, nch=nch: e.dma_start(
                    out=BIG1[:, vhs:vhs + nch * 128].rearrange("p (c e) -> p c e", c=nch),
                    in_=v_dr[0:nch * 128, h * 128:h * 128 + 128].rearrange(
                        "(c p) e -> p c e", p=128)),
                    reads=[("vdr", t, eh) for t in range(nch) for eh in range(2)],
                    writes=b1(vhs, vhs + 2048))

            emit_q_load(0)
            if NH > 1:
                emit_q_load(1)
            emit_kv_load(0)
            for h in range(NH):
                if h + 2 < NH:
                    emit_q_load(h + 2)
                if h + 1 < NH:
                    emit_kv_load(h + 1)
                kvh = h // 4
                qgs = QG0 + (h % 3) * 512
                if is_b:
                    kts = KT0 + (h % 2) * 2048
                    vhs = VH0 + (h % 2) * 2048
                    kt_ap = (lambda c, kts=kts: BIG1[:, kts + c * 128:kts + c * 128 + 128])
                    kt_rd = (lambda c, kts=kts: b1(kts + c * 128))
                    v_ap = (lambda c, vhs=vhs: BIG1[:, vhs + c * 128:vhs + c * 128 + 128])
                    v_rd = (lambda c, vhs=vhs: b1(vhs + c * 128))
                else:
                    kt_ap = (lambda c, kvh=kvh: BIG2[:, kvh * 2048 + c * 128:kvh * 2048 + c * 128 + 128])
                    kt_rd = (lambda c, kvh=kvh: b2(kvh * 2048 + c * 128))
                    v_ap = (lambda c, kvh=kvh: BIG2[:, 8192 + c * 512 + kvh * 128:8192 + c * 512 + kvh * 128 + 128])
                    v_rd = (lambda c: b2(8192 + c * 512))
                dgb = 2048 if h % 2 == 0 else 3584
                dnb = 1536
                ztb = 1024
                bcb = 2560 + (h % 2) * 512
                mc = (h % 2) * 4
                mb = (h % 2) * 512
                bcs_s = (h % 2) * 512
                for j in range(4):
                    qt = qt0 + j
                    em.t(lambda e, j=j, qt=qt, qgs=qgs, kt_ap=kt_ap, dgb=dgb: e.matmul(
                        PS[:, dgb + j * 128:dgb + j * 128 + 128],
                        BIG1[:, qgs + j * 128:qgs + j * 128 + 128],
                        kt_ap(qt), start=True, stop=True),
                        reads=b1(qgs) + kt_rd(qt), writes=pst(dgb))
                    em.v(lambda e, j=j, dgb=dgb, mc=mc: e.reduce_max(
                        MCOL[:, mc + j:mc + j + 1], PS[:, dgb + j * 128:dgb + j * 128 + 128],
                        axis=AX.X),
                        reads=pst(dgb), writes=[("MCOL", mc + j)])
                    em.t(lambda e, j=j, dnb=dnb, mc=mc: e.transpose(
                        PS[0:1, dnb + j * 128:dnb + j * 128 + 128],
                        MCOL[:, mc + j:mc + j + 1], IDE[:, :]),
                        reads=[("MCOL", mc + j), ("IDE",)], writes=pst(dnb))
                em.s(lambda e, dnb=dnb, mb=mb: e.activation(
                    MROW[0:1, mb:mb + 512], PS[0:1, dnb:dnb + 512], AF.Copy,
                    scale=f32(-1.0)),
                    reads=pst(dnb), writes=[("MROW", mb)])
                for c in range(nch):
                    pb = (c % 2) * 512
                    pbs = PB0 + (c % 3) * 512
                    em.t(lambda e, c=c, pb=pb, qgs=qgs, kt_ap=kt_ap: e.matmul(
                        PS[:, pb:pb + 512], kt_ap(c), BIG1[:, qgs:qgs + 512],
                        start=True, stop=False),
                        reads=kt_rd(c) + b1(qgs), writes=pst(pb))
                    em.t(lambda e, pb=pb, mb=mb: e.matmul(
                        PS[:, pb:pb + 512], ONERB[0:1, :], MROW[0:1, mb:mb + 512],
                        start=False, stop=True),
                        reads=[("ONERB",), ("MROW", mb)], writes=pst(pb))
                    em.s(lambda e, pb=pb, pbs=pbs: e.activation(
                        BIG1[:, pbs:pbs + 512], PS[:, pb:pb + 512], AF.Exp),
                        reads=pst(pb), writes=b1(pbs))
                    if causal and c >= qt0:
                        r = c - qt0
                        em.v(lambda e, r=r, pbs=pbs: e.tensor_mul(
                            BIG1[:, pbs:pbs + 512], BIG1[:, pbs:pbs + 512],
                            MSK[:, r * 512:r * 512 + 512]),
                            reads=b1(pbs) + [("MSK",)], writes=b1(pbs))
                    em.t(lambda e, c=c, pbs=pbs, dnb=dnb: e.matmul(
                        PS[0:1, dnb:dnb + 512], ONEC[:, :], BIG1[:, pbs:pbs + 512],
                        start=(c == 0), stop=(c == nch - 1)),
                        reads=[("ONEC",)] + b1(pbs), writes=pst(dnb))
                    em.t(lambda e, c=c, pbs=pbs, v_ap=v_ap, ztb=ztb: e.matmul(
                        PS[:, ztb:ztb + 512], v_ap(c), BIG1[:, pbs:pbs + 512],
                        start=(c == 0), stop=(c == nch - 1)),
                        reads=v_rd(c) + b1(pbs), writes=pst(ztb))
                em.v(lambda e, dnb=dnb: e.reciprocal(RCROW[0:1, :], PS[0:1, dnb:dnb + 512]),
                     reads=pst(dnb), writes=[("RCROW",)])
                em.t(lambda e, bcb=bcb: e.matmul(PS[:, bcb:bcb + 512], ONERF[0:1, :],
                                                 RCROW[0:1, :], start=True, stop=True),
                     reads=[("ONERF",), ("RCROW",)], writes=pst(bcb))
                em.s(lambda e, bcb=bcb, bcs_s=bcs_s: e.activation(
                    BCS[:, bcs_s:bcs_s + 512], PS[:, bcb:bcb + 512], AF.Copy,
                    scale=f32(1.0)),
                    reads=pst(bcb), writes=[("BCS", bcs_s)])
                em.v(lambda e, h=h, ztb=ztb, bcs_s=bcs_s: e.tensor_mul(
                    BIG1[:, ZA0 + h * 512:ZA0 + h * 512 + 512], PS[:, ztb:ztb + 512],
                    BCS[:, bcs_s:bcs_s + 512]),
                    reads=pst(ztb) + [("BCS", bcs_s)], writes=b1(ZA0 + h * 512))
            for jp in range(2):
                for h in range(NH):
                    ws = WO0 + (h % 2) * 2048
                    if is_b:
                        em.dpool(lambda e, h=h, ws=ws: e.dma_start(
                            out=BIG1[:, ws:ws + 2048], in_=woT_dr[h]),
                            reads=[("wos", h)], writes=b1(ws, ws + 2048))
                    else:
                        em.dpool(lambda e, h=h, ws=ws: e.dma_start(
                            out=BIG1[:, ws:ws + 2048],
                            in_=wo_bf[h * 128:h * 128 + 128, :]),
                            reads=[("aob" if kind == "attn_a" else "fob", h)],
                            writes=b1(ws, ws + 2048))
                    for jj in range(2):
                        j = jp * 2 + jj
                        for n in range(4):
                            em.t(lambda e, h=h, j=j, jj=jj, n=n, ws=ws: e.matmul(
                                PS[:, jj * 2048 + n * 512:jj * 2048 + n * 512 + 512],
                                BIG1[:, ZA0 + h * 512 + j * 128:ZA0 + h * 512 + j * 128 + 128],
                                BIG1[:, ws + n * 512:ws + n * 512 + 512],
                                start=(h == 0), stop=(h == NH - 1)),
                                reads=b1(ZA0 + h * 512) + b1(ws + n * 512),
                                writes=pst(jj * 2048 + n * 512))
                for jj in range(2):
                    j = jp * 2 + jj
                    st = qt0 + j
                    obanks = pst(jj * 2048, jj * 2048 + 2048)
                    if kind != "ffn" or "plainffn" in os.environ.get("DEV_TAP", ""):
                        osl = 4096 + jj * 2048
                        em.v(lambda e, jj=jj, osl=osl: e.tensor_copy(
                            STF[:, osl:osl + 2048], PS[:, jj * 2048:jj * 2048 + 2048]),
                            reads=obanks, writes=stf(osl))
                        em.dpool(lambda e, st=st, osl=osl: e.dma_start(
                            out=xdst[st * 128:st * 128 + 128, :],
                            in_=STF[:, osl:osl + 2048], accum_op=ALU.add),
                            reads=stf(osl), writes=[(xtag, st)])
                    else:
                        ffn_epilogue(sub, st, jj, xdst, xtag)

    def ffn_epilogue(sub, st, jj, xdst, xtag):
        xs = (st % 2) * 2048
        EB = 208 + jj * 8
        TB = jj * 56
        obanks = pst(jj * 2048, jj * 2048 + 2048)
        em.dsync(lambda e, xs=xs, st=st: e.dma_start(
            out=STF[:, xs:xs + 2048], in_=xbuf[st * 128:st * 128 + 128, :]),
            reads=[("xb", st)], writes=stf(xs))
        em.s(lambda e, xs=xs, st=st: e.activation(
            STF[:, 4096:6144], STF[:, xs:xs + 2048], AF.Identity,
            bias=STASH[:, 48 + st:48 + st + 1], scale=STASH[:, 32 + st:32 + st + 1]),
            reads=stf(xs) + [("STH", "bln", st), ("STH", "r", st)],
            writes=stf(4096))
        em.v(lambda e, jj=jj: e.tensor_add(
            STF[:, 6144:8192], STF[:, 4096:6144], PS[:, jj * 2048:jj * 2048 + 2048]),
            reads=stf(4096) + obanks, writes=stf(6144))
        em.dpool(lambda e, st=st: e.dma_start(
            out=xdst[st * 128:st * 128 + 128, :], in_=STF[:, 6144:8192],
            accum_op=ALU.add),
            reads=stf(6144), writes=[(xtag, st)])
        em.s(lambda e, xs=xs: e.activation(
            STF[:, xs:xs + 2048], STF[:, 6144:8192], AF.Square, accum_out=SMF[:, EB:EB + 1]),
            reads=stf(6144), writes=stf(xs) + [("SMF", EB + 0)])
        em.v(lambda e: e.tensor_scalar_mul(SMF[:, EB + 1:EB + 2], SMF[:, EB:EB + 1], 1.0 / D),
             reads=[("SMF", EB + 0)], writes=[("SMF", EB + 1)])
        em.s(lambda e: e.activation(SMF[:, EB + 2:EB + 3], SMF[:, EB + 1:EB + 2], AF.Ln, bias=f32(1e-6)),
             reads=[("SMF", EB + 1)], writes=[("SMF", EB + 2)])
        em.s(lambda e: e.activation(SMF[:, EB + 3:EB + 4], SMF[:, EB + 2:EB + 3], AF.Exp, scale=f32(-0.5)),
             reads=[("SMF", EB + 2)], writes=[("SMF", EB + 3)])
        em.v(lambda e: e.tensor_scalar(
            STF[:, 4096:6144], STF[:, 6144:8192], SMF[:, EB + 3:EB + 4], None, ALU.mult),
            reads=stf(6144) + [("SMF", EB + 3)], writes=stf(4096))
        em.v(lambda e: e.tensor_copy(STB[:, 0:2048], STF[:, 4096:6144]),
             reads=stf(4096), writes=stb(0))
        em.dsync(lambda e, st=st: e.dma_start(
            out=t_dr[st * 128:st * 128 + 128, :], in_=STB[:, 0:2048]),
            reads=stb(0), writes=[("tdr", st)])
        rb0 = jj * 2048
        for dc in range(16):
            pb = rb0 + (dc % 2) * 512
            em.t(lambda e, dc=dc, pb=pb: e.transpose(
                PS[:, pb:pb + 128], STF[:, 4096 + dc * 128:4096 + dc * 128 + 128],
                IDE[:, :]),
                reads=stf(4096) + [("IDE",)], writes=pst(pb))
            em.v(lambda e, dc=dc, pb=pb: e.tensor_scalar(
                BCS[:, 0:128], PS[:, pb:pb + 128], LNC[:, 16 + dc:16 + dc + 1],
                None, ALU.mult),
                reads=pst(pb) + [("LNC", 2)], writes=[("BCS",)])
            em.t(lambda e, dc=dc, rb0=rb0: e.matmul(
                PS[:, rb0 + 1024:rb0 + 1032], BCS[:, 0:128], WR8[:, dc * 8:dc * 8 + 8],
                start=(dc == 0), stop=(dc == 15)),
                reads=[("BCS",), ("WR8",)], writes=pst(rb0 + 1024))
        em.s(lambda e, rb0=rb0: e.activation(T8[:, TB + 0:TB + 8], PS[:, rb0 + 1024:rb0 + 1032], AF.Sigmoid),
             reads=pst(rb0 + 1024), writes=[("T8", jj, 0)])
        em.v(lambda e: e.tensor_add(T8[:, TB + 8:TB + 16], T8[:, TB + 0:TB + 8], T8[:, 112:120]),
             reads=[("T8", jj, 0), ("T8", "bias")], writes=[("T8", jj, 1)])
        em.v(lambda e: e.reduce_max(SMF[:, EB + 4:EB + 5], T8[:, TB + 8:TB + 16], axis=AX.X),
             reads=[("T8", jj, 1)], writes=[("SMF", EB + 4)])
        em.v(lambda e: e.tensor_scalar(T8[:, TB + 16:TB + 24], T8[:, TB + 8:TB + 16], SMF[:, EB + 4:EB + 5],
                                       None, ALU.is_ge),
             reads=[("T8", jj, 1), ("SMF", EB + 4)], writes=[("T8", jj, 2)])
        em.v(lambda e: e.scalar_tensor_tensor(T8[:, TB + 24:TB + 32], T8[:, TB + 16:TB + 24], -1e30,
                                              T8[:, TB + 8:TB + 16], ALU.mult, ALU.add),
             reads=[("T8", jj, 2), ("T8", jj, 1)], writes=[("T8", jj, 3)])
        em.v(lambda e: e.reduce_max(SMF[:, EB + 5:EB + 6], T8[:, TB + 24:TB + 32], axis=AX.X),
             reads=[("T8", jj, 3)], writes=[("SMF", EB + 5)])
        em.v(lambda e: e.tensor_scalar(T8[:, TB + 32:TB + 40], T8[:, TB + 24:TB + 32], SMF[:, EB + 5:EB + 6],
                                       None, ALU.is_ge),
             reads=[("T8", jj, 3), ("SMF", EB + 5)], writes=[("T8", jj, 4)])
        em.v(lambda e: e.tensor_add(T8[:, TB + 40:TB + 48], T8[:, TB + 16:TB + 24], T8[:, TB + 32:TB + 40]),
             reads=[("T8", jj, 2), ("T8", jj, 4)], writes=[("T8", jj, 5)])
        em.v(lambda e: e.tensor_mul(T8[:, TB + 48:TB + 56], T8[:, TB + 0:TB + 8], T8[:, TB + 40:TB + 48]),
             reads=[("T8", jj, 0), ("T8", jj, 5)], writes=[("T8", jj, 6)])
        em.v(lambda e: e.reduce_sum(SMF[:, EB + 6:EB + 7], T8[:, TB + 48:TB + 56], axis=AX.X),
             reads=[("T8", jj, 6)], writes=[("SMF", EB + 6)])
        em.v(lambda e: e.reciprocal(SMF[:, EB + 7:EB + 8], SMF[:, EB + 6:EB + 7]),
             reads=[("SMF", EB + 6)], writes=[("SMF", EB + 7)])
        em.v(lambda e, st=st: e.tensor_scalar(
            CMB[:, st * 8:st * 8 + 8], T8[:, TB + 48:TB + 56], SMF[:, EB + 7:EB + 8], None, ALU.mult),
            reads=[("T8", jj, 6), ("SMF", EB + 7)], writes=[("CMB", st)])

    def moe(sub):
        if "nomoe" in os.environ.get("DEV_TAP", ""):
            return
        xdst = out_d if sub == N_SUB - 1 else xbuf
        xtag = "ob" if sub == N_SUB - 1 else "xb"
        WA, WB, TT0, WD0 = 0, 16384, 16384, 8192
        if "spill" in os.environ.get("DEV_TAP", ""):
            em.dpool(lambda e: e.dma_start(out=zall_dr[0, :, 0:128], in_=CMB[:, :]),
                     reads=[("CMB", t) for t in range(NT)], writes=[("zspill", 0)])
        for ex in range(8):
            for dc in range(16):
                em.dpool(lambda e, ex=ex, dc=dc: e.dma_start(
                    out=BIG1[:, WA + dc * 1024:WA + (dc + 1) * 1024],
                    in_=wg[ex, dc * 128:(dc + 1) * 128, :]),
                    reads=[], writes=b1(WA + dc * 1024, WA + (dc + 1) * 1024))
                em.dpool(lambda e, ex=ex, dc=dc: e.dma_start(
                    out=BIG1[:, WB + dc * 1024:WB + (dc + 1) * 1024],
                    in_=wu[ex, dc * 128:(dc + 1) * 128, :]),
                    reads=[], writes=b1(WB + dc * 1024, WB + (dc + 1) * 1024))
            for sh in range(2):
                for dc in range(16):
                    em.dscal(lambda e, dc=dc, sh=sh: e.dma_start(
                        out=BIG2[:, TT0 + dc * 1024:TT0 + dc * 1024 + 1024],
                        in_=t_dr[sh * 1024:sh * 1024 + 1024, dc * 128:dc * 128 + 128],
                        transpose=True),
                        reads=[("tdr", t) for t in range(sh * 8, sh * 8 + 8)],
                        writes=b2(TT0 + dc * 1024, TT0 + (dc + 1) * 1024))
                    em.v(lambda e, dc=dc: e.tensor_scalar(
                        BIG2[:, TT0 + dc * 1024:TT0 + dc * 1024 + 1024],
                        BIG2[:, TT0 + dc * 1024:TT0 + dc * 1024 + 1024],
                        LNC[:, 16 + dc:16 + dc + 1], None, ALU.mult),
                        reads=b2(TT0 + dc * 1024, TT0 + (dc + 1) * 1024) + [("LNC", 2)],
                        writes=b2(TT0 + dc * 1024, TT0 + (dc + 1) * 1024))
                for ic in range(8):
                    gb = (ic % 2) * 1024
                    ub = 2048 + (ic % 2) * 1024
                    sls = 4096 + (ic % 2) * 1024
                    for dc in range(16):
                        for n in range(2):
                            em.t(lambda e, ic=ic, dc=dc, n=n, gb=gb: e.matmul(
                                PS[:, gb + n * 512:gb + n * 512 + 512],
                                BIG1[:, WA + dc * 1024 + ic * 128:WA + dc * 1024 + ic * 128 + 128],
                                BIG2[:, TT0 + dc * 1024 + n * 512:TT0 + dc * 1024 + n * 512 + 512],
                                start=(dc == 0), stop=(dc == 15)),
                                reads=b1(WA + dc * 1024 + ic * 128) + b2(TT0 + dc * 1024, TT0 + (dc + 1) * 1024),
                                writes=pst(gb + n * 512))
                            em.t(lambda e, ic=ic, dc=dc, n=n, ub=ub: e.matmul(
                                PS[:, ub + n * 512:ub + n * 512 + 512],
                                BIG1[:, WB + dc * 1024 + ic * 128:WB + dc * 1024 + ic * 128 + 128],
                                BIG2[:, TT0 + dc * 1024 + n * 512:TT0 + dc * 1024 + n * 512 + 512],
                                start=(dc == 0), stop=(dc == 15)),
                                reads=b1(WB + dc * 1024 + ic * 128) + b2(TT0 + dc * 1024, TT0 + (dc + 1) * 1024),
                                writes=pst(ub + n * 512))
                    em.s(lambda e, gb=gb, sls=sls: e.activation(
                        STF[:, sls:sls + 1024], PS[:, gb:gb + 1024], AF.Silu),
                        reads=pst(gb, gb + 1024), writes=stf(sls))
                    em.v(lambda e, ic=ic, ub=ub, sls=sls: e.tensor_mul(
                        BIG2[:, ic * 1024:ic * 1024 + 1024], STF[:, sls:sls + 1024],
                        PS[:, ub:ub + 1024]),
                        reads=stf(sls) + pst(ub, ub + 1024), writes=b2(ic * 1024, (ic + 1) * 1024))
                for dh in range(2):
                    wdb = WD0 if (sh * 2 + dh) % 2 == 0 or 'nowddb' in os.environ.get('DEV_TAP','') else 16384
                    for ic in range(8):
                        em.dpool(lambda e, ex=ex, dh=dh, ic=ic, wdb=wdb: e.dma_start(
                            out=BIG2[:, wdb + ic * 1024:wdb + (ic + 1) * 1024],
                            in_=wd[ex, ic * 128:(ic + 1) * 128, dh * 1024:dh * 1024 + 1024]),
                            reads=[], writes=b2(wdb + ic * 1024, wdb + (ic + 1) * 1024))
                    for st_l in range(8):
                        st = sh * 8 + st_l
                        pa = 2048 * (st_l % 2)
                        for ic in range(8):
                            for n in range(2):
                                em.t(lambda e, ic=ic, st_l=st_l, n=n, pa=pa, wdb=wdb: e.matmul(
                                    PS[:, pa + n * 512:pa + n * 512 + 512],
                                    BIG2[:, ic * 1024 + st_l * 128:ic * 1024 + st_l * 128 + 128],
                                    BIG2[:, wdb + ic * 1024 + n * 512:wdb + ic * 1024 + n * 512 + 512],
                                    start=(ic == 0), stop=(ic == 7)),
                                    reads=b2(ic * 1024 + st_l * 128) + b2(wdb + ic * 1024, wdb + (ic + 1) * 1024),
                                    writes=pst(pa + n * 512))
                        osl = 4096 + (st_l % 2) * 2048
                        em.v(lambda e, st=st, pa=pa, osl=osl, ex=ex: e.tensor_scalar(
                            STF[:, osl:osl + 1024], PS[:, pa:pa + 1024],
                            CMB[:, st * 8 + ex:st * 8 + ex + 1], None, ALU.mult),
                            reads=pst(pa, pa + 1024) + [("CMB", st)], writes=stf(osl))
                        em.dpool(lambda e, st=st, osl=osl, dh=dh: e.dma_start(
                            out=xdst[st * 128:st * 128 + 128, dh * 1024:dh * 1024 + 1024],
                            in_=STF[:, osl:osl + 1024], accum_op=ALU.add),
                            reads=stf(osl), writes=[(xtag, st)])

    kinds = ["attn_a", "ffn", "attn_b", "ffn"]
    for sub in range(N_SUB):
        kind = kinds[sub]
        phase0(sub, kind)
        load_yT(kind)
        if kind == "attn_b":
            phase1_b()
        else:
            phase1_qkv(kind)
        phase2(sub, kind)
        if kind == "ffn":
            moe(sub)

    em.build(nc, block, sems)
    st_ctx.close()
    return nc


def kernel(**inputs):
    consts = host_consts()
    nc = build_program()
    x_full = np.ascontiguousarray(np.asarray(inputs["L0_attn_a_x"]))
    base = dict(
        aq=inputs["L0_attn_a_q_kernel"], ak=inputs["L0_attn_a_k_kernel"],
        av=inputs["L0_attn_a_v_kernel"], ao=inputs["L0_attn_a_out_kernel"],
        bwq=inputs["L0_attn_b_W_Q"], bwk=inputs["L0_attn_b_W_K"],
        bwv=inputs["L0_attn_b_W_V"], bwo=inputs["L0_attn_b_W_O"],
        ln1=inputs["L0_ffn_ln1_w"], ln2=inputs["L0_ffn_ln2_w"],
        fwq=inputs["L0_ffn_wq"], fwk=inputs["L0_ffn_wk"],
        fwv=inputs["L0_ffn_wv"], fwo=inputs["L0_ffn_wo"],
        qnw=inputs["L0_ffn_qnorm_w"], knw=inputs["L0_ffn_knorm_w"],
        rw=inputs["L0_ffn_router_w"], rb=inputs["L0_ffn_router_bias"],
        wg=inputs["L0_ffn_wg"], wu=inputs["L0_ffn_wu"], wd=inputs["L0_ffn_wd"],
        **consts,
    )
    base = {k: np.ascontiguousarray(np.asarray(v)) for k, v in base.items()}
    in_maps = []
    for c in range(NCORES):
        m = dict(base)
        m["x_in"] = np.ascontiguousarray(x_full[c])
        in_maps.append(m)
    res = run_bass_kernel_spmd(nc, in_maps, core_ids=list(range(NCORES)))
    if "spill" in os.environ.get("DEV_TAP", ""):
        import pickle
        with open("/root/problem/_dev_spill.pkl", "wb") as f:
            pickle.dump({k: np.asarray(res.results[0][k]).astype(np.float32)
                         for k in ("y_dr", "q_dr", "k_dr", "t_dr", "v_dr", "zall_dr")
                         if k in res.results[0]}, f)
    return np.stack([r["out"] for r in res.results], axis=0)
